# version 01
# baseline (speedup 1.0000x reference)
import os
import numpy as np
import ml_dtypes
import concourse.bass as bass
import concourse.mybir as mybir
from concourse.bass_utils import run_bass_kernel_spmd
from contextlib import ExitStack

F32 = mybir.dt.float32
BF16 = mybir.dt.bfloat16
F16 = mybir.dt.float16
F8 = mybir.dt.float8e4
I8 = mybir.dt.int8
AF = mybir.ActivationFunctionType
ALU = mybir.AluOpType
AX = mybir.AxisListType

B, T, D = 4, 4096, 1024
HID = 4096
NT = T // 128
f32 = np.float32
UPTO = int(os.environ.get("KMK_UPTO", "99"))
RMS_EPS = np.float32(1.1920929e-07)


def _rotl(x, d):
    return ((x << np.uint32(d)) | (x >> np.uint32(32 - d))).astype(np.uint32)

def _tf2x32(k1, k2, x0, x1):
    ks0 = np.uint32(k1); ks1 = np.uint32(k2)
    ks2 = np.uint32(ks0 ^ ks1 ^ np.uint32(0x1BD11BDA))
    x0 = (x0 + ks0).astype(np.uint32); x1 = (x1 + ks1).astype(np.uint32)
    rot = [[13, 15, 26, 6], [17, 29, 16, 24]]
    ks = [ks0, ks1, ks2]
    sched = [(rot[0], ks[1], ks[2], 1), (rot[1], ks[2], ks[0], 2),
             (rot[0], ks[0], ks[1], 3), (rot[1], ks[1], ks[2], 4),
             (rot[0], ks[2], ks[0], 5)]
    for rs, ka, kb_, i in sched:
        for r in rs:
            x0 = (x0 + x1).astype(np.uint32); x1 = _rotl(x1, r)
            x1 = (x1 ^ x0).astype(np.uint32)
        x0 = (x0 + ka).astype(np.uint32)
        x1 = (x1 + kb_ + np.uint32(i)).astype(np.uint32)
    return x0, x1

def dropout_keep_mask():
    n = B * T * D
    x64_mode = os.environ.get("BASS_SIM_FP64") == "1"
    out = np.empty(n, dtype=bool)
    step = 1 << 22
    for s in range(0, n, step):
        i = np.arange(s, min(n, s + step), dtype=np.uint64)
        hi = (i >> np.uint64(32)).astype(np.uint32)
        lo = (i & np.uint64(0xFFFFFFFF)).astype(np.uint32)
        o0, o1 = _tf2x32(0, 42, hi, lo)
        if not x64_mode:
            bits = (o0 ^ o1).astype(np.uint32)
            fb = ((bits >> np.uint32(9)) | np.uint32(0x3F800000)).astype(np.uint32)
            u = fb.view(np.float32) - np.float32(1.0)
            out[s:s + len(i)] = u < np.float32(0.8)
        else:
            bits = (o0.astype(np.uint64) << np.uint64(32)) | o1.astype(np.uint64)
            fb = (bits >> np.uint64(12)) | np.uint64(0x3FF0000000000000)
            u = fb.view(np.float64) - 1.0
            out[s:s + len(i)] = u < 0.8
    return out.reshape(B, T, D)


class KB:
    ENGS = ("sync", "vector", "scalar", "tensor", "gpsimd")

    def __init__(self, nc, ctx):
        self.nc = nc
        self.blk = None
        self.sem = {e: ctx.enter_context(nc.semaphore("S" + e)) for e in self.ENGS}
        self.tick = {e: 0 for e in self.ENGS}

    def step(self, eng, emit, deps=(), n_dma=0):
        sem = self.sem
        deps = [d for d in deps if d is not None]
        me = sem[eng]
        self.tick[eng] += 16 * n_dma if n_dma else 1
        ticket = self.tick[eng]

        def closure(e):
            for (s, v) in deps:
                e.wait_ge(sem[s], v)
            emit(e)
            if not n_dma:
                e.drain().then_inc(me, 1)
        getattr(self.blk, eng)(closure)
        return (eng, ticket)

    def dma(self, e, out, in_):
        e.dma_start(out=out, in_=in_).then_inc(self.sem["sync"], 16)


def build_program():
    nc = bass.Bass("TRN2", target_bir_lowering=False)
    g = {}

    def inp(name, shape, dt=F32):
        g[name] = nc.dram_tensor(name, shape, dt, kind="ExternalInput").ap()

    inp("x_in", [T, D]); inp("mask_in", [T, D])
    inp("l0_w1t", [D, HID]); inp("l0_w2t", [HID, D])
    inp("l1_w1t", [D, HID]); inp("l1_w2t", [HID, D])
    inp("l0_ln1w_t", [128, D]); inp("l0_ln1b_t", [128, D])
    inp("l0_ln2w_t", [128, HID]); inp("l0_ln2b_t", [128, HID])
    inp("l0_b1_t", [128, HID]); inp("l0_b2_t", [128, D])
    inp("l1_b1_t", [128, HID]); inp("l1_b2_t", [128, D])
    inp("w_dq_t", [64, 512]); inp("w_uq_t", [512, 64])
    inp("w_dkv_t", [64, 1024]); inp("w_ukv_t", [1024, 128])
    inp("w_qr_t", [64, 64]); inp("w_kr_t", [64, 64]); inp("w_o_t", [64, 64])
    inp("qnw_c", [64, 1]); inp("knw_c", [64, 1])
    inp("w_qkv", [D, 3 * D]); inp("bqkv_cols", [128, 24])
    inp("w_gate", [D, D]); inp("bgate_cols", [128, 8])
    inp("w_proj", [D, D]); inp("b_proj_t", [128, D])
    inp("l1_lng_t", [128, D]); inp("l1_lnb_t", [128, D])
    inp("ident", [128, 128], BF16)
    inp("ones_col_f32", [128, 1])
    inp("ones_row_f32", [1, 128])
    inp("ones11", [1, 1])
    inp("tri_masks", [128, 4 * 512], BF16)
    inp("ltri_sm", [128, 128], BF16)

    g["y_out"] = nc.dram_tensor("y_out", [T, D], F32, kind="ExternalOutput").ap()
    for n, shape, dt in (("R", [T, D], F32), ("A", [T, D], F32), ("SC2", [T, 1], F32),
                         ("W1Q0", [D, HID], BF16), ("W2Q0", [HID, D], BF16),
                         ("W1Q1", [D, HID], BF16), ("W2Q1", [HID, D], BF16),
                         ("HQ", [T, HID], BF16), ("UT", [D, T], BF16)):
        g[n] = nc.dram_tensor(n, shape, dt, kind="Internal").ap()

    ctx = ExitStack()
    kb = KB(nc, ctx)
    ident = ctx.enter_context(nc.sbuf_tensor("identt", [128, 128], BF16))
    onescf = ctx.enter_context(nc.sbuf_tensor("onescf", [128, 1], F32))
    onesrf = ctx.enter_context(nc.sbuf_tensor("onesrf", [1, 128], F32))
    ones11 = ctx.enter_context(nc.sbuf_tensor("ones11s", [1, 1], F32))
    onescb = ctx.enter_context(nc.sbuf_tensor("onescbs", [128, 1], BF16))
    scal = ctx.enter_context(nc.sbuf_tensor("scal", [1, 8], F32))
    scol = ctx.enter_context(nc.sbuf_tensor("scol", [128, 4], F32))

    PH = [0]
    def S(name, shape, dt):
        return nc.sbuf_tensor(f"{name}_p{PH[0]}", shape, dt)
    def PP(name, shape, dt=F32):
        return nc.psum_tensor(f"{name}_p{PH[0]}", shape, dt)

    with nc.Block() as blk:
        kb.blk = blk
        kb.step("sync", lambda e: [kb.dma(e, ident[:], g["ident"][:]),
                                   kb.dma(e, onescf[:], g["ones_col_f32"][:]),
                                   kb.dma(e, onesrf[:], g["ones_row_f32"][:]),
                                   kb.dma(e, ones11[:], g["ones11"][:])], n_dma=4)
        kb.step("vector", lambda e: e.tensor_copy(onescb[:], onescf[:]),
                deps=[("sync", kb.tick["sync"])])

    with ExitStack() as pc:
        wt = [pc.enter_context(S(f"wqt{i}", [128, HID], F32)) for i in range(2)]
        acc = pc.enter_context(S("wqacc", [128, 2], F32))
        w8 = pc.enter_context(S("wqw8", [128, HID], BF16))
        wi8 = pc.enter_context(S("wqi8", [128, HID], I8))
        wf8 = pc.enter_context(S("wqf8", [128, HID], F8))
        psr = pc.enter_context(PP("wqpsr", [1, 1], F32))
        psc = pc.enter_context(PP("wqpsc", [128, 1], F32))
        with nc.Block() as blk:
            kb.blk = blk
            ldconst = kb.tick["sync"]
            for wi, (src, dst, ntile, wid) in enumerate(
                    (("l0_w1t", "W1Q0", 8, HID), ("l0_w2t", "W2Q0", 32, D))):
                red = None
                for i in range(ntile):
                    buf = wt[i % 2]
                    ldt = kb.step("sync", (lambda e, buf=buf, i=i, src=src, wid=wid:
                                           kb.dma(e, buf[:, 0:wid],
                                                  g[src][i * 128:(i + 1) * 128, :])),
                                  deps=[red], n_dma=1)
                    if i == 0:
                        red = kb.step("vector", (lambda e, buf=buf, wid=wid:
                                                 e.tensor_reduce(acc[:, 0:1], buf[:, 0:wid],
                                                                 AX.X, ALU.add,
                                                                 apply_absolute_value=True)),
                                      deps=[ldt])
                    else:
                        red = kb.step("vector", (lambda e, buf=buf, wid=wid:
                                                 e.tensor_reduce(acc[:, 1:2], buf[:, 0:wid],
                                                                 AX.X, ALU.add,
                                                                 apply_absolute_value=True)),
                                      deps=[ldt])
                        red = kb.step("vector", lambda e: e.tensor_add(acc[:, 0:1],
                                                                       acc[:, 0:1],
                                                                       acc[:, 1:2]),
                                      deps=[red])
                t = kb.step("tensor", lambda e: e.matmul(psr[:], acc[:, 0:1], onescf[:],
                                                         start=True, stop=True), deps=[red])
                t = kb.step("vector", (lambda e, ntile=ntile, wid=wid:
                                       e.tensor_scalar(scal[0:1, 0:1], psr[:],
                                                       float(1.0 / (wid * ntile * 128)),
                                                       None, ALU.mult)), deps=[t])
                t = kb.step("vector", lambda e: e.tensor_scalar_max(
                    scal[0:1, 0:1], scal[0:1, 0:1], float(1e-5)), deps=[t])
                t = kb.step("vector", lambda e: e.reciprocal(scal[0:1, 1:2],
                                                             scal[0:1, 0:1]), deps=[t])
                t = kb.step("tensor", lambda e: e.matmul(psc[:], onesrf[:], scal[0:1, 1:2],
                                                         start=True, stop=True), deps=[t])
                t = kb.step("vector", (lambda e, wi=wi:
                                       e.tensor_copy(scol[:, wi:wi + 1], psc[:])), deps=[t])
                t = kb.step("tensor", lambda e: e.matmul(psc[:], onesrf[:], scal[0:1, 0:1],
                                                         start=True, stop=True), deps=[t])
                t = kb.step("vector", (lambda e, wi=wi:
                                       e.tensor_copy(scol[:, wi + 2:wi + 3], psc[:])), deps=[t])
                qd = t
                for i in range(ntile):
                    buf = wt[i % 2]
                    ldt = kb.step("sync", (lambda e, buf=buf, i=i, src=src, wid=wid:
                                           kb.dma(e, buf[:, 0:wid],
                                                  g[src][i * 128:(i + 1) * 128, :])),
                                  deps=[qd], n_dma=1)
                    qd = kb.step("vector", (lambda e, buf=buf, wid=wid, wi=wi:
                        e.tensor_scalar(buf[:, 0:wid], buf[:, 0:wid], scol[:, wi:wi + 1],
                                        None, ALU.mult)), deps=[ldt])
                    qd = kb.step("vector", (lambda e, buf=buf, wid=wid:
                        e.tensor_scalar(buf[:, 0:wid], buf[:, 0:wid], 12582912.0, None,
                                        ALU.add)), deps=[qd])
                    qd = kb.step("vector", (lambda e, buf=buf, wid=wid:
                        e.tensor_scalar(buf[:, 0:wid], buf[:, 0:wid], -12582912.0, None,
                                        ALU.add)), deps=[qd])
                    qd = kb.step("vector", (lambda e, buf=buf, wid=wid:
                        e.tensor_scalar(w8[:, 0:wid], buf[:, 0:wid], float(1.0), float(-1.0),
                                        ALU.min, ALU.max)), deps=[qd])
                    kb.step("sync", (lambda e, i=i, dst=dst, wid=wid:
                                     kb.dma(e, g[dst][i * 128:(i + 1) * 128, :],
                                            w8[:, 0:wid])), deps=[qd], n_dma=1)
            for src, dst, ntile, wid in (("l1_w1t", "W1Q1", 8, HID),
                                         ("l1_w2t", "W2Q1", 32, D)):
                qd = None
                for i in range(ntile):
                    buf = wt[i % 2]
                    ldt = kb.step("sync", (lambda e, buf=buf, i=i, src=src, wid=wid:
                                           kb.dma(e, buf[:, 0:wid],
                                                  g[src][i * 128:(i + 1) * 128, :])),
                                  deps=[qd], n_dma=1)
                    qd = kb.step("vector", (lambda e, buf=buf, wid=wid:
                        e.tensor_copy(wf8[:, 0:wid], buf[:, 0:wid])), deps=[ldt])
                    qd = kb.step("vector", (lambda e, buf=buf, wid=wid:
                        e.tensor_copy(w8[:, 0:wid], wf8[:, 0:wid])), deps=[qd])
                    kb.step("sync", (lambda e, i=i, dst=dst, wid=wid:
                                     kb.dma(e, g[dst][i * 128:(i + 1) * 128, :],
                                            w8[:, 0:wid])), deps=[qd], n_dma=1)


    def phase_entry():
        deps_all = [(en, kb.tick[en]) for en in KB.ENGS]
        for en in KB.ENGS:
            kb.step(en, lambda e: None, deps=deps_all)

    def emit_ln(deps, CL, src, dst, scratch, d, eps=np.float32(1e-5)):
        t = kb.step("vector", lambda e: e.tensor_reduce(CL[:, 0:1], src, AX.X, ALU.add),
                    deps=deps)
        t = kb.step("scalar", lambda e: e.activation(scratch, src, AF.Square, bias=f32(0.0),
                                                     scale=f32(1.0), accum_out=CL[:, 1:2]),
                    deps=[t])
        t = kb.step("vector", lambda e: [
            e.tensor_scalar(CL[:, 2:3], CL[:, 0:1], float(1.0 / d), None, ALU.mult),
            e.tensor_scalar(CL[:, 5:6], CL[:, 0:1], float(-1.0 / d), None, ALU.mult),
            e.tensor_scalar(CL[:, 4:5], CL[:, 1:2], float(1.0 / d), None, ALU.mult)], deps=[t])
        t = kb.step("vector", lambda e: e.tensor_tensor(CL[:, 3:4], CL[:, 2:3],
                                                        CL[:, 2:3], ALU.mult), deps=[t])
        t = kb.step("vector", lambda e: e.tensor_sub(CL[:, 4:5], CL[:, 4:5], CL[:, 3:4]),
                    deps=[t])
        t = kb.step("scalar", lambda e: e.activation(CL[:, 6:7], CL[:, 4:5], AF.Sqrt,
                                                     bias=eps, scale=f32(1.0)), deps=[t])
        t = kb.step("vector", lambda e: e.reciprocal(CL[:, 6:7], CL[:, 6:7]), deps=[t])
        t = kb.step("vector", lambda e: e.tensor_scalar(dst, src, CL[:, 5:6], CL[:, 6:7],
                                                        ALU.add, ALU.mult), deps=[t])
        return t

    def ffn_a(kind, first, fold):
        PH[0] += 1
        with ExitStack() as pc:
            RB = pc.enter_context(S("fa_rb", [128, D], F32))
            AB = pc.enter_context(S("fa_ab", [128, D], F32)) if fold else None
            U = pc.enter_context(S("fa_u", [128, D], F32))
            H = pc.enter_context(S("fa_h", [128, HID], F32))
            QB = pc.enter_context(S("fa_qb", [128, HID], BF16))
            QT = pc.enter_context(S("fa_qt", [128, D], BF16))
            TI8 = pc.enter_context(S("fa_i8", [128, HID], I8))
            FQ8 = pc.enter_context(S("fa_f8", [128, HID], F8))
            H16 = pc.enter_context(S("fa_h16", [128, HID], F16)) if kind == "l1" else None
            CL = pc.enter_context(S("fa_cl", [128, 8], F32))
            CF = pc.enter_context(S("fa_cf", [128, HID], F32))
            B1 = pc.enter_context(S("fa_b1", [128, HID], F32))
            W1 = [pc.enter_context(S(f"fa_w1{i}", [128, HID], BF16))
                  for i in range(8)]
            if kind == "l0":
                LN1W = pc.enter_context(S("fa_l1w", [128, D], F32))
                LN1B = pc.enter_context(S("fa_l1b", [128, D], F32))
                LN2W = pc.enter_context(S("fa_l2w", [128, HID], F32))
                LN2B = pc.enter_context(S("fa_l2b", [128, HID], F32))
            PSA = pc.enter_context(PP("fa_psa", [128, 512], F32))
            PSB = pc.enter_context(PP("fa_psb", [128, 512], F32))
            PST = pc.enter_context(PP("fa_pst", [128, 128], BF16))
            with nc.Block() as blk:
                kb.blk = blk
                phase_entry()
                w1src = g["W1Q0"] if kind == "l0" else g["W1Q1"]
                ldw = kb.step("sync", lambda e: [
                    *[kb.dma(e, W1[i][:], w1src[i * 128:(i + 1) * 128, :]) for i in range(8)],
                    kb.dma(e, B1[:], g["l0_b1_t" if kind == "l0" else "l1_b1_t"][:])],
                    n_dma=9)
                cdep = ldw
                if kind == "l0":
                    cdep = kb.step("sync", lambda e: [
                        kb.dma(e, LN1W[:], g["l0_ln1w_t"][:]),
                        kb.dma(e, LN1B[:], g["l0_ln1b_t"][:]),
                        kb.dma(e, LN2W[:], g["l0_ln2w_t"][:]),
                        kb.dma(e, LN2B[:], g["l0_ln2b_t"][:])], deps=[cdep], n_dma=4)
                else:
                    cdep = kb.step("vector", lambda e: e.tensor_copy(FQ8[:], B1[:]),
                                   deps=[cdep])
                    cdep = kb.step("vector", lambda e: e.tensor_copy(B1[:], FQ8[:]),
                                   deps=[cdep])

                prev = cdep
                for i in range(NT):
                    src = g["x_in"] if first else g["R"]
                    def ld_emit(e, i=i):
                        kb.dma(e, RB[:], src[i * 128:(i + 1) * 128, :])
                        if fold:
                            kb.dma(e, AB[:], g["A"][i * 128:(i + 1) * 128, :])
                    ldt = kb.step("sync", ld_emit, deps=[prev], n_dma=2 if fold else 1)
                    dep = ldt
                    if fold:
                        dep = kb.step("vector", lambda e: e.tensor_add(RB[:], RB[:], AB[:]),
                                      deps=[ldt])
                    t = emit_ln([dep], CL, RB[:], U[:], H[:, 0:D], D)
                    if kind == "l0":
                        t = emit_ln([t], CL, U[:], U[:], H[:, 0:D], D)
                        t = kb.step("vector", lambda e: e.tensor_mul(U[:], U[:], LN1W[:]),
                                    deps=[t])
                        t = kb.step("vector", lambda e: e.tensor_add(U[:], U[:], LN1B[:]),
                                    deps=[t])
                        t = kb.step("vector", lambda e: e.tensor_reduce(
                            CL[:, 0:1], U[:], AX.X, ALU.max, apply_absolute_value=True),
                                    deps=[t])
                        t = kb.step("vector", lambda e: e.tensor_scalar_max(
                            CL[:, 0:1], CL[:, 0:1], float(1e-5)), deps=[t])
                        t = kb.step("vector", lambda e: e.reciprocal(CL[:, 1:2], CL[:, 0:1]),
                                    deps=[t])
                        t = kb.step("vector", lambda e: e.tensor_scalar(
                            CL[:, 1:2], CL[:, 1:2], 127.0, None, ALU.mult), deps=[t])
                        t = kb.step("vector", lambda e: [
                            e.tensor_scalar(H[:, 0:D], U[:], CL[:, 1:2], None, ALU.mult),
                            e.tensor_scalar(CL[:, 7:8], CL[:, 0:1], scol[:, 2:3],
                                            float(1.0 / 127.0), ALU.mult, ALU.mult)], deps=[t])
                        t = kb.step("vector", lambda e: e.tensor_scalar(
                            H[:, 0:D], H[:, 0:D], 12582912.0, None, ALU.add), deps=[t])
                        t = kb.step("vector", lambda e: e.tensor_scalar(
                            QB[:, 0:D], H[:, 0:D], -12582912.0, None, ALU.add), deps=[t])
                    else:
                        t = kb.step("vector", lambda e: e.tensor_copy(FQ8[:, 0:D], U[:]),
                                    deps=[t])
                        t = kb.step("vector", lambda e: e.tensor_copy(QB[:, 0:D], FQ8[:, 0:D]),
                                    deps=[t])
                    ev = t
                    for k in range(8):
                        tk = kb.step("tensor", (lambda e, k=k:
                                                e.transpose(PST[:], QB[:, k * 128:(k + 1) * 128],
                                                            ident[:])), deps=[ev])
                        ev = kb.step("vector", (lambda e, k=k:
                                                e.tensor_copy(QT[:, k * 128:(k + 1) * 128],
                                                              PST[:])), deps=[tk])
                    hdone = []
                    for n in range(8):
                        ps = PSA if n % 2 == 0 else PSB
                        mdeps = [ev] if n < 2 else [ev, hdone[n - 2]]
                        def mm1(e, n=n, ps=ps):
                            for k in range(8):
                                e.matmul(ps[:], QT[:, k * 128:(k + 1) * 128],
                                         W1[k][:, n * 512:(n + 1) * 512],
                                         start=(k == 0), stop=(k == 7))
                        mt = kb.step("tensor", mm1, deps=mdeps)
                        if kind == "l0":
                            et = kb.step("vector", (lambda e, n=n, ps=ps:
                                                    e.scalar_tensor_tensor(
                                                        H[:, n * 512:(n + 1) * 512], ps[:],
                                                        CL[:, 7:8],
                                                        B1[:, n * 512:(n + 1) * 512],
                                                        ALU.mult, ALU.add)), deps=[mt])
                            et = kb.step("scalar", (lambda e, n=n:
                                                    e.activation(H[:, n * 512:(n + 1) * 512],
                                                                 H[:, n * 512:(n + 1) * 512],
                                                                 AF.Gelu, bias=f32(0.0),
                                                                 scale=f32(1.0))), deps=[et])
                        else:
                            et = kb.step("vector", (lambda e, n=n, ps=ps:
                                e.tensor_add(H[:, n * 512:(n + 1) * 512], ps[:],
                                             B1[:, n * 512:(n + 1) * 512])), deps=[mt])
                            et = kb.step("vector", (lambda e, n=n:
                                e.tensor_copy(H16[:, n * 512:(n + 1) * 512],
                                              H[:, n * 512:(n + 1) * 512])), deps=[et])
                            et = kb.step("vector", (lambda e, n=n:
                                e.tensor_scalar_max(H[:, n * 512:(n + 1) * 512],
                                                    H16[:, n * 512:(n + 1) * 512],
                                                    float(0.0))), deps=[et])
                        hdone.append(et)
                    t = kb.step("vector", lambda e: None, deps=hdone[-2:])
                    if kind == "l0":
                        t = emit_ln([t], CL, H[:], H[:], CF[:], HID)
                        t = kb.step("vector", lambda e: e.tensor_mul(H[:], H[:], LN2W[:]),
                                    deps=[t])
                        t = kb.step("vector", lambda e: e.tensor_add(H[:], H[:], LN2B[:]),
                                    deps=[t])
                        t = kb.step("vector", lambda e: e.tensor_reduce(
                            CL[:, 0:1], H[:], AX.X, ALU.max, apply_absolute_value=True),
                                    deps=[t])
                        t = kb.step("vector", lambda e: e.tensor_scalar_max(
                            CL[:, 0:1], CL[:, 0:1], float(1e-5)), deps=[t])
                        t = kb.step("vector", lambda e: e.reciprocal(CL[:, 1:2], CL[:, 0:1]),
                                    deps=[t])
                        t = kb.step("vector", lambda e: e.tensor_scalar(
                            CL[:, 1:2], CL[:, 1:2], 127.0, None, ALU.mult), deps=[t])
                        t = kb.step("vector", lambda e: [
                            e.tensor_scalar(H[:], H[:], CL[:, 1:2], None, ALU.mult),
                            e.tensor_scalar(CL[:, 7:8], CL[:, 0:1], scol[:, 3:4],
                                            float(1.0 / 127.0), ALU.mult, ALU.mult)], deps=[t])
                        t = kb.step("vector", lambda e: e.tensor_scalar(
                            H[:], H[:], 12582912.0, None, ALU.add), deps=[t])
                        t = kb.step("vector", lambda e: e.tensor_scalar(
                            QB[:], H[:], -12582912.0, None, ALU.add), deps=[t])
                        kb.step("sync", (lambda e, i=i:
                                         kb.dma(e, g["SC2"][i * 128:(i + 1) * 128, :],
                                                CL[:, 7:8])), deps=[t], n_dma=1)
                    else:
                        t = kb.step("vector", lambda e: e.tensor_copy(FQ8[:], H[:]),
                                    deps=[t])
                        t = kb.step("vector", lambda e: e.tensor_copy(QB[:], FQ8[:]),
                                    deps=[t])
                    prev = kb.step("sync", (lambda e, i=i:
                                            kb.dma(e, g["HQ"][i * 128:(i + 1) * 128, :],
                                                   QB[:])), deps=[t], n_dma=1)

    def ffn_b(kind, first, fold, out_final):
        PH[0] += 1
        with ExitStack() as pc:
            RB = pc.enter_context(S("fb_rb", [128, D], F32))
            AB = pc.enter_context(S("fb_ab", [128, D], F32)) if fold else None
            MBm = pc.enter_context(S("fb_mb", [128, D], F32)) if kind == "l1" else None
            U = pc.enter_context(S("fb_u", [128, D], F32))
            H16 = pc.enter_context(S("fb_h16", [128, D], F16)) if kind == "l1" else None
            QB = pc.enter_context(S("fb_qb", [128, HID], BF16))
            QT = pc.enter_context(S("fb_qt", [128, HID], BF16))
            CL = pc.enter_context(S("fb_cl", [128, 2], F32))
            B2 = pc.enter_context(S("fb_b2", [128, D], F32))
            FQ8 = pc.enter_context(S("fb_f8", [128, D], F8)) if kind == "l1" else None
            W2 = [pc.enter_context(S(f"fb_w2{i}", [128, D], BF16))
                  for i in range(32)]
            PSA = pc.enter_context(PP("fb_psa", [128, 512], F32))
            PSB = pc.enter_context(PP("fb_psb", [128, 512], F32))
            PST = pc.enter_context(PP("fb_pst", [128, 128], BF16))
            with nc.Block() as blk:
                kb.blk = blk
                phase_entry()
                w2src = g["W2Q0"] if kind == "l0" else g["W2Q1"]
                ldw = kb.step("sync", lambda e: [
                    *[kb.dma(e, W2[i][:], w2src[i * 128:(i + 1) * 128, :]) for i in range(32)],
                    kb.dma(e, B2[:], g["l0_b2_t" if kind == "l0" else "l1_b2_t"][:])],
                    n_dma=33)
                cdep = ldw
                if kind == "l1":
                    cdep = kb.step("vector", lambda e: e.tensor_copy(FQ8[:], B2[:]),
                                   deps=[ldw])
                    cdep = kb.step("vector", lambda e: e.tensor_copy(B2[:], FQ8[:]),
                                   deps=[cdep])
                prev = cdep
                for i in range(NT):
                    src = g["x_in"] if first else g["R"]
                    def ld_emit(e, i=i):
                        kb.dma(e, RB[:], src[i * 128:(i + 1) * 128, :])
                        kb.dma(e, QB[:], g["HQ"][i * 128:(i + 1) * 128, :])
                        if fold:
                            kb.dma(e, AB[:], g["A"][i * 128:(i + 1) * 128, :])
                        if kind == "l1":
                            kb.dma(e, MBm[:], g["mask_in"][i * 128:(i + 1) * 128, :])
                        if kind == "l0":
                            kb.dma(e, CL[:, 0:1], g["SC2"][i * 128:(i + 1) * 128, :])
                    nd = 2 + (1 if fold else 0) + 1
                    ldt = kb.step("sync", ld_emit, deps=[prev], n_dma=nd)
                    dep = ldt
                    if fold:
                        dep = kb.step("vector", lambda e: e.tensor_add(RB[:], RB[:], AB[:]),
                                      deps=[ldt])
                    ev = dep
                    for k in range(32):
                        tk = kb.step("tensor", (lambda e, k=k:
                                                e.transpose(PST[:], QB[:, k * 128:(k + 1) * 128],
                                                            ident[:])), deps=[ev])
                        ev = kb.step("vector", (lambda e, k=k:
                                                e.tensor_copy(QT[:, k * 128:(k + 1) * 128],
                                                              PST[:])), deps=[tk])
                    edone = []
                    for n in range(2):
                        ps = PSA if n % 2 == 0 else PSB
                        def mm2(e, n=n, ps=ps):
                            for k in range(32):
                                e.matmul(ps[:], QT[:, k * 128:(k + 1) * 128],
                                         W2[k][:, n * 512:(n + 1) * 512],
                                         start=(k == 0), stop=(k == 31))
                        mt = kb.step("tensor", mm2, deps=[ev])
                        sl = slice(n * 512, (n + 1) * 512)
                        if kind == "l0":
                            et = kb.step("vector", (lambda e, ps=ps, sl=sl:
                                e.scalar_tensor_tensor(U[:, sl], ps[:], CL[:, 0:1],
                                                       B2[:, sl], ALU.mult, ALU.add)),
                                         deps=[mt])
                            et = kb.step("vector", (lambda e, sl=sl:
                                e.scalar_tensor_tensor(RB[:, sl], U[:, sl], float(0.5),
                                                       RB[:, sl], ALU.mult, ALU.add)),
                                         deps=[et])
                        else:
                            et = kb.step("vector", (lambda e, ps=ps, sl=sl:
                                e.tensor_add(U[:, sl], ps[:], B2[:, sl])), deps=[mt])
                            et = kb.step("vector", (lambda e, sl=sl:
                                e.tensor_copy(H16[:, sl], U[:, sl])), deps=[et])
                            et = kb.step("vector", (lambda e, sl=sl:
                                e.tensor_tensor(U[:, sl], H16[:, sl], MBm[:, sl], ALU.mult)),
                                         deps=[et])
                            et = kb.step("vector", (lambda e, sl=sl:
                                e.scalar_tensor_tensor(RB[:, sl], U[:, sl], float(0.5),
                                                       RB[:, sl], ALU.mult, ALU.add)),
                                         deps=[et])
                        edone.append(et)
                    dst = g["y_out"] if out_final else g["R"]
                    prev = kb.step("sync", (lambda e, i=i, dst=dst:
                                            kb.dma(e, dst[i * 128:(i + 1) * 128, :], RB[:])),
                                   deps=edone, n_dma=1)

    def ut_phase():
        PH[0] += 1
        with ExitStack() as pc:
            RB = pc.enter_context(S("ut_rb", [128, D], F32))
            U = pc.enter_context(S("ut_u", [128, D], F32))
            UB = pc.enter_context(S("ut_ub", [128, D], BF16))
            UTT = pc.enter_context(S("ut_t", [128, D], BF16))
            CL = pc.enter_context(S("ut_cl", [128, 8], F32))
            PST = pc.enter_context(PP("ut_pst", [128, 128], BF16))
            with nc.Block() as blk:
                kb.blk = blk
                phase_entry()
                prev = None
                for i in range(NT):
                    ldt = kb.step("sync", (lambda e, i=i:
                                           kb.dma(e, RB[:], g["R"][i * 128:(i + 1) * 128, :])),
                                  deps=[prev], n_dma=1)
                    t = emit_ln([ldt], CL, RB[:], U[:], U[:], D)
                    t = kb.step("vector", lambda e: e.tensor_copy(UB[:], U[:]), deps=[t])
                    ev = t
                    for k in range(8):
                        tk = kb.step("tensor", (lambda e, k=k:
                                                e.transpose(PST[:], UB[:, k * 128:(k + 1) * 128],
                                                            ident[:])), deps=[ev])
                        ev = kb.step("vector", (lambda e, k=k:
                                                e.tensor_copy(UTT[:, k * 128:(k + 1) * 128],
                                                              PST[:])), deps=[tk])
                    def st_emit(e, i=i):
                        for k in range(8):
                            kb.dma(e, g["UT"][k * 128:(k + 1) * 128, i * 128:(i + 1) * 128],
                                   UTT[:, k * 128:(k + 1) * 128])
                    prev = kb.step("sync", st_emit, deps=[ev], n_dma=8)

    def l0_attn_phase():
        PH[0] += 1
        with ExitStack() as pc:
            UH = pc.enter_context(S("la_uh", [64, T], BF16))
            WDQ = pc.enter_context(S("la_wdq", [64, 512], BF16))
            WUQ = pc.enter_context(S("la_wuq", [128, 256], BF16))
            WDKV = pc.enter_context(S("la_wdkv", [64, 1024], BF16))
            WUKV = pc.enter_context(S("la_wukv", [128, 1024], BF16))
            WQR = pc.enter_context(S("la_wqr", [64, 64], BF16))
            WKR = pc.enter_context(S("la_wkr", [64, 64], BF16))
            WO = pc.enter_context(S("la_wo", [64, 64], BF16))
            QNW = pc.enter_context(S("la_qnw", [64, 1], F32))
            KNW = pc.enter_context(S("la_knw", [64, 1], F32))
            WF = pc.enter_context(S("la_wf", [128, 1024], F32))
            CQT = [pc.enter_context(S(f"la_cqt{l}", [128, T], BF16))
                   for l in range(4)]
            CKV = pc.enter_context(S("la_ckv", [128, 4096], BF16))
            QTt = pc.enter_context(S("la_qt", [128, T], BF16))
            KTt = pc.enter_context(S("la_kt", [128, T], BF16))
            VA = pc.enter_context(S("la_va", [128, 32 * 65], BF16))
            SQs = pc.enter_context(S("la_sq", [128, 512], F32))
            U64 = pc.enter_context(S("la_u64", [64, 512], F32))
            VTs = pc.enter_context(S("la_vt", [64, 512], BF16))
            RST = pc.enter_context(S("la_rst", [1, 512], F32))
            ROW = pc.enter_context(S("la_row", [1, 512], F32))
            MX = pc.enter_context(S("la_mx", [1, 4], F32))
            BCOL = pc.enter_context(S("la_bcol", [128, 1], F32))
            ZC = pc.enter_context(S("la_zc", [128, 1], F32))
            ES = pc.enter_context(S("la_es", [128, 512], BF16))
            CTXs = pc.enter_context(S("la_ctx", [64, 512], BF16))
            ZR = pc.enter_context(S("la_zr", [1, 512], F32))
            OH = pc.enter_context(S("la_oh", [128, 64], F32))
            TRI = pc.enter_context(S("la_tri", [128, 2048], BF16))
            PSA = pc.enter_context(PP("la_psa", [128, 512], F32))
            PSB = pc.enter_context(PP("la_psb", [64, 512], F32))
            PSS = pc.enter_context(PP("la_pss", [128, 512], F32))
            PS65 = pc.enter_context(PP("la_ps65", [65, 512], F32))
            PSO = pc.enter_context(PP("la_pso", [128, 64], F32))
            PST = pc.enter_context(PP("la_pst", [128, 128], BF16))
            with nc.Block() as blk:
                kb.blk = blk
                phase_entry()
                wdep = None
                for src, dst, p, w in (("w_dq_t", WDQ, 64, 512), ("w_uq_t", WUQ, 128, None),
                                       ("w_dkv_t", WDKV, 64, 1024), ("w_ukv_t", WUKV, 128, None),
                                       ("w_qr_t", WQR, 64, 64), ("w_kr_t", WKR, 64, 64),
                                       ("w_o_t", WO, 64, 64)):
                    if src == "w_uq_t":
                        def ld_uq(e, src=src):
                            for c in range(4):
                                kb.dma(e, WF[:, c * 64:(c + 1) * 64],
                                       g[src][c * 128:(c + 1) * 128, :])
                        lt = kb.step("sync", ld_uq, deps=[wdep], n_dma=4)
                        wdep = kb.step("vector", (lambda e, dst=dst:
                                                  e.tensor_copy(dst[:, 0:256], WF[:, 0:256])),
                                       deps=[lt])
                    elif src == "w_ukv_t":
                        def ld_ukv(e, src=src):
                            for c in range(8):
                                kb.dma(e, WF[:, c * 128:(c + 1) * 128],
                                       g[src][c * 128:(c + 1) * 128, :])
                        lt = kb.step("sync", ld_ukv, deps=[wdep], n_dma=8)
                        wdep = kb.step("vector", (lambda e, dst=dst:
                                                  e.tensor_copy(dst[:, 0:1024], WF[:, 0:1024])),
                                       deps=[lt])
                    else:
                        lt = kb.step("sync", (lambda e, src=src, p=p, w=w:
                                              kb.dma(e, WF[0:p, 0:w], g[src][:])),
                                     deps=[wdep], n_dma=1)
                        wdep = kb.step("vector", (lambda e, dst=dst, p=p, w=w:
                                                  e.tensor_copy(dst[0:p, 0:w], WF[0:p, 0:w])),
                                       deps=[lt])
                lt = kb.step("sync", lambda e: [kb.dma(e, QNW[:], g["qnw_c"][:]),
                                                kb.dma(e, KNW[:], g["knw_c"][:]),
                                                kb.dma(e, TRI[:], g["tri_masks"][:])],
                             deps=[wdep], n_dma=3)
                vset = kb.step("vector", lambda e: [
                    e.memset(VA[:, 64 + 65 * c:65 + 65 * c], 1.0) for c in range(32)],
                    deps=[lt])
                hprev = vset
                for h in range(16):
                    ldu = kb.step("sync", (lambda e, h=h:
                                           kb.dma(e, UH[:], g["UT"][h * 64:(h + 1) * 64, :])),
                                  deps=[hprev], n_dma=1)
                    tprev = ldu
                    for tc in range(8):
                        tsl = slice(tc * 512, (tc + 1) * 512)
                        for l in range(4):
                            mt = kb.step("tensor", (lambda e, l=l, tsl=tsl:
                                                    e.matmul(PSA[:], WDQ[:, l * 128:(l + 1) * 128],
                                                             UH[:, tsl], start=True, stop=True)),
                                         deps=[tprev])
                            tprev = kb.step("vector", (lambda e, l=l, tsl=tsl:
                                                       e.tensor_copy(CQT[l][:, tsl], PSA[:])),
                                            deps=[mt])
                        def mmqc(e, tsl=tsl):
                            for l in range(4):
                                e.matmul(PSB[:], WUQ[:, l * 64:(l + 1) * 64],
                                         CQT[l][:, tsl], start=(l == 0), stop=(l == 3))
                        mt = kb.step("tensor", mmqc, deps=[tprev])
                        t = kb.step("vector", lambda e: e.tensor_copy(U64[:], PSB[:]),
                                    deps=[mt])
                        t = kb.step("vector", lambda e: e.tensor_tensor(
                            SQs[0:64, :], U64[:], U64[:], ALU.mult), deps=[t])
                        t = kb.step("tensor", lambda e: e.matmul(PSS[0:1, :], onescf[0:64, :],
                                                                 SQs[0:64, :], start=True,
                                                                 stop=True), deps=[t])
                        t = kb.step("scalar", lambda e: e.activation(
                            RST[:], PSS[0:1, :], AF.Sqrt, bias=RMS_EPS,
                            scale=f32(1.0 / 64.0)), deps=[t])
                        t = kb.step("vector", lambda e: e.reciprocal(RST[:], RST[:]), deps=[t])
                        t = kb.step("tensor", lambda e: e.matmul(PSS[0:64, :], onesrf[0:1, 0:64],
                                                                 RST[:], start=True, stop=True),
                                    deps=[t])
                        t = kb.step("vector", lambda e: e.tensor_tensor(
                            U64[:], U64[:], PSS[0:64, :], ALU.mult), deps=[t])
                        t = kb.step("vector", (lambda e, tsl=tsl:
                            e.tensor_scalar(QTt[0:64, tsl], U64[:], QNW[:], float(0.125),
                                            ALU.mult, ALU.mult)), deps=[t])
                        mt = kb.step("tensor", (lambda e, tsl=tsl:
                                                e.matmul(PSB[:], WQR[:], QTt[0:64, tsl],
                                                         start=True, stop=True)), deps=[t])
                        t = kb.step("vector", (lambda e, tsl=tsl:
                                               e.tensor_copy(QTt[64:128, tsl], PSB[:])),
                                    deps=[mt])
                        for l in range(8):
                            mt = kb.step("tensor", (lambda e, l=l, tsl=tsl:
                                                    e.matmul(PSA[:], WDKV[:, l * 128:(l + 1) * 128],
                                                             UH[:, tsl], start=True, stop=True)),
                                         deps=[t])
                            t = kb.step("vector", (lambda e, l=l:
                                                   e.tensor_copy(CKV[:, l * 512:(l + 1) * 512],
                                                                 PSA[:])), deps=[mt])
                        def mmkv(e):
                            for l in range(8):
                                e.matmul(PSA[:], WUKV[:, l * 128:(l + 1) * 128],
                                         CKV[:, l * 512:(l + 1) * 512],
                                         start=(l == 0), stop=(l == 7))
                        mt = kb.step("tensor", mmkv, deps=[t])
                        t = kb.step("vector", lambda e: e.tensor_copy(U64[:], PSA[0:64, :]),
                                    deps=[mt])
                        t = kb.step("vector", lambda e: e.tensor_tensor(
                            SQs[0:64, :], U64[:], U64[:], ALU.mult), deps=[t])
                        t = kb.step("tensor", lambda e: e.matmul(PSS[0:1, :], onescf[0:64, :],
                                                                 SQs[0:64, :], start=True,
                                                                 stop=True), deps=[t])
                        t = kb.step("scalar", lambda e: e.activation(
                            RST[:], PSS[0:1, :], AF.Sqrt, bias=RMS_EPS,
                            scale=f32(1.0 / 64.0)), deps=[t])
                        t = kb.step("vector", lambda e: e.reciprocal(RST[:], RST[:]), deps=[t])
                        t = kb.step("tensor", lambda e: e.matmul(PSS[0:64, :], onesrf[0:1, 0:64],
                                                                 RST[:], start=True, stop=True),
                                    deps=[t])
                        t = kb.step("vector", lambda e: [
                            e.tensor_tensor(U64[:], U64[:], PSS[0:64, :], ALU.mult),
                            e.tensor_copy(VTs[:], PSA[64:128, :])], deps=[t])
                        t = kb.step("vector", (lambda e, tsl=tsl:
                            e.tensor_scalar(KTt[0:64, tsl], U64[:], KNW[:], None, ALU.mult)),
                                    deps=[t])
                        mt = kb.step("tensor", (lambda e, tsl=tsl:
                                                e.matmul(PSB[:], WKR[:], UH[:, tsl],
                                                         start=True, stop=True)), deps=[t])
                        t = kb.step("vector", (lambda e, tsl=tsl:
                                               e.tensor_copy(KTt[64:128, tsl], PSB[:])),
                                    deps=[mt])
                        for j in range(4):
                            tk = kb.step("tensor", (lambda e, j=j:
                                                    e.transpose(PST[0:128, 0:64],
                                                                VTs[:, j * 128:(j + 1) * 128],
                                                                ident[0:64, 0:64])), deps=[t])
                            t = kb.step("vector", (lambda e, j=j, tc=tc:
                                                   e.tensor_copy(
                                                       VA[:, (tc * 4 + j) * 65:(tc * 4 + j) * 65 + 64],
                                                       PST[0:128, 0:64])), deps=[tk])
                        tprev = t
                    t = kb.step("vector", lambda e: e.memset(MX[:, 0:2], 0.0), deps=[tprev])
                    for which, XT, mxi in (("q", QTt, 0), ("k", KTt, 1)):
                        for tc in range(8):
                            tsl = slice(tc * 512, (tc + 1) * 512)
                            t2 = kb.step("scalar", (lambda e, XT=XT, tsl=tsl:
                                                    e.activation(SQs[:], XT[:, tsl], AF.Square,
                                                                 bias=f32(0.0), scale=f32(1.0))),
                                         deps=[t])
                            t2 = kb.step("tensor", lambda e: e.matmul(PSS[0:1, :], onescf[:],
                                                                      SQs[:], start=True,
                                                                      stop=True), deps=[t2])
                            t2 = kb.step("vector", lambda e: e.tensor_copy(
                                ROW[:], PSS[0:1, :]), deps=[t2])
                            t2 = kb.step("vector", lambda e: e.tensor_reduce(
                                MX[:, 2:3], ROW[:], AX.X, ALU.max), deps=[t2])
                            t = kb.step("vector", (lambda e, mxi=mxi:
                                e.tensor_max(MX[:, mxi:mxi + 1], MX[:, mxi:mxi + 1],
                                             MX[:, 2:3])), deps=[t2])
                    t = kb.step("vector", lambda e: e.tensor_tensor(MX[:, 2:3], MX[:, 0:1],
                                                                    MX[:, 1:2], ALU.mult),
                                deps=[t])
                    t = kb.step("scalar", lambda e: e.activation(MX[:, 3:4], MX[:, 2:3],
                                                                 AF.Sqrt, bias=f32(0.0),
                                                                 scale=f32(1.0)), deps=[t])
                    t = kb.step("vector", lambda e: e.tensor_scalar(MX[:, 3:4], MX[:, 3:4],
                                                                    float(-1.0), None, ALU.mult),
                                deps=[t])
                    t = kb.step("tensor", lambda e: e.matmul(PSO[:, 0:1], onesrf[:], MX[:, 3:4],
                                                             start=True, stop=True), deps=[t])
                    t = kb.step("vector", lambda e: e.tensor_copy(BCOL[:], PSO[:, 0:1]),
                                deps=[t])
                    hq = t
                    for qt in range(8):
                        qsl = slice(qt * 512, (qt + 1) * 512)
                        nk = 4 * qt + 4
                        edep = hq
                        for kt in range(nk):
                            mt = kb.step("tensor", (lambda e, kt=kt, qsl=qsl:
                                                    e.matmul(PSS[:], KTt[:, kt * 128:(kt + 1) * 128],
                                                             QTt[:, qsl], start=True, stop=True)),
                                         deps=[edep])
                            et = kb.step("scalar", lambda e: e.activation(
                                ES[:], PSS[:], AF.Exp, bias=BCOL[:], scale=f32(1.0)),
                                deps=[mt])
                            if kt >= 4 * qt:
                                off = kt - 4 * qt
                                et = kb.step("vector", (lambda e, off=off:
                                                        e.tensor_tensor(ES[:], ES[:],
                                                                        TRI[:, off * 512:(off + 1) * 512],
                                                                        ALU.mult)), deps=[et])
                            edep = kb.step("tensor", (lambda e, kt=kt, nk=nk:
                                                      e.matmul(PS65[:], VA[:, kt * 65:kt * 65 + 65],
                                                               ES[:], start=(kt == 0),
                                                               stop=(kt == nk - 1))), deps=[et])
                        t = kb.step("vector", lambda e: [
                            e.tensor_copy(CTXs[:], PS65[0:64, :]),
                            e.tensor_copy(ZR[:], PS65[64:65, :])], deps=[edep])
                        t = kb.step("vector", lambda e: e.tensor_scalar(
                            ZR[:], ZR[:], 1e-30, None, ALU.add), deps=[t])
                        t = kb.step("vector", lambda e: e.reciprocal(ZR[:], ZR[:]), deps=[t])
                        pvt = t
                        for qs in range(4):
                            mt = kb.step("tensor", (lambda e, qs=qs:
                                                    e.matmul(PSO[:], CTXs[:, qs * 128:(qs + 1) * 128],
                                                             WO[:], start=True, stop=True)),
                                         deps=[t, pvt])
                            zt = kb.step("tensor", (lambda e, qs=qs:
                                                    e.matmul(PSS[0:128, 0:1],
                                                             ZR[0:1, qs * 128:(qs + 1) * 128],
                                                             ones11[:], start=True, stop=True)),
                                         deps=[t, pvt])
                            vt = kb.step("vector", lambda e: e.tensor_copy(
                                ZC[:], PSS[0:128, 0:1]), deps=[mt, zt, hq])
                            vt = kb.step("vector", lambda e: e.tensor_scalar(
                                OH[:], PSO[:], ZC[:], None, ALU.mult), deps=[vt])
                            pvt = vt
                            hq = kb.step("sync", (lambda e, qt=qt, qs=qs, h=h:
                                                  kb.dma(e, g["A"][(qt * 4 + qs) * 128:(qt * 4 + qs + 1) * 128,
                                                                   h * 64:(h + 1) * 64],
                                                         OH[:])), deps=[vt], n_dma=1)
                    hprev = hq

    def l1_attn_phase():
        PH[0] += 1
        SEG = 1024
        with ExitStack() as pc:
            RB = pc.enter_context(S("lb_rb", [128, D], F32))
            XN = pc.enter_context(S("lb_xn", [128, D], F32))
            CL = pc.enter_context(S("lb_cl", [128, 8], F32))
            LNG = pc.enter_context(S("lb_lng", [128, D], F32))
            LNB = pc.enter_context(S("lb_lnb", [128, D], F32))
            BQC = pc.enter_context(S("lb_bqc", [128, 24], F32))
            BGC = pc.enter_context(S("lb_bgc", [128, 8], F32))
            BPR = pc.enter_context(S("lb_bpr", [128, D], F32))
            XNT = pc.enter_context(S("lb_xnt", [128, 4096], BF16))
            XRT = pc.enter_context(S("lb_xrt", [128, 4096], BF16))
            WQKV = [pc.enter_context(S(f"lb_wq{i}", [128, 3 * D], BF16)) for i in range(8)]
            WG = [pc.enter_context(S(f"lb_wg{i}", [128, D], BF16)) for i in range(8)]
            WP = [pc.enter_context(S(f"lb_wp{i}", [128, D], BF16)) for i in range(8)]
            WF = pc.enter_context(S("lb_wf", [128, 3 * D], F32))
            QKV = [pc.enter_context(S(f"lb_qkv{i}", [128, 512], BF16)) for i in range(24)]
            GT = [pc.enter_context(S(f"lb_gt{i}", [128, 512], BF16)) for i in range(8)]
            GR = pc.enter_context(S("lb_gr", [1, 512], F32))
            EW = pc.enter_context(S("lb_ew", [128, 512], F32))
            EW2 = pc.enter_context(S("lb_ew2", [128, 512], F32))
            OT = [pc.enter_context(S(f"lb_ot{i}", [128, SEG], BF16)) for i in range(8)]
            KC = pc.enter_context(S("lb_kc", [128, 64], BF16))
            VC = pc.enter_context(S("lb_vc", [128, 65], BF16))
            SM = pc.enter_context(S("lb_sm", [128, 128], BF16))
            NTs = pc.enter_context(S("lb_nt", [64, 128], BF16))
            DR = pc.enter_context(S("lb_dr", [1, 128], F32))
            KVS = pc.enter_context(S("lb_kvs", [64, 1536], BF16))
            LTRI = pc.enter_context(S("lb_ltri", [128, 128], BF16))
            AB = pc.enter_context(S("lb_abf", [128, D], F32))
            XB = pc.enter_context(S("lb_xb", [128, D], BF16))
            QH0 = pc.enter_context(S("lb_qh0", [64, 128], BF16))
            PSA = pc.enter_context(PP("lb_psa", [128, 512]))
            PSR = pc.enter_context(PP("lb_psr", [1, 512]))
            PSG = pc.enter_context(PP("lb_psg", [128, 512]))
            PSN = pc.enter_context(PP("lb_psn", [65, 128]))
            PSK = pc.enter_context(PP("lb_psk", [64, 1536]))
            PST = pc.enter_context(PP("lb_pst", [128, 128], BF16))
            with nc.Block() as blk:
                kb.blk = blk
                phase_entry()
                ldt = None
                for i in range(8):
                    lt = kb.step("sync", (lambda e, i=i:
                                          kb.dma(e, WF[:, 0:3 * D],
                                                 g["w_qkv"][i * 128:(i + 1) * 128, :])),
                                 deps=[ldt], n_dma=1)
                    ldt = kb.step("vector", (lambda e, i=i:
                                             e.tensor_copy(WQKV[i][:], WF[:, 0:3 * D])),
                                  deps=[lt])
                for i in range(8):
                    lt = kb.step("sync", (lambda e, i=i:
                                          kb.dma(e, WF[:, 0:D],
                                                 g["w_gate"][i * 128:(i + 1) * 128, :])),
                                 deps=[ldt], n_dma=1)
                    ldt = kb.step("vector", (lambda e, i=i:
                                             e.tensor_copy(WG[i][:], WF[:, 0:D])), deps=[lt])
                for i in range(8):
                    lt = kb.step("sync", (lambda e, i=i:
                                          kb.dma(e, WF[:, 0:D],
                                                 g["w_proj"][i * 128:(i + 1) * 128, :])),
                                 deps=[ldt], n_dma=1)
                    ldt = kb.step("vector", (lambda e, i=i:
                                             e.tensor_copy(WP[i][:], WF[:, 0:D])), deps=[lt])
                ldt = kb.step("sync", lambda e: [
                    kb.dma(e, LNG[:], g["l1_lng_t"][:]), kb.dma(e, LNB[:], g["l1_lnb_t"][:]),
                    kb.dma(e, BQC[:], g["bqkv_cols"][:]), kb.dma(e, BGC[:], g["bgate_cols"][:]),
                    kb.dma(e, BPR[:], g["b_proj_t"][:]), kb.dma(e, LTRI[:], g["ltri_sm"][:])],
                    deps=[ldt], n_dma=6)
                t = kb.step("vector", lambda e: [
                    e.memset(KVS[:], 0.0),
                    e.memset(VC[:, 64:65], 1.0)], deps=[ldt])
                sprev = t
                stq = None
                for seg in range(4):
                    for half in range(2):
                        base_tok = seg * SEG + half * 512
                        t = sprev
                        for ti in range(4):
                            tok = base_tok + ti * 128
                            ldt = kb.step("sync", (lambda e, tok=tok:
                                                   kb.dma(e, RB[:], g["R"][tok:tok + 128, :])),
                                          deps=[t], n_dma=1)
                            t = emit_ln([ldt], CL, RB[:], XN[:], XN[:], D)
                            t = kb.step("vector", lambda e: e.tensor_mul(XN[:], XN[:], LNG[:]),
                                        deps=[t])
                            t = kb.step("vector", lambda e: e.tensor_add(XN[:], XN[:], LNB[:]),
                                        deps=[t])
                            t = kb.step("vector", lambda e: e.tensor_copy(XB[:], XN[:]),
                                        deps=[t])
                            for k in range(8):
                                tk = kb.step("tensor", (lambda e, k=k:
                                                        e.transpose(PST[:],
                                                                    XB[:, k * 128:(k + 1) * 128],
                                                                    ident[:])), deps=[t])
                                t = kb.step("vector", (lambda e, k=k, ti=ti:
                                                       e.tensor_copy(
                                                           XNT[:, k * 512 + ti * 128:k * 512 + (ti + 1) * 128],
                                                           PST[:])), deps=[tk])
                            t = kb.step("vector", lambda e: e.tensor_copy(XB[:], RB[:]),
                                        deps=[t])
                            for k in range(8):
                                tk = kb.step("tensor", (lambda e, k=k:
                                                        e.transpose(PST[:],
                                                                    XB[:, k * 128:(k + 1) * 128],
                                                                    ident[:])), deps=[t])
                                t = kb.step("vector", (lambda e, k=k, ti=ti:
                                                       e.tensor_copy(
                                                           XRT[:, k * 512 + ti * 128:k * 512 + (ti + 1) * 128],
                                                           PST[:])), deps=[tk])
                        gdone = []
                        for m in range(8):
                            def mmg(e, m=m):
                                for k in range(8):
                                    e.matmul(PSA[:], WG[k][:, m * 128:(m + 1) * 128],
                                             XRT[:, k * 512:(k + 1) * 512],
                                             start=(k == 0), stop=(k == 7))
                            mt = kb.step("tensor", mmg, deps=[t] + ([gdone[-1]] if gdone else []))
                            gt = kb.step("scalar", (lambda e, m=m:
                                                    e.activation(GT[m][:], PSA[:], AF.Sigmoid,
                                                                 bias=BGC[:, m:m + 1],
                                                                 scale=f32(1.0))), deps=[mt])
                            gdone.append(gt)
                        def mmsum(e):
                            for m in range(8):
                                e.matmul(PSR[:], onescb[:], GT[m][:],
                                         start=(m == 0), stop=(m == 7))
                        t = kb.step("tensor", mmsum, deps=gdone)
                        t = kb.step("vector", lambda e: e.tensor_scalar(
                            GR[:], PSR[:], float(1.0 / D), 1e-05, ALU.mult, ALU.add), deps=[t])
                        t = kb.step("vector", lambda e: e.reciprocal(GR[:], GR[:]), deps=[t])
                        t = kb.step("tensor", lambda e: e.matmul(PSG[:], onesrf[:], GR[:],
                                                                 start=True, stop=True), deps=[t])
                        gd = t
                        qdone = []
                        for m in range(24):
                            def mmq(e, m=m):
                                for k in range(8):
                                    e.matmul(PSA[:], WQKV[k][:, m * 128:(m + 1) * 128],
                                             XNT[:, k * 512:(k + 1) * 512],
                                             start=(k == 0), stop=(k == 7))
                            mt = kb.step("tensor", mmq, deps=[gd] + ([qdone[-1]] if qdone else []))
                            qt_ = kb.step("scalar", (lambda e, m=m:
                                                     e.activation(QKV[m][:], PSA[:], AF.Identity,
                                                                  bias=BQC[:, m:m + 1],
                                                                  scale=f32(1.0))), deps=[mt])
                            qdone.append(qt_)
                        gq = qdone[-1]
                        for m in range(16):
                            gq = kb.step("vector", (lambda e, m=m:
                                                    e.tensor_tensor(EW[:], QKV[m][:], PSG[:],
                                                                    ALU.mult)), deps=[gq, qdone[m]])
                            gq = kb.step("vector", lambda e: e.tensor_scalar_min(
                                EW2[:], EW[:], float(0.0)), deps=[gq])
                            gq = kb.step("scalar", lambda e: e.activation(
                                EW2[:], EW2[:], AF.Exp, bias=f32(0.0), scale=f32(1.0)),
                                deps=[gq])
                            gq = kb.step("vector", lambda e: e.tensor_scalar_max(
                                EW[:], EW[:], float(0.0)), deps=[gq])
                            gq = kb.step("vector", (lambda e, m=m:
                                                    e.tensor_add(QKV[m][:], EW2[:], EW[:])),
                                         deps=[gq])
                        sc = gq
                        for h in range(16):
                            mq = h // 2
                            r0 = (h % 2) * 64
                            hbase = (h // 7) * 512 + (h % 7) * 65
                            hs = slice(hbase, hbase + 65)
                            for c in range(4):
                                gc = seg * 8 + half * 4 + c
                                csl = slice(c * 128, (c + 1) * 128)
                                qh = QKV[mq]
                                kh = QKV[8 + mq]
                                vh = QKV[16 + mq]
                                tk = kb.step("tensor", (lambda e, kh=kh, r0=r0, csl=csl:
                                                        e.transpose(PST[0:128, 0:64],
                                                                    kh[r0:r0 + 64, csl],
                                                                    ident[r0:r0 + 64, r0:r0 + 64])),
                                             deps=[sc])
                                sc = kb.step("vector", lambda e: e.tensor_copy(
                                    KC[:], PST[0:128, 0:64]), deps=[tk])
                                tk = kb.step("tensor", (lambda e, vh=vh, r0=r0, csl=csl:
                                                        e.transpose(PST[0:128, 0:64],
                                                                    vh[r0:r0 + 64, csl],
                                                                    ident[r0:r0 + 64, r0:r0 + 64])),
                                             deps=[sc])
                                sc = kb.step("vector", lambda e: e.tensor_copy(
                                    VC[:, 0:64], PST[0:128, 0:64]), deps=[tk])
                                mt = kb.step("tensor", (lambda e, kh=kh, qh=qh, r0=r0, csl=csl:
                                                        e.matmul(PSA[:, 0:128],
                                                                 kh[r0:r0 + 64, csl],
                                                                 qh[r0:r0 + 64, csl],
                                                                 start=True, stop=True)),
                                             deps=[sc])
                                sc = kb.step("vector", lambda e: e.tensor_tensor(
                                    SM[:], PSA[:, 0:128], LTRI[:], ALU.mult), deps=[mt])
                                sc = kb.step("vector", (lambda e, qh=qh, r0=r0, csl=csl:
                                                        e.tensor_copy(QH0[:], qh[r0:r0 + 64, csl])),
                                             deps=[sc])
                                mt = kb.step("tensor", (lambda e, hs=hs:
                                                        [e.matmul(PSN[:], VC[:], SM[:],
                                                                  start=True, stop=False),
                                                         e.matmul(PSN[:], KVS[:, hs],
                                                                  QH0[:],
                                                                  start=False, stop=True)]),
                                             deps=[sc])
                                st_ = kb.step("tensor", (lambda e, hs=hs, gc=gc:
                                                         e.matmul(PSK[:, hs], KC[:], VC[:],
                                                                  start=(gc == 0),
                                                                  stop=(gc == 31),
                                                                  skip_group_check=True)),
                                              deps=[mt])
                                sc = kb.step("vector", (lambda e, hs=hs:
                                                        e.tensor_copy(KVS[:, hs], PSK[:, hs])),
                                             deps=[st_])
                                sc2 = kb.step("vector", lambda e: e.tensor_copy(
                                    NTs[:], PSN[0:64, :]), deps=[mt, sc])
                                sc2 = kb.step("vector", lambda e: e.tensor_scalar(
                                    DR[:], PSN[64:65, :], 1e-05, None, ALU.add), deps=[sc2])
                                sc2 = kb.step("vector", lambda e: e.reciprocal(DR[:], DR[:]),
                                              deps=[sc2])
                                bt = kb.step("tensor", lambda e: e.matmul(
                                    PSG[0:64, 0:128], onesrf[0:1, 0:64], DR[:],
                                    start=True, stop=True), deps=[sc2])
                                sc = kb.step("vector", (lambda e, mq=mq, r0=r0, half=half, csl=csl:
                                                        e.tensor_tensor(
                                                            OT[mq][r0:r0 + 64,
                                                                   half * 512 + csl.start:half * 512 + csl.stop],
                                                            NTs[:], PSG[0:64, 0:128],
                                                            ALU.mult)), deps=[bt, sc])
                        sprev = sc
                    pdone = sprev
                    for ti8 in range(8):
                        edone = []
                        for n in range(2):
                            def mmp(e, ti8=ti8, n=n):
                                for k in range(8):
                                    e.matmul(PSA[:], OT[k][:, ti8 * 128:(ti8 + 1) * 128],
                                             WP[k][:, n * 512:(n + 1) * 512],
                                             start=(k == 0), stop=(k == 7))
                            mt = kb.step("tensor", mmp, deps=[pdone])
                            et = kb.step("vector", (lambda e, n=n: e.tensor_add(
                                AB[:, n * 512:(n + 1) * 512], PSA[:],
                                BPR[:, n * 512:(n + 1) * 512])), deps=[mt])
                            edone.append(et)
                            pdone = et
                        stq = kb.step("sync", (lambda e, seg=seg, ti8=ti8:
                                               kb.dma(e, g["A"][seg * SEG + ti8 * 128:
                                                                seg * SEG + (ti8 + 1) * 128, :],
                                                      AB[:])), deps=edone + [stq], n_dma=1)
                        pdone = stq
                    sprev = stq


    seq = []
    seq.append(("ffn_a", dict(kind="l0", first=True, fold=False)))
    seq.append(("ffn_b", dict(kind="l0", first=True, fold=False, out_final=False)))
    if UPTO >= 2:
        seq.append(("ut", {}))
        seq.append(("l0at", {}))
    if UPTO >= 3:
        seq.append(("ffn_a", dict(kind="l0", first=False, fold=True)))
        seq.append(("ffn_b", dict(kind="l0", first=False, fold=True, out_final=False)))
    if UPTO >= 4:
        seq.append(("ffn_a", dict(kind="l1", first=False, fold=False)))
        seq.append(("ffn_b", dict(kind="l1", first=False, fold=False, out_final=False)))
    if UPTO >= 5:
        seq.append(("l1at", {}))
    if UPTO >= 6:
        seq.append(("ffn_a", dict(kind="l1", first=False, fold=True)))
        seq.append(("ffn_b", dict(kind="l1", first=False, fold=True, out_final=True)))
    final_is_R = UPTO < 6

    for name, kw in seq:
        if name == "ffn_a":
            ffn_a(**kw)
        elif name == "ffn_b":
            ffn_b(**kw)
        elif name == "ut":
            ut_phase()
        elif name == "l0at":
            l0_attn_phase()
        elif name == "l1at":
            l1_attn_phase()

    if final_is_R:
        with ExitStack() as pc:
            RB = pc.enter_context(S("cp_rb", [128, D], F32))
            AB = pc.enter_context(S("cp_ab", [128, D], F32))
            with nc.Block() as blk:
                kb.blk = blk
                phase_entry()
                prev = None
                fold_dbg = (UPTO in (2, 5))
                for i in range(NT):
                    nd = 2 if fold_dbg else 1
                    def ld(e, i=i):
                        kb.dma(e, RB[:], g["R"][i * 128:(i + 1) * 128, :])
                        if fold_dbg:
                            kb.dma(e, AB[:], g["A"][i * 128:(i + 1) * 128, :])
                    ldt = kb.step("sync", ld, deps=[prev], n_dma=nd)
                    dep = ldt
                    if fold_dbg:
                        dep = kb.step("vector", lambda e: e.tensor_add(RB[:], RB[:], AB[:]),
                                      deps=[ldt])
                    prev = kb.step("sync", (lambda e, i=i:
                                            kb.dma(e, g["y_out"][i * 128:(i + 1) * 128, :],
                                                   RB[:])), deps=[dep], n_dma=1)
        with nc.Block() as blk:
            kb.blk = blk
            phase_entry()
            kb.step("sync", lambda e: None, deps=[("sync", kb.tick["sync"])])

    return nc, g, ctx



_BUILD_CACHE = {}

def _tile_rows(v, n=128):
    return np.broadcast_to(np.asarray(v, np.float32).reshape(1, -1), (n, len(np.ravel(v)))).copy()

def kernel(**inputs):
    I = {k: np.ascontiguousarray(np.asarray(v, np.float32)) for k, v in inputs.items()}
    keep = dropout_keep_mask()
    maskscale = np.where(keep, np.float32(1.25), np.float32(0.0)).astype(np.float32)

    nc, g, ctx = _BUILD_CACHE.get("nc", (None, None, None))
    if nc is None:
        nc, g, ctx = build_program()
        _BUILD_CACHE["nc"] = (nc, g, ctx)

    ident = np.eye(128, dtype=ml_dtypes.bfloat16)
    tri = np.zeros((128, 4 * 512), dtype=ml_dtypes.bfloat16)
    for off in range(4):
        k_idx = np.arange(128)[:, None] + off * 128
        q_idx = np.arange(512)[None, :]
        tri[:, off * 512:(off + 1) * 512] = (k_idx <= q_idx).astype(ml_dtypes.bfloat16)
    ltri = (np.arange(128)[:, None] <= np.arange(128)[None, :]).astype(ml_dtypes.bfloat16)

    common = {
        "l0_w1t": I["L0_ffn_w1"].T.copy(), "l0_w2t": I["L0_ffn_w2"].T.copy(),
        "l1_w1t": I["L1_ffn_w1"].T.copy(), "l1_w2t": I["L1_ffn_w2"].T.copy(),
        "l0_ln1w_t": _tile_rows(I["L0_ffn_ln1_w"]), "l0_ln1b_t": _tile_rows(I["L0_ffn_ln1_b"]),
        "l0_ln2w_t": _tile_rows(I["L0_ffn_ln2_w"]), "l0_ln2b_t": _tile_rows(I["L0_ffn_ln2_b"]),
        "l0_b1_t": _tile_rows(I["L0_ffn_b1"]), "l0_b2_t": _tile_rows(I["L0_ffn_b2"]),
        "l1_b1_t": _tile_rows(I["L1_ffn_b1"]), "l1_b2_t": _tile_rows(I["L1_ffn_b2"]),
        "w_dq_t": I["L0_attn_W_DQ"].T.copy(), "w_uq_t": I["L0_attn_W_UQ"].T.copy(),
        "w_dkv_t": I["L0_attn_W_DKV"].T.copy(), "w_ukv_t": I["L0_attn_W_UKV"].T.copy(),
        "w_qr_t": I["L0_attn_W_QR"].T.copy(), "w_kr_t": I["L0_attn_W_KR"].T.copy(),
        "w_o_t": I["L0_attn_W_O"].T.copy(),
        "qnw_c": I["L0_attn_q_norm_w"].reshape(64, 1).copy(),
        "knw_c": I["L0_attn_k_norm_w"].reshape(64, 1).copy(),
        "w_qkv": I["L1_attn_w_qkv"].copy(),
        "bqkv_cols": I["L1_attn_b_qkv"].reshape(24, 128).T.copy(),
        "w_gate": I["L1_attn_w_gate"].copy(),
        "bgate_cols": I["L1_attn_b_gate"].reshape(8, 128).T.copy(),
        "w_proj": I["L1_attn_w_proj"].copy(),
        "b_proj_t": _tile_rows(I["L1_attn_b_proj"]),
        "l1_lng_t": _tile_rows(I["L1_attn_ln_g"]), "l1_lnb_t": _tile_rows(I["L1_attn_ln_b"]),
        "ident": ident, "ones_col_f32": np.ones((128, 1), np.float32),
        "ones_row_f32": np.ones((1, 128), np.float32),
        "ones11": np.ones((1, 1), np.float32),
        "tri_masks": tri, "ltri_sm": ltri,
    }
    in_maps = []
    for c in range(8):
        b = c % 4
        m = dict(common)
        m["x_in"] = I["L0_attn_x"][b]
        m["mask_in"] = maskscale[b]
        in_maps.append(m)
    res = run_bass_kernel_spmd(nc, in_maps, core_ids=list(range(8)))
    out = np.stack([res.results[c]["y_out"] for c in range(4)], axis=0)
    return out.astype(np.float32)


# version 02
# speedup vs baseline: 1.0983x; 1.0983x over previous
import os
import numpy as np
import ml_dtypes
import concourse.bass as bass
import concourse.mybir as mybir
from concourse.bass_utils import run_bass_kernel_spmd
from contextlib import ExitStack

F32 = mybir.dt.float32
BF16 = mybir.dt.bfloat16
F16 = mybir.dt.float16
F8 = mybir.dt.float8e4
I8 = mybir.dt.int8
AF = mybir.ActivationFunctionType
ALU = mybir.AluOpType
AX = mybir.AxisListType

B, T, D = 4, 4096, 1024
HID = 4096
NT = T // 128
f32 = np.float32
UPTO = int(os.environ.get("KMK_UPTO", "99"))
RMS_EPS = np.float32(1.1920929e-07)


def _rotl(x, d):
    return ((x << np.uint32(d)) | (x >> np.uint32(32 - d))).astype(np.uint32)

def _tf2x32(k1, k2, x0, x1):
    ks0 = np.uint32(k1); ks1 = np.uint32(k2)
    ks2 = np.uint32(ks0 ^ ks1 ^ np.uint32(0x1BD11BDA))
    x0 = (x0 + ks0).astype(np.uint32); x1 = (x1 + ks1).astype(np.uint32)
    rot = [[13, 15, 26, 6], [17, 29, 16, 24]]
    ks = [ks0, ks1, ks2]
    sched = [(rot[0], ks[1], ks[2], 1), (rot[1], ks[2], ks[0], 2),
             (rot[0], ks[0], ks[1], 3), (rot[1], ks[1], ks[2], 4),
             (rot[0], ks[2], ks[0], 5)]
    for rs, ka, kb_, i in sched:
        for r in rs:
            x0 = (x0 + x1).astype(np.uint32); x1 = _rotl(x1, r)
            x1 = (x1 ^ x0).astype(np.uint32)
        x0 = (x0 + ka).astype(np.uint32)
        x1 = (x1 + kb_ + np.uint32(i)).astype(np.uint32)
    return x0, x1

def dropout_keep_mask():
    n = B * T * D
    x64_mode = os.environ.get("BASS_SIM_FP64") == "1"
    out = np.empty(n, dtype=bool)
    step = 1 << 22
    for s in range(0, n, step):
        i = np.arange(s, min(n, s + step), dtype=np.uint64)
        hi = (i >> np.uint64(32)).astype(np.uint32)
        lo = (i & np.uint64(0xFFFFFFFF)).astype(np.uint32)
        o0, o1 = _tf2x32(0, 42, hi, lo)
        if not x64_mode:
            bits = (o0 ^ o1).astype(np.uint32)
            fb = ((bits >> np.uint32(9)) | np.uint32(0x3F800000)).astype(np.uint32)
            u = fb.view(np.float32) - np.float32(1.0)
            out[s:s + len(i)] = u < np.float32(0.8)
        else:
            bits = (o0.astype(np.uint64) << np.uint64(32)) | o1.astype(np.uint64)
            fb = (bits >> np.uint64(12)) | np.uint64(0x3FF0000000000000)
            u = fb.view(np.float64) - 1.0
            out[s:s + len(i)] = u < 0.8
    return out.reshape(B, T, D)


class KB:
    ENGS = ("sync", "vector", "scalar", "tensor", "gpsimd")

    def __init__(self, nc, ctx):
        self.nc = nc
        self.blk = None
        self.sem = {e: ctx.enter_context(nc.semaphore("S" + e)) for e in self.ENGS}
        self.tick = {e: 0 for e in self.ENGS}

    def step(self, eng, emit, deps=(), n_dma=0):
        sem = self.sem
        deps = [d for d in deps if d is not None]
        me = sem[eng]
        self.tick[eng] += 16 * n_dma if n_dma else 1
        ticket = self.tick[eng]

        def closure(e):
            for (s, v) in deps:
                e.wait_ge(sem[s], v)
            r = emit(e)
            if not n_dma:
                if isinstance(r, list):
                    r = r[-1] if r else None
                if r is not None and hasattr(r, "then_inc"):
                    r.then_inc(me, 1)
                else:
                    e.drain().then_inc(me, 1)
        getattr(self.blk, eng)(closure)
        return (eng, ticket)

    def dma(self, e, out, in_):
        e.dma_start(out=out, in_=in_).then_inc(self.sem["sync"], 16)


def build_program():
    nc = bass.Bass("TRN2", target_bir_lowering=False)
    g = {}

    def inp(name, shape, dt=F32):
        g[name] = nc.dram_tensor(name, shape, dt, kind="ExternalInput").ap()

    inp("x_in", [T, D]); inp("mask_in", [T, D])
    inp("l0_w1t", [D, HID]); inp("l0_w2t", [HID, D])
    inp("l1_w1t", [D, HID]); inp("l1_w2t", [HID, D])
    inp("l0_ln1w_t", [128, D]); inp("l0_ln1b_t", [128, D])
    inp("l0_ln2w_t", [128, HID]); inp("l0_ln2b_t", [128, HID])
    inp("l0_b1_t", [128, HID]); inp("l0_b2_t", [128, D])
    inp("l1_b1_t", [128, HID]); inp("l1_b2_t", [128, D])
    inp("w_dq_t", [64, 512]); inp("w_uq_t", [512, 64])
    inp("w_dkv_t", [64, 1024]); inp("w_ukv_t", [1024, 128])
    inp("w_qr_t", [64, 64]); inp("w_kr_t", [64, 64]); inp("w_o_t", [64, 64])
    inp("qnw_c", [64, 1]); inp("knw_c", [64, 1])
    inp("w_qkv", [D, 3 * D]); inp("bqkv_cols", [128, 24])
    inp("w_gate", [D, D]); inp("bgate_cols", [128, 8])
    inp("w_proj", [D, D]); inp("b_proj_t", [128, D])
    inp("l1_lng_t", [128, D]); inp("l1_lnb_t", [128, D])
    inp("ident", [128, 128], BF16)
    inp("ones_col_f32", [128, 1])
    inp("ones_row_f32", [1, 128])
    inp("ones11", [1, 1])
    inp("tri_masks", [128, 4 * 512], BF16)
    inp("ltri_sm", [128, 128], BF16)

    g["y_out"] = nc.dram_tensor("y_out", [T, D], F32, kind="ExternalOutput").ap()
    for n, shape, dt in (("R", [T, D], F32), ("A", [T, D], F32), ("SC2", [T, 1], F32),
                         ("W1Q0", [D, HID], BF16), ("W2Q0", [HID, D], BF16),
                         ("W1Q1", [D, HID], BF16), ("W2Q1", [HID, D], BF16),
                         ("HQ", [T, HID], BF16), ("UT", [D, T], BF16)):
        g[n] = nc.dram_tensor(n, shape, dt, kind="Internal").ap()

    ctx = ExitStack()
    kb = KB(nc, ctx)
    ident = ctx.enter_context(nc.sbuf_tensor("identt", [128, 128], BF16))
    onescf = ctx.enter_context(nc.sbuf_tensor("onescf", [128, 1], F32))
    onesrf = ctx.enter_context(nc.sbuf_tensor("onesrf", [1, 128], F32))
    ones11 = ctx.enter_context(nc.sbuf_tensor("ones11s", [1, 1], F32))
    onescb = ctx.enter_context(nc.sbuf_tensor("onescbs", [128, 1], BF16))
    scal = ctx.enter_context(nc.sbuf_tensor("scal", [1, 8], F32))
    scol = ctx.enter_context(nc.sbuf_tensor("scol", [128, 4], F32))

    PH = [0]
    def S(name, shape, dt):
        return nc.sbuf_tensor(f"{name}_p{PH[0]}", shape, dt)
    def PP(name, shape, dt=F32):
        return nc.psum_tensor(f"{name}_p{PH[0]}", shape, dt)

    with nc.Block() as blk:
        kb.blk = blk
        kb.step("sync", lambda e: [kb.dma(e, ident[:], g["ident"][:]),
                                   kb.dma(e, onescf[:], g["ones_col_f32"][:]),
                                   kb.dma(e, onesrf[:], g["ones_row_f32"][:]),
                                   kb.dma(e, ones11[:], g["ones11"][:])], n_dma=4)
        kb.step("vector", lambda e: e.tensor_copy(onescb[:], onescf[:]),
                deps=[("sync", kb.tick["sync"])])

    with ExitStack() as pc:
        wt = [pc.enter_context(S(f"wqt{i}", [128, HID], F32)) for i in range(2)]
        acc = pc.enter_context(S("wqacc", [128, 2], F32))
        w8 = pc.enter_context(S("wqw8", [128, HID], BF16))
        wi8 = pc.enter_context(S("wqi8", [128, HID], I8))
        wf8 = pc.enter_context(S("wqf8", [128, HID], F8))
        psr = pc.enter_context(PP("wqpsr", [1, 1], F32))
        psc = pc.enter_context(PP("wqpsc", [128, 1], F32))
        with nc.Block() as blk:
            kb.blk = blk
            ldconst = kb.tick["sync"]
            for wi, (src, dst, ntile, wid) in enumerate(
                    (("l0_w1t", "W1Q0", 8, HID), ("l0_w2t", "W2Q0", 32, D))):
                red = None
                for i in range(ntile):
                    buf = wt[i % 2]
                    ldt = kb.step("sync", (lambda e, buf=buf, i=i, src=src, wid=wid:
                                           kb.dma(e, buf[:, 0:wid],
                                                  g[src][i * 128:(i + 1) * 128, :])),
                                  deps=[red], n_dma=1)
                    if i == 0:
                        red = kb.step("vector", (lambda e, buf=buf, wid=wid:
                                                 e.tensor_reduce(acc[:, 0:1], buf[:, 0:wid],
                                                                 AX.X, ALU.add,
                                                                 apply_absolute_value=True)),
                                      deps=[ldt])
                    else:
                        red = kb.step("vector", (lambda e, buf=buf, wid=wid:
                                                 e.tensor_reduce(acc[:, 1:2], buf[:, 0:wid],
                                                                 AX.X, ALU.add,
                                                                 apply_absolute_value=True)),
                                      deps=[ldt])
                        red = kb.step("vector", lambda e: e.tensor_add(acc[:, 0:1],
                                                                       acc[:, 0:1],
                                                                       acc[:, 1:2]),
                                      deps=[red])
                t = kb.step("tensor", lambda e: e.matmul(psr[:], acc[:, 0:1], onescf[:],
                                                         start=True, stop=True), deps=[red])
                t = kb.step("vector", (lambda e, ntile=ntile, wid=wid:
                                       e.tensor_scalar(scal[0:1, 0:1], psr[:],
                                                       float(1.0 / (wid * ntile * 128)),
                                                       None, ALU.mult)), deps=[t])
                t = kb.step("vector", lambda e: e.tensor_scalar_max(
                    scal[0:1, 0:1], scal[0:1, 0:1], float(1e-5)), deps=[t])
                t = kb.step("vector", lambda e: e.reciprocal(scal[0:1, 1:2],
                                                             scal[0:1, 0:1]), deps=[t])
                t = kb.step("tensor", lambda e: e.matmul(psc[:], onesrf[:], scal[0:1, 1:2],
                                                         start=True, stop=True), deps=[t])
                t = kb.step("vector", (lambda e, wi=wi:
                                       e.tensor_copy(scol[:, wi:wi + 1], psc[:])), deps=[t])
                t = kb.step("tensor", lambda e: e.matmul(psc[:], onesrf[:], scal[0:1, 0:1],
                                                         start=True, stop=True), deps=[t])
                t = kb.step("vector", (lambda e, wi=wi:
                                       e.tensor_copy(scol[:, wi + 2:wi + 3], psc[:])), deps=[t])
                qd = t
                for i in range(ntile):
                    buf = wt[i % 2]
                    ldt = kb.step("sync", (lambda e, buf=buf, i=i, src=src, wid=wid:
                                           kb.dma(e, buf[:, 0:wid],
                                                  g[src][i * 128:(i + 1) * 128, :])),
                                  deps=[qd], n_dma=1)
                    qd = kb.step("vector", (lambda e, buf=buf, wid=wid, wi=wi:
                        e.tensor_scalar(buf[:, 0:wid], buf[:, 0:wid], scol[:, wi:wi + 1],
                                        None, ALU.mult)), deps=[ldt])
                    qd = kb.step("vector", (lambda e, buf=buf, wid=wid:
                        e.tensor_scalar(buf[:, 0:wid], buf[:, 0:wid], 12582912.0, None,
                                        ALU.add)), deps=[qd])
                    qd = kb.step("vector", (lambda e, buf=buf, wid=wid:
                        e.tensor_scalar(buf[:, 0:wid], buf[:, 0:wid], -12582912.0, None,
                                        ALU.add)), deps=[qd])
                    qd = kb.step("vector", (lambda e, buf=buf, wid=wid:
                        e.tensor_scalar(w8[:, 0:wid], buf[:, 0:wid], float(1.0), float(-1.0),
                                        ALU.min, ALU.max)), deps=[qd])
                    kb.step("sync", (lambda e, i=i, dst=dst, wid=wid:
                                     kb.dma(e, g[dst][i * 128:(i + 1) * 128, :],
                                            w8[:, 0:wid])), deps=[qd], n_dma=1)
            for src, dst, ntile, wid in (("l1_w1t", "W1Q1", 8, HID),
                                         ("l1_w2t", "W2Q1", 32, D)):
                qd = None
                for i in range(ntile):
                    buf = wt[i % 2]
                    ldt = kb.step("sync", (lambda e, buf=buf, i=i, src=src, wid=wid:
                                           kb.dma(e, buf[:, 0:wid],
                                                  g[src][i * 128:(i + 1) * 128, :])),
                                  deps=[qd], n_dma=1)
                    qd = kb.step("vector", (lambda e, buf=buf, wid=wid:
                        e.tensor_copy(wf8[:, 0:wid], buf[:, 0:wid])), deps=[ldt])
                    qd = kb.step("vector", (lambda e, buf=buf, wid=wid:
                        e.tensor_copy(w8[:, 0:wid], wf8[:, 0:wid])), deps=[qd])
                    kb.step("sync", (lambda e, i=i, dst=dst, wid=wid:
                                     kb.dma(e, g[dst][i * 128:(i + 1) * 128, :],
                                            w8[:, 0:wid])), deps=[qd], n_dma=1)


    def phase_entry():
        deps_all = [(en, kb.tick[en]) for en in KB.ENGS]
        for en in KB.ENGS:
            kb.step(en, lambda e: None, deps=deps_all)

    def emit_ln(deps, CL, src, dst, scratch, d, eps=np.float32(1e-5)):
        t = kb.step("vector", lambda e: e.tensor_reduce(CL[:, 0:1], src, AX.X, ALU.add),
                    deps=deps)
        t = kb.step("scalar", lambda e: e.activation(scratch, src, AF.Square, bias=f32(0.0),
                                                     scale=f32(1.0), accum_out=CL[:, 1:2]),
                    deps=[t])
        t = kb.step("vector", lambda e: [
            e.tensor_scalar(CL[:, 2:3], CL[:, 0:1], float(1.0 / d), None, ALU.mult),
            e.tensor_scalar(CL[:, 5:6], CL[:, 0:1], float(-1.0 / d), None, ALU.mult),
            e.tensor_scalar(CL[:, 4:5], CL[:, 1:2], float(1.0 / d), None, ALU.mult)], deps=[t])
        t = kb.step("vector", lambda e: e.tensor_tensor(CL[:, 3:4], CL[:, 2:3],
                                                        CL[:, 2:3], ALU.mult), deps=[t])
        t = kb.step("vector", lambda e: e.tensor_sub(CL[:, 4:5], CL[:, 4:5], CL[:, 3:4]),
                    deps=[t])
        t = kb.step("scalar", lambda e: e.activation(CL[:, 6:7], CL[:, 4:5], AF.Sqrt,
                                                     bias=eps, scale=f32(1.0)), deps=[t])
        t = kb.step("vector", lambda e: e.reciprocal(CL[:, 6:7], CL[:, 6:7]), deps=[t])
        t = kb.step("vector", lambda e: e.tensor_scalar(dst, src, CL[:, 5:6], CL[:, 6:7],
                                                        ALU.add, ALU.mult), deps=[t])
        return t

    def ffn_a(kind, first, fold):
        PH[0] += 1
        with ExitStack() as pc:
            RB = pc.enter_context(S("fa_rb", [128, D], F32))
            AB = pc.enter_context(S("fa_ab", [128, D], F32)) if fold else None
            U = pc.enter_context(S("fa_u", [128, D], F32))
            H = pc.enter_context(S("fa_h", [128, HID], F32))
            QB = pc.enter_context(S("fa_qb", [128, HID], BF16))
            QT = pc.enter_context(S("fa_qt", [128, D], BF16))
            TI8 = pc.enter_context(S("fa_i8", [128, HID], I8))
            FQ8 = pc.enter_context(S("fa_f8", [128, HID], F8))
            H16 = pc.enter_context(S("fa_h16", [128, HID], F16)) if kind == "l1" else None
            CL = pc.enter_context(S("fa_cl", [128, 8], F32))
            CF = pc.enter_context(S("fa_cf", [128, HID], F32))
            B1 = pc.enter_context(S("fa_b1", [128, HID], F32))
            W1 = [pc.enter_context(S(f"fa_w1{i}", [128, HID], BF16))
                  for i in range(8)]
            if kind == "l0":
                LN1W = pc.enter_context(S("fa_l1w", [128, D], F32))
                LN1B = pc.enter_context(S("fa_l1b", [128, D], F32))
                LN2W = pc.enter_context(S("fa_l2w", [128, HID], F32))
                LN2B = pc.enter_context(S("fa_l2b", [128, HID], F32))
            PSA = pc.enter_context(PP("fa_psa", [128, 512], F32))
            PSB = pc.enter_context(PP("fa_psb", [128, 512], F32))
            PST = pc.enter_context(PP("fa_pst", [128, 128], BF16))
            with nc.Block() as blk:
                kb.blk = blk
                phase_entry()
                w1src = g["W1Q0"] if kind == "l0" else g["W1Q1"]
                ldw = kb.step("sync", lambda e: [
                    *[kb.dma(e, W1[i][:], w1src[i * 128:(i + 1) * 128, :]) for i in range(8)],
                    kb.dma(e, B1[:], g["l0_b1_t" if kind == "l0" else "l1_b1_t"][:])],
                    n_dma=9)
                cdep = ldw
                if kind == "l0":
                    cdep = kb.step("sync", lambda e: [
                        kb.dma(e, LN1W[:], g["l0_ln1w_t"][:]),
                        kb.dma(e, LN1B[:], g["l0_ln1b_t"][:]),
                        kb.dma(e, LN2W[:], g["l0_ln2w_t"][:]),
                        kb.dma(e, LN2B[:], g["l0_ln2b_t"][:])], deps=[cdep], n_dma=4)
                else:
                    cdep = kb.step("vector", lambda e: e.tensor_copy(FQ8[:], B1[:]),
                                   deps=[cdep])
                    cdep = kb.step("vector", lambda e: e.tensor_copy(B1[:], FQ8[:]),
                                   deps=[cdep])

                prev = cdep
                for i in range(NT):
                    src = g["x_in"] if first else g["R"]
                    def ld_emit(e, i=i):
                        kb.dma(e, RB[:], src[i * 128:(i + 1) * 128, :])
                        if fold:
                            kb.dma(e, AB[:], g["A"][i * 128:(i + 1) * 128, :])
                    ldt = kb.step("sync", ld_emit, deps=[prev], n_dma=2 if fold else 1)
                    dep = ldt
                    if fold:
                        dep = kb.step("vector", lambda e: e.tensor_add(RB[:], RB[:], AB[:]),
                                      deps=[ldt])
                    t = emit_ln([dep], CL, RB[:], U[:], H[:, 0:D], D)
                    if kind == "l0":
                        t = emit_ln([t], CL, U[:], U[:], H[:, 0:D], D)
                        t = kb.step("vector", lambda e: e.tensor_mul(U[:], U[:], LN1W[:]),
                                    deps=[t])
                        t = kb.step("vector", lambda e: e.tensor_add(U[:], U[:], LN1B[:]),
                                    deps=[t])
                        t = kb.step("vector", lambda e: e.tensor_reduce(
                            CL[:, 0:1], U[:], AX.X, ALU.max, apply_absolute_value=True),
                                    deps=[t])
                        t = kb.step("vector", lambda e: e.tensor_scalar_max(
                            CL[:, 0:1], CL[:, 0:1], float(1e-5)), deps=[t])
                        t = kb.step("vector", lambda e: e.reciprocal(CL[:, 1:2], CL[:, 0:1]),
                                    deps=[t])
                        t = kb.step("vector", lambda e: e.tensor_scalar(
                            CL[:, 1:2], CL[:, 1:2], 127.0, None, ALU.mult), deps=[t])
                        t = kb.step("vector", lambda e: [
                            e.tensor_scalar(H[:, 0:D], U[:], CL[:, 1:2], None, ALU.mult),
                            e.tensor_scalar(CL[:, 7:8], CL[:, 0:1], scol[:, 2:3],
                                            float(1.0 / 127.0), ALU.mult, ALU.mult)], deps=[t])
                        t = kb.step("vector", lambda e: e.tensor_scalar(
                            H[:, 0:D], H[:, 0:D], 12582912.0, None, ALU.add), deps=[t])
                        t = kb.step("vector", lambda e: e.tensor_scalar(
                            QB[:, 0:D], H[:, 0:D], -12582912.0, None, ALU.add), deps=[t])
                    else:
                        t = kb.step("vector", lambda e: e.tensor_copy(FQ8[:, 0:D], U[:]),
                                    deps=[t])
                        t = kb.step("vector", lambda e: e.tensor_copy(QB[:, 0:D], FQ8[:, 0:D]),
                                    deps=[t])
                    ev = t
                    for k in range(8):
                        tk = kb.step("tensor", (lambda e, k=k:
                                                e.transpose(PST[:], QB[:, k * 128:(k + 1) * 128],
                                                            ident[:])), deps=[ev])
                        ev = kb.step("vector", (lambda e, k=k:
                                                e.tensor_copy(QT[:, k * 128:(k + 1) * 128],
                                                              PST[:])), deps=[tk])
                    hdone = []
                    for n in range(8):
                        ps = PSA if n % 2 == 0 else PSB
                        mdeps = [ev] if n < 2 else [ev, hdone[n - 2]]
                        def mm1(e, n=n, ps=ps):
                            for k in range(8):
                                e.matmul(ps[:], QT[:, k * 128:(k + 1) * 128],
                                         W1[k][:, n * 512:(n + 1) * 512],
                                         start=(k == 0), stop=(k == 7))
                        mt = kb.step("tensor", mm1, deps=mdeps)
                        if kind == "l0":
                            et = kb.step("vector", (lambda e, n=n, ps=ps:
                                                    e.scalar_tensor_tensor(
                                                        H[:, n * 512:(n + 1) * 512], ps[:],
                                                        CL[:, 7:8],
                                                        B1[:, n * 512:(n + 1) * 512],
                                                        ALU.mult, ALU.add)), deps=[mt])
                            et = kb.step("scalar", (lambda e, n=n:
                                                    e.activation(H[:, n * 512:(n + 1) * 512],
                                                                 H[:, n * 512:(n + 1) * 512],
                                                                 AF.Gelu, bias=f32(0.0),
                                                                 scale=f32(1.0))), deps=[et])
                        else:
                            et = kb.step("vector", (lambda e, n=n, ps=ps:
                                e.tensor_add(H[:, n * 512:(n + 1) * 512], ps[:],
                                             B1[:, n * 512:(n + 1) * 512])), deps=[mt])
                            et = kb.step("vector", (lambda e, n=n:
                                e.tensor_copy(H16[:, n * 512:(n + 1) * 512],
                                              H[:, n * 512:(n + 1) * 512])), deps=[et])
                            et = kb.step("vector", (lambda e, n=n:
                                e.tensor_scalar_max(H[:, n * 512:(n + 1) * 512],
                                                    H16[:, n * 512:(n + 1) * 512],
                                                    float(0.0))), deps=[et])
                        hdone.append(et)
                    t = kb.step("vector", lambda e: None, deps=hdone[-2:])
                    if kind == "l0":
                        t = emit_ln([t], CL, H[:], H[:], CF[:], HID)
                        t = kb.step("vector", lambda e: e.tensor_mul(H[:], H[:], LN2W[:]),
                                    deps=[t])
                        t = kb.step("vector", lambda e: e.tensor_add(H[:], H[:], LN2B[:]),
                                    deps=[t])
                        t = kb.step("vector", lambda e: e.tensor_reduce(
                            CL[:, 0:1], H[:], AX.X, ALU.max, apply_absolute_value=True),
                                    deps=[t])
                        t = kb.step("vector", lambda e: e.tensor_scalar_max(
                            CL[:, 0:1], CL[:, 0:1], float(1e-5)), deps=[t])
                        t = kb.step("vector", lambda e: e.reciprocal(CL[:, 1:2], CL[:, 0:1]),
                                    deps=[t])
                        t = kb.step("vector", lambda e: e.tensor_scalar(
                            CL[:, 1:2], CL[:, 1:2], 127.0, None, ALU.mult), deps=[t])
                        t = kb.step("vector", lambda e: [
                            e.tensor_scalar(H[:], H[:], CL[:, 1:2], None, ALU.mult),
                            e.tensor_scalar(CL[:, 7:8], CL[:, 0:1], scol[:, 3:4],
                                            float(1.0 / 127.0), ALU.mult, ALU.mult)], deps=[t])
                        t = kb.step("vector", lambda e: e.tensor_scalar(
                            H[:], H[:], 12582912.0, None, ALU.add), deps=[t])
                        t = kb.step("vector", lambda e: e.tensor_scalar(
                            QB[:], H[:], -12582912.0, None, ALU.add), deps=[t])
                        kb.step("sync", (lambda e, i=i:
                                         kb.dma(e, g["SC2"][i * 128:(i + 1) * 128, :],
                                                CL[:, 7:8])), deps=[t], n_dma=1)
                    else:
                        t = kb.step("vector", lambda e: e.tensor_copy(FQ8[:], H[:]),
                                    deps=[t])
                        t = kb.step("vector", lambda e: e.tensor_copy(QB[:], FQ8[:]),
                                    deps=[t])
                    prev = kb.step("sync", (lambda e, i=i:
                                            kb.dma(e, g["HQ"][i * 128:(i + 1) * 128, :],
                                                   QB[:])), deps=[t], n_dma=1)

    def ffn_b(kind, first, fold, out_final):
        PH[0] += 1
        with ExitStack() as pc:
            RB = pc.enter_context(S("fb_rb", [128, D], F32))
            AB = pc.enter_context(S("fb_ab", [128, D], F32)) if fold else None
            MBm = pc.enter_context(S("fb_mb", [128, D], F32)) if kind == "l1" else None
            U = pc.enter_context(S("fb_u", [128, D], F32))
            H16 = pc.enter_context(S("fb_h16", [128, D], F16)) if kind == "l1" else None
            QB = pc.enter_context(S("fb_qb", [128, HID], BF16))
            QT = pc.enter_context(S("fb_qt", [128, HID], BF16))
            CL = pc.enter_context(S("fb_cl", [128, 2], F32))
            B2 = pc.enter_context(S("fb_b2", [128, D], F32))
            FQ8 = pc.enter_context(S("fb_f8", [128, D], F8)) if kind == "l1" else None
            W2 = [pc.enter_context(S(f"fb_w2{i}", [128, D], BF16))
                  for i in range(32)]
            PSA = pc.enter_context(PP("fb_psa", [128, 512], F32))
            PSB = pc.enter_context(PP("fb_psb", [128, 512], F32))
            PST = pc.enter_context(PP("fb_pst", [128, 128], BF16))
            with nc.Block() as blk:
                kb.blk = blk
                phase_entry()
                w2src = g["W2Q0"] if kind == "l0" else g["W2Q1"]
                ldw = kb.step("sync", lambda e: [
                    *[kb.dma(e, W2[i][:], w2src[i * 128:(i + 1) * 128, :]) for i in range(32)],
                    kb.dma(e, B2[:], g["l0_b2_t" if kind == "l0" else "l1_b2_t"][:])],
                    n_dma=33)
                cdep = ldw
                if kind == "l1":
                    cdep = kb.step("vector", lambda e: e.tensor_copy(FQ8[:], B2[:]),
                                   deps=[ldw])
                    cdep = kb.step("vector", lambda e: e.tensor_copy(B2[:], FQ8[:]),
                                   deps=[cdep])
                prev = cdep
                for i in range(NT):
                    src = g["x_in"] if first else g["R"]
                    def ld_emit(e, i=i):
                        kb.dma(e, RB[:], src[i * 128:(i + 1) * 128, :])
                        kb.dma(e, QB[:], g["HQ"][i * 128:(i + 1) * 128, :])
                        if fold:
                            kb.dma(e, AB[:], g["A"][i * 128:(i + 1) * 128, :])
                        if kind == "l1":
                            kb.dma(e, MBm[:], g["mask_in"][i * 128:(i + 1) * 128, :])
                        if kind == "l0":
                            kb.dma(e, CL[:, 0:1], g["SC2"][i * 128:(i + 1) * 128, :])
                    nd = 2 + (1 if fold else 0) + 1
                    ldt = kb.step("sync", ld_emit, deps=[prev], n_dma=nd)
                    dep = ldt
                    if fold:
                        dep = kb.step("vector", lambda e: e.tensor_add(RB[:], RB[:], AB[:]),
                                      deps=[ldt])
                    ev = dep
                    for k in range(32):
                        tk = kb.step("tensor", (lambda e, k=k:
                                                e.transpose(PST[:], QB[:, k * 128:(k + 1) * 128],
                                                            ident[:])), deps=[ev])
                        ev = kb.step("vector", (lambda e, k=k:
                                                e.tensor_copy(QT[:, k * 128:(k + 1) * 128],
                                                              PST[:])), deps=[tk])
                    edone = []
                    for n in range(2):
                        ps = PSA if n % 2 == 0 else PSB
                        def mm2(e, n=n, ps=ps):
                            for k in range(32):
                                e.matmul(ps[:], QT[:, k * 128:(k + 1) * 128],
                                         W2[k][:, n * 512:(n + 1) * 512],
                                         start=(k == 0), stop=(k == 31))
                        mt = kb.step("tensor", mm2, deps=[ev])
                        sl = slice(n * 512, (n + 1) * 512)
                        if kind == "l0":
                            et = kb.step("vector", (lambda e, ps=ps, sl=sl:
                                e.scalar_tensor_tensor(U[:, sl], ps[:], CL[:, 0:1],
                                                       B2[:, sl], ALU.mult, ALU.add)),
                                         deps=[mt])
                            et = kb.step("vector", (lambda e, sl=sl:
                                e.scalar_tensor_tensor(RB[:, sl], U[:, sl], float(0.5),
                                                       RB[:, sl], ALU.mult, ALU.add)),
                                         deps=[et])
                        else:
                            et = kb.step("vector", (lambda e, ps=ps, sl=sl:
                                e.tensor_add(U[:, sl], ps[:], B2[:, sl])), deps=[mt])
                            et = kb.step("vector", (lambda e, sl=sl:
                                e.tensor_copy(H16[:, sl], U[:, sl])), deps=[et])
                            et = kb.step("vector", (lambda e, sl=sl:
                                e.tensor_tensor(U[:, sl], H16[:, sl], MBm[:, sl], ALU.mult)),
                                         deps=[et])
                            et = kb.step("vector", (lambda e, sl=sl:
                                e.scalar_tensor_tensor(RB[:, sl], U[:, sl], float(0.5),
                                                       RB[:, sl], ALU.mult, ALU.add)),
                                         deps=[et])
                        edone.append(et)
                    dst = g["y_out"] if out_final else g["R"]
                    prev = kb.step("sync", (lambda e, i=i, dst=dst:
                                            kb.dma(e, dst[i * 128:(i + 1) * 128, :], RB[:])),
                                   deps=edone, n_dma=1)

    def ut_phase():
        PH[0] += 1
        with ExitStack() as pc:
            RB = pc.enter_context(S("ut_rb", [128, D], F32))
            U = pc.enter_context(S("ut_u", [128, D], F32))
            UB = pc.enter_context(S("ut_ub", [128, D], BF16))
            UTT = pc.enter_context(S("ut_t", [128, D], BF16))
            CL = pc.enter_context(S("ut_cl", [128, 8], F32))
            PST = pc.enter_context(PP("ut_pst", [128, 128], BF16))
            with nc.Block() as blk:
                kb.blk = blk
                phase_entry()
                prev = None
                for i in range(NT):
                    ldt = kb.step("sync", (lambda e, i=i:
                                           kb.dma(e, RB[:], g["R"][i * 128:(i + 1) * 128, :])),
                                  deps=[prev], n_dma=1)
                    t = emit_ln([ldt], CL, RB[:], U[:], U[:], D)
                    t = kb.step("vector", lambda e: e.tensor_copy(UB[:], U[:]), deps=[t])
                    ev = t
                    for k in range(8):
                        tk = kb.step("tensor", (lambda e, k=k:
                                                e.transpose(PST[:], UB[:, k * 128:(k + 1) * 128],
                                                            ident[:])), deps=[ev])
                        ev = kb.step("vector", (lambda e, k=k:
                                                e.tensor_copy(UTT[:, k * 128:(k + 1) * 128],
                                                              PST[:])), deps=[tk])
                    def st_emit(e, i=i):
                        for k in range(8):
                            kb.dma(e, g["UT"][k * 128:(k + 1) * 128, i * 128:(i + 1) * 128],
                                   UTT[:, k * 128:(k + 1) * 128])
                    prev = kb.step("sync", st_emit, deps=[ev], n_dma=8)

    def l0_attn_phase():
        PH[0] += 1
        with ExitStack() as pc:
            UH = pc.enter_context(S("la_uh", [64, T], BF16))
            WDQ = pc.enter_context(S("la_wdq", [64, 512], BF16))
            WUQ = pc.enter_context(S("la_wuq", [128, 256], BF16))
            WDKV = pc.enter_context(S("la_wdkv", [64, 1024], BF16))
            WUKV = pc.enter_context(S("la_wukv", [128, 1024], BF16))
            WQR = pc.enter_context(S("la_wqr", [64, 64], BF16))
            WKR = pc.enter_context(S("la_wkr", [64, 64], BF16))
            WO = pc.enter_context(S("la_wo", [64, 64], BF16))
            QNW = pc.enter_context(S("la_qnw", [64, 1], F32))
            KNW = pc.enter_context(S("la_knw", [64, 1], F32))
            WF = pc.enter_context(S("la_wf", [128, 1024], F32))
            CQT = [pc.enter_context(S(f"la_cqt{l}", [128, T], BF16))
                   for l in range(4)]
            CKV = pc.enter_context(S("la_ckv", [128, 4096], BF16))
            QTt = pc.enter_context(S("la_qt", [128, T], BF16))
            KTt = pc.enter_context(S("la_kt", [128, T], BF16))
            VA = pc.enter_context(S("la_va", [128, 32 * 65], BF16))
            SQs = pc.enter_context(S("la_sq", [128, 512], F32))
            U64 = pc.enter_context(S("la_u64", [64, 512], F32))
            VTs = pc.enter_context(S("la_vt", [64, 512], BF16))
            RST = pc.enter_context(S("la_rst", [1, 512], F32))
            ROW = pc.enter_context(S("la_row", [1, 512], F32))
            MX = pc.enter_context(S("la_mx", [1, 4], F32))
            BCOL = pc.enter_context(S("la_bcol", [128, 1], F32))
            ZC = pc.enter_context(S("la_zc", [128, 1], F32))
            ES = pc.enter_context(S("la_es", [128, 512], BF16))
            CTXs = pc.enter_context(S("la_ctx", [64, 512], BF16))
            ZR = pc.enter_context(S("la_zr", [1, 512], F32))
            OH = pc.enter_context(S("la_oh", [128, 64], F32))
            TRI = pc.enter_context(S("la_tri", [128, 2048], BF16))
            PSA = pc.enter_context(PP("la_psa", [128, 512], F32))
            PSB = pc.enter_context(PP("la_psb", [64, 512], F32))
            PSS = pc.enter_context(PP("la_pss", [128, 512], F32))
            PS65 = pc.enter_context(PP("la_ps65", [65, 512], F32))
            PSO = pc.enter_context(PP("la_pso", [128, 64], F32))
            PST = pc.enter_context(PP("la_pst", [128, 128], BF16))
            with nc.Block() as blk:
                kb.blk = blk
                phase_entry()
                wdep = None
                for src, dst, p, w in (("w_dq_t", WDQ, 64, 512), ("w_uq_t", WUQ, 128, None),
                                       ("w_dkv_t", WDKV, 64, 1024), ("w_ukv_t", WUKV, 128, None),
                                       ("w_qr_t", WQR, 64, 64), ("w_kr_t", WKR, 64, 64),
                                       ("w_o_t", WO, 64, 64)):
                    if src == "w_uq_t":
                        def ld_uq(e, src=src):
                            for c in range(4):
                                kb.dma(e, WF[:, c * 64:(c + 1) * 64],
                                       g[src][c * 128:(c + 1) * 128, :])
                        lt = kb.step("sync", ld_uq, deps=[wdep], n_dma=4)
                        wdep = kb.step("vector", (lambda e, dst=dst:
                                                  e.tensor_copy(dst[:, 0:256], WF[:, 0:256])),
                                       deps=[lt])
                    elif src == "w_ukv_t":
                        def ld_ukv(e, src=src):
                            for c in range(8):
                                kb.dma(e, WF[:, c * 128:(c + 1) * 128],
                                       g[src][c * 128:(c + 1) * 128, :])
                        lt = kb.step("sync", ld_ukv, deps=[wdep], n_dma=8)
                        wdep = kb.step("vector", (lambda e, dst=dst:
                                                  e.tensor_copy(dst[:, 0:1024], WF[:, 0:1024])),
                                       deps=[lt])
                    else:
                        lt = kb.step("sync", (lambda e, src=src, p=p, w=w:
                                              kb.dma(e, WF[0:p, 0:w], g[src][:])),
                                     deps=[wdep], n_dma=1)
                        wdep = kb.step("vector", (lambda e, dst=dst, p=p, w=w:
                                                  e.tensor_copy(dst[0:p, 0:w], WF[0:p, 0:w])),
                                       deps=[lt])
                lt = kb.step("sync", lambda e: [kb.dma(e, QNW[:], g["qnw_c"][:]),
                                                kb.dma(e, KNW[:], g["knw_c"][:]),
                                                kb.dma(e, TRI[:], g["tri_masks"][:])],
                             deps=[wdep], n_dma=3)
                vset = kb.step("vector", lambda e: [
                    e.memset(VA[:, 64 + 65 * c:65 + 65 * c], 1.0) for c in range(32)],
                    deps=[lt])
                hprev = vset
                for h in range(16):
                    ldu = kb.step("sync", (lambda e, h=h:
                                           kb.dma(e, UH[:], g["UT"][h * 64:(h + 1) * 64, :])),
                                  deps=[hprev], n_dma=1)
                    tprev = ldu
                    for tc in range(8):
                        tsl = slice(tc * 512, (tc + 1) * 512)
                        for l in range(4):
                            mt = kb.step("tensor", (lambda e, l=l, tsl=tsl:
                                                    e.matmul(PSA[:], WDQ[:, l * 128:(l + 1) * 128],
                                                             UH[:, tsl], start=True, stop=True)),
                                         deps=[tprev])
                            tprev = kb.step("vector", (lambda e, l=l, tsl=tsl:
                                                       e.tensor_copy(CQT[l][:, tsl], PSA[:])),
                                            deps=[mt])
                        def mmqc(e, tsl=tsl):
                            for l in range(4):
                                e.matmul(PSB[:], WUQ[:, l * 64:(l + 1) * 64],
                                         CQT[l][:, tsl], start=(l == 0), stop=(l == 3))
                        mt = kb.step("tensor", mmqc, deps=[tprev])
                        t = kb.step("vector", lambda e: e.tensor_copy(U64[:], PSB[:]),
                                    deps=[mt])
                        t = kb.step("vector", lambda e: e.tensor_tensor(
                            SQs[0:64, :], U64[:], U64[:], ALU.mult), deps=[t])
                        t = kb.step("tensor", lambda e: e.matmul(PSS[0:1, :], onescf[0:64, :],
                                                                 SQs[0:64, :], start=True,
                                                                 stop=True), deps=[t])
                        t = kb.step("scalar", lambda e: e.activation(
                            RST[:], PSS[0:1, :], AF.Sqrt, bias=RMS_EPS,
                            scale=f32(1.0 / 64.0)), deps=[t])
                        t = kb.step("vector", lambda e: e.reciprocal(RST[:], RST[:]), deps=[t])
                        t = kb.step("tensor", lambda e: e.matmul(PSS[0:64, :], onesrf[0:1, 0:64],
                                                                 RST[:], start=True, stop=True),
                                    deps=[t])
                        t = kb.step("vector", lambda e: e.tensor_tensor(
                            U64[:], U64[:], PSS[0:64, :], ALU.mult), deps=[t])
                        t = kb.step("vector", (lambda e, tsl=tsl:
                            e.tensor_scalar(QTt[0:64, tsl], U64[:], QNW[:], float(0.125),
                                            ALU.mult, ALU.mult)), deps=[t])
                        mt = kb.step("tensor", (lambda e, tsl=tsl:
                                                e.matmul(PSB[:], WQR[:], QTt[0:64, tsl],
                                                         start=True, stop=True)), deps=[t])
                        t = kb.step("vector", (lambda e, tsl=tsl:
                                               e.tensor_copy(QTt[64:128, tsl], PSB[:])),
                                    deps=[mt])
                        for l in range(8):
                            mt = kb.step("tensor", (lambda e, l=l, tsl=tsl:
                                                    e.matmul(PSA[:], WDKV[:, l * 128:(l + 1) * 128],
                                                             UH[:, tsl], start=True, stop=True)),
                                         deps=[t])
                            t = kb.step("vector", (lambda e, l=l:
                                                   e.tensor_copy(CKV[:, l * 512:(l + 1) * 512],
                                                                 PSA[:])), deps=[mt])
                        def mmkv(e):
                            for l in range(8):
                                e.matmul(PSA[:], WUKV[:, l * 128:(l + 1) * 128],
                                         CKV[:, l * 512:(l + 1) * 512],
                                         start=(l == 0), stop=(l == 7))
                        mt = kb.step("tensor", mmkv, deps=[t])
                        t = kb.step("vector", lambda e: e.tensor_copy(U64[:], PSA[0:64, :]),
                                    deps=[mt])
                        t = kb.step("vector", lambda e: e.tensor_tensor(
                            SQs[0:64, :], U64[:], U64[:], ALU.mult), deps=[t])
                        t = kb.step("tensor", lambda e: e.matmul(PSS[0:1, :], onescf[0:64, :],
                                                                 SQs[0:64, :], start=True,
                                                                 stop=True), deps=[t])
                        t = kb.step("scalar", lambda e: e.activation(
                            RST[:], PSS[0:1, :], AF.Sqrt, bias=RMS_EPS,
                            scale=f32(1.0 / 64.0)), deps=[t])
                        t = kb.step("vector", lambda e: e.reciprocal(RST[:], RST[:]), deps=[t])
                        t = kb.step("tensor", lambda e: e.matmul(PSS[0:64, :], onesrf[0:1, 0:64],
                                                                 RST[:], start=True, stop=True),
                                    deps=[t])
                        t = kb.step("vector", lambda e: [
                            e.tensor_tensor(U64[:], U64[:], PSS[0:64, :], ALU.mult),
                            e.tensor_copy(VTs[:], PSA[64:128, :])], deps=[t])
                        t = kb.step("vector", (lambda e, tsl=tsl:
                            e.tensor_scalar(KTt[0:64, tsl], U64[:], KNW[:], None, ALU.mult)),
                                    deps=[t])
                        mt = kb.step("tensor", (lambda e, tsl=tsl:
                                                e.matmul(PSB[:], WKR[:], UH[:, tsl],
                                                         start=True, stop=True)), deps=[t])
                        t = kb.step("vector", (lambda e, tsl=tsl:
                                               e.tensor_copy(KTt[64:128, tsl], PSB[:])),
                                    deps=[mt])
                        for j in range(4):
                            tk = kb.step("tensor", (lambda e, j=j:
                                                    e.transpose(PST[0:128, 0:64],
                                                                VTs[:, j * 128:(j + 1) * 128],
                                                                ident[0:64, 0:64])), deps=[t])
                            t = kb.step("vector", (lambda e, j=j, tc=tc:
                                                   e.tensor_copy(
                                                       VA[:, (tc * 4 + j) * 65:(tc * 4 + j) * 65 + 64],
                                                       PST[0:128, 0:64])), deps=[tk])
                        tprev = t
                    t = kb.step("vector", lambda e: e.memset(MX[:, 0:2], 0.0), deps=[tprev])
                    for which, XT, mxi in (("q", QTt, 0), ("k", KTt, 1)):
                        for tc in range(8):
                            tsl = slice(tc * 512, (tc + 1) * 512)
                            t2 = kb.step("scalar", (lambda e, XT=XT, tsl=tsl:
                                                    e.activation(SQs[:], XT[:, tsl], AF.Square,
                                                                 bias=f32(0.0), scale=f32(1.0))),
                                         deps=[t])
                            t2 = kb.step("tensor", lambda e: e.matmul(PSS[0:1, :], onescf[:],
                                                                      SQs[:], start=True,
                                                                      stop=True), deps=[t2])
                            t2 = kb.step("vector", lambda e: e.tensor_copy(
                                ROW[:], PSS[0:1, :]), deps=[t2])
                            t2 = kb.step("vector", lambda e: e.tensor_reduce(
                                MX[:, 2:3], ROW[:], AX.X, ALU.max), deps=[t2])
                            t = kb.step("vector", (lambda e, mxi=mxi:
                                e.tensor_max(MX[:, mxi:mxi + 1], MX[:, mxi:mxi + 1],
                                             MX[:, 2:3])), deps=[t2])
                    t = kb.step("vector", lambda e: e.tensor_tensor(MX[:, 2:3], MX[:, 0:1],
                                                                    MX[:, 1:2], ALU.mult),
                                deps=[t])
                    t = kb.step("scalar", lambda e: e.activation(MX[:, 3:4], MX[:, 2:3],
                                                                 AF.Sqrt, bias=f32(0.0),
                                                                 scale=f32(1.0)), deps=[t])
                    t = kb.step("vector", lambda e: e.tensor_scalar(MX[:, 3:4], MX[:, 3:4],
                                                                    float(-1.0), None, ALU.mult),
                                deps=[t])
                    t = kb.step("tensor", lambda e: e.matmul(PSO[:, 0:1], onesrf[:], MX[:, 3:4],
                                                             start=True, stop=True), deps=[t])
                    t = kb.step("vector", lambda e: e.tensor_copy(BCOL[:], PSO[:, 0:1]),
                                deps=[t])
                    hq = t
                    for qt in range(8):
                        qsl = slice(qt * 512, (qt + 1) * 512)
                        nk = 4 * qt + 4
                        edep = hq
                        for kt in range(nk):
                            mt = kb.step("tensor", (lambda e, kt=kt, qsl=qsl:
                                                    e.matmul(PSS[:], KTt[:, kt * 128:(kt + 1) * 128],
                                                             QTt[:, qsl], start=True, stop=True)),
                                         deps=[edep])
                            et = kb.step("scalar", lambda e: e.activation(
                                ES[:], PSS[:], AF.Exp, bias=BCOL[:], scale=f32(1.0)),
                                deps=[mt])
                            if kt >= 4 * qt:
                                off = kt - 4 * qt
                                et = kb.step("vector", (lambda e, off=off:
                                                        e.tensor_tensor(ES[:], ES[:],
                                                                        TRI[:, off * 512:(off + 1) * 512],
                                                                        ALU.mult)), deps=[et])
                            edep = kb.step("tensor", (lambda e, kt=kt, nk=nk:
                                                      e.matmul(PS65[:], VA[:, kt * 65:kt * 65 + 65],
                                                               ES[:], start=(kt == 0),
                                                               stop=(kt == nk - 1))), deps=[et])
                        t = kb.step("vector", lambda e: [
                            e.tensor_copy(CTXs[:], PS65[0:64, :]),
                            e.tensor_copy(ZR[:], PS65[64:65, :])], deps=[edep])
                        t = kb.step("vector", lambda e: e.tensor_scalar(
                            ZR[:], ZR[:], 1e-30, None, ALU.add), deps=[t])
                        t = kb.step("vector", lambda e: e.reciprocal(ZR[:], ZR[:]), deps=[t])
                        pvt = t
                        for qs in range(4):
                            mt = kb.step("tensor", (lambda e, qs=qs:
                                                    e.matmul(PSO[:], CTXs[:, qs * 128:(qs + 1) * 128],
                                                             WO[:], start=True, stop=True)),
                                         deps=[t, pvt])
                            zt = kb.step("tensor", (lambda e, qs=qs:
                                                    e.matmul(PSS[0:128, 0:1],
                                                             ZR[0:1, qs * 128:(qs + 1) * 128],
                                                             ones11[:], start=True, stop=True)),
                                         deps=[t, pvt])
                            vt = kb.step("vector", lambda e: e.tensor_copy(
                                ZC[:], PSS[0:128, 0:1]), deps=[mt, zt, hq])
                            vt = kb.step("vector", lambda e: e.tensor_scalar(
                                OH[:], PSO[:], ZC[:], None, ALU.mult), deps=[vt])
                            pvt = vt
                            hq = kb.step("sync", (lambda e, qt=qt, qs=qs, h=h:
                                                  kb.dma(e, g["A"][(qt * 4 + qs) * 128:(qt * 4 + qs + 1) * 128,
                                                                   h * 64:(h + 1) * 64],
                                                         OH[:])), deps=[vt], n_dma=1)
                    hprev = hq

    def l1_attn_phase():
        PH[0] += 1
        SEG = 1024
        with ExitStack() as pc:
            RB = pc.enter_context(S("lb_rb", [128, D], F32))
            XN = pc.enter_context(S("lb_xn", [128, D], F32))
            CL = pc.enter_context(S("lb_cl", [128, 8], F32))
            LNG = pc.enter_context(S("lb_lng", [128, D], F32))
            LNB = pc.enter_context(S("lb_lnb", [128, D], F32))
            BQC = pc.enter_context(S("lb_bqc", [128, 24], F32))
            BGC = pc.enter_context(S("lb_bgc", [128, 8], F32))
            BPR = pc.enter_context(S("lb_bpr", [128, D], F32))
            XNT = pc.enter_context(S("lb_xnt", [128, 4096], BF16))
            XRT = pc.enter_context(S("lb_xrt", [128, 4096], BF16))
            WQKV = [pc.enter_context(S(f"lb_wq{i}", [128, 3 * D], BF16)) for i in range(8)]
            WG = [pc.enter_context(S(f"lb_wg{i}", [128, D], BF16)) for i in range(8)]
            WP = [pc.enter_context(S(f"lb_wp{i}", [128, D], BF16)) for i in range(8)]
            WF = pc.enter_context(S("lb_wf", [128, 3 * D], F32))
            QKV = [pc.enter_context(S(f"lb_qkv{i}", [128, 512], BF16)) for i in range(24)]
            GT = [pc.enter_context(S(f"lb_gt{i}", [128, 512], BF16)) for i in range(8)]
            GR = pc.enter_context(S("lb_gr", [1, 512], F32))
            EW = pc.enter_context(S("lb_ew", [128, 512], F32))
            EW2 = pc.enter_context(S("lb_ew2", [128, 512], F32))
            OT = [pc.enter_context(S(f"lb_ot{i}", [128, SEG], BF16)) for i in range(8)]
            KC = pc.enter_context(S("lb_kc", [128, 64], BF16))
            VC = pc.enter_context(S("lb_vc", [128, 65], BF16))
            SM = pc.enter_context(S("lb_sm", [128, 128], BF16))
            NTs = pc.enter_context(S("lb_nt", [64, 128], BF16))
            DR = pc.enter_context(S("lb_dr", [1, 128], F32))
            KVS = pc.enter_context(S("lb_kvs", [64, 1536], BF16))
            LTRI = pc.enter_context(S("lb_ltri", [128, 128], BF16))
            AB = pc.enter_context(S("lb_abf", [128, D], F32))
            XB = pc.enter_context(S("lb_xb", [128, D], BF16))
            QH0 = pc.enter_context(S("lb_qh0", [64, 128], BF16))
            PSA = pc.enter_context(PP("lb_psa", [128, 512]))
            PSR = pc.enter_context(PP("lb_psr", [1, 512]))
            PSG = pc.enter_context(PP("lb_psg", [128, 512]))
            PSN = pc.enter_context(PP("lb_psn", [65, 128]))
            PSK = pc.enter_context(PP("lb_psk", [64, 1536]))
            PST = pc.enter_context(PP("lb_pst", [128, 128], BF16))
            with nc.Block() as blk:
                kb.blk = blk
                phase_entry()
                ldt = None
                for i in range(8):
                    lt = kb.step("sync", (lambda e, i=i:
                                          kb.dma(e, WF[:, 0:3 * D],
                                                 g["w_qkv"][i * 128:(i + 1) * 128, :])),
                                 deps=[ldt], n_dma=1)
                    ldt = kb.step("vector", (lambda e, i=i:
                                             e.tensor_copy(WQKV[i][:], WF[:, 0:3 * D])),
                                  deps=[lt])
                for i in range(8):
                    lt = kb.step("sync", (lambda e, i=i:
                                          kb.dma(e, WF[:, 0:D],
                                                 g["w_gate"][i * 128:(i + 1) * 128, :])),
                                 deps=[ldt], n_dma=1)
                    ldt = kb.step("vector", (lambda e, i=i:
                                             e.tensor_copy(WG[i][:], WF[:, 0:D])), deps=[lt])
                for i in range(8):
                    lt = kb.step("sync", (lambda e, i=i:
                                          kb.dma(e, WF[:, 0:D],
                                                 g["w_proj"][i * 128:(i + 1) * 128, :])),
                                 deps=[ldt], n_dma=1)
                    ldt = kb.step("vector", (lambda e, i=i:
                                             e.tensor_copy(WP[i][:], WF[:, 0:D])), deps=[lt])
                ldt = kb.step("sync", lambda e: [
                    kb.dma(e, LNG[:], g["l1_lng_t"][:]), kb.dma(e, LNB[:], g["l1_lnb_t"][:]),
                    kb.dma(e, BQC[:], g["bqkv_cols"][:]), kb.dma(e, BGC[:], g["bgate_cols"][:]),
                    kb.dma(e, BPR[:], g["b_proj_t"][:]), kb.dma(e, LTRI[:], g["ltri_sm"][:])],
                    deps=[ldt], n_dma=6)
                t = kb.step("vector", lambda e: [
                    e.memset(KVS[:], 0.0),
                    e.memset(VC[:, 64:65], 1.0)], deps=[ldt])
                sprev = t
                stq = None
                for seg in range(4):
                    for half in range(2):
                        base_tok = seg * SEG + half * 512
                        t = sprev
                        for ti in range(4):
                            tok = base_tok + ti * 128
                            ldt = kb.step("sync", (lambda e, tok=tok:
                                                   kb.dma(e, RB[:], g["R"][tok:tok + 128, :])),
                                          deps=[t], n_dma=1)
                            t = emit_ln([ldt], CL, RB[:], XN[:], XN[:], D)
                            t = kb.step("vector", lambda e: e.tensor_mul(XN[:], XN[:], LNG[:]),
                                        deps=[t])
                            t = kb.step("vector", lambda e: e.tensor_add(XN[:], XN[:], LNB[:]),
                                        deps=[t])
                            t = kb.step("vector", lambda e: e.tensor_copy(XB[:], XN[:]),
                                        deps=[t])
                            for k in range(8):
                                tk = kb.step("tensor", (lambda e, k=k:
                                                        e.transpose(PST[:],
                                                                    XB[:, k * 128:(k + 1) * 128],
                                                                    ident[:])), deps=[t])
                                t = kb.step("vector", (lambda e, k=k, ti=ti:
                                                       e.tensor_copy(
                                                           XNT[:, k * 512 + ti * 128:k * 512 + (ti + 1) * 128],
                                                           PST[:])), deps=[tk])
                            t = kb.step("vector", lambda e: e.tensor_copy(XB[:], RB[:]),
                                        deps=[t])
                            for k in range(8):
                                tk = kb.step("tensor", (lambda e, k=k:
                                                        e.transpose(PST[:],
                                                                    XB[:, k * 128:(k + 1) * 128],
                                                                    ident[:])), deps=[t])
                                t = kb.step("vector", (lambda e, k=k, ti=ti:
                                                       e.tensor_copy(
                                                           XRT[:, k * 512 + ti * 128:k * 512 + (ti + 1) * 128],
                                                           PST[:])), deps=[tk])
                        gdone = []
                        for m in range(8):
                            def mmg(e, m=m):
                                for k in range(8):
                                    e.matmul(PSA[:], WG[k][:, m * 128:(m + 1) * 128],
                                             XRT[:, k * 512:(k + 1) * 512],
                                             start=(k == 0), stop=(k == 7))
                            mt = kb.step("tensor", mmg, deps=[t] + ([gdone[-1]] if gdone else []))
                            gt = kb.step("scalar", (lambda e, m=m:
                                                    e.activation(GT[m][:], PSA[:], AF.Sigmoid,
                                                                 bias=BGC[:, m:m + 1],
                                                                 scale=f32(1.0))), deps=[mt])
                            gdone.append(gt)
                        def mmsum(e):
                            for m in range(8):
                                e.matmul(PSR[:], onescb[:], GT[m][:],
                                         start=(m == 0), stop=(m == 7))
                        t = kb.step("tensor", mmsum, deps=gdone)
                        t = kb.step("vector", lambda e: e.tensor_scalar(
                            GR[:], PSR[:], float(1.0 / D), 1e-05, ALU.mult, ALU.add), deps=[t])
                        t = kb.step("vector", lambda e: e.reciprocal(GR[:], GR[:]), deps=[t])
                        t = kb.step("tensor", lambda e: e.matmul(PSG[:], onesrf[:], GR[:],
                                                                 start=True, stop=True), deps=[t])
                        gd = t
                        qdone = []
                        for m in range(24):
                            def mmq(e, m=m):
                                for k in range(8):
                                    e.matmul(PSA[:], WQKV[k][:, m * 128:(m + 1) * 128],
                                             XNT[:, k * 512:(k + 1) * 512],
                                             start=(k == 0), stop=(k == 7))
                            mt = kb.step("tensor", mmq, deps=[gd] + ([qdone[-1]] if qdone else []))
                            qt_ = kb.step("scalar", (lambda e, m=m:
                                                     e.activation(QKV[m][:], PSA[:], AF.Identity,
                                                                  bias=BQC[:, m:m + 1],
                                                                  scale=f32(1.0))), deps=[mt])
                            qdone.append(qt_)
                        gq = qdone[-1]
                        for m in range(16):
                            gq = kb.step("vector", (lambda e, m=m:
                                                    e.tensor_tensor(EW[:], QKV[m][:], PSG[:],
                                                                    ALU.mult)), deps=[gq, qdone[m]])
                            gq = kb.step("vector", lambda e: e.tensor_scalar_min(
                                EW2[:], EW[:], float(0.0)), deps=[gq])
                            gq = kb.step("scalar", lambda e: e.activation(
                                EW2[:], EW2[:], AF.Exp, bias=f32(0.0), scale=f32(1.0)),
                                deps=[gq])
                            gq = kb.step("vector", lambda e: e.tensor_scalar_max(
                                EW[:], EW[:], float(0.0)), deps=[gq])
                            gq = kb.step("vector", (lambda e, m=m:
                                                    e.tensor_add(QKV[m][:], EW2[:], EW[:])),
                                         deps=[gq])
                        sc = gq
                        for h in range(16):
                            mq = h // 2
                            r0 = (h % 2) * 64
                            hbase = (h // 7) * 512 + (h % 7) * 65
                            hs = slice(hbase, hbase + 65)
                            for c in range(4):
                                gc = seg * 8 + half * 4 + c
                                csl = slice(c * 128, (c + 1) * 128)
                                qh = QKV[mq]
                                kh = QKV[8 + mq]
                                vh = QKV[16 + mq]
                                tk = kb.step("tensor", (lambda e, kh=kh, r0=r0, csl=csl:
                                                        e.transpose(PST[0:128, 0:64],
                                                                    kh[r0:r0 + 64, csl],
                                                                    ident[r0:r0 + 64, r0:r0 + 64])),
                                             deps=[sc])
                                sc = kb.step("vector", lambda e: e.tensor_copy(
                                    KC[:], PST[0:128, 0:64]), deps=[tk])
                                tk = kb.step("tensor", (lambda e, vh=vh, r0=r0, csl=csl:
                                                        e.transpose(PST[0:128, 0:64],
                                                                    vh[r0:r0 + 64, csl],
                                                                    ident[r0:r0 + 64, r0:r0 + 64])),
                                             deps=[sc])
                                sc = kb.step("vector", lambda e: e.tensor_copy(
                                    VC[:, 0:64], PST[0:128, 0:64]), deps=[tk])
                                mt = kb.step("tensor", (lambda e, kh=kh, qh=qh, r0=r0, csl=csl:
                                                        e.matmul(PSA[:, 0:128],
                                                                 kh[r0:r0 + 64, csl],
                                                                 qh[r0:r0 + 64, csl],
                                                                 start=True, stop=True)),
                                             deps=[sc])
                                sc = kb.step("vector", lambda e: e.tensor_tensor(
                                    SM[:], PSA[:, 0:128], LTRI[:], ALU.mult), deps=[mt])
                                sc = kb.step("vector", (lambda e, qh=qh, r0=r0, csl=csl:
                                                        e.tensor_copy(QH0[:], qh[r0:r0 + 64, csl])),
                                             deps=[sc])
                                mt = kb.step("tensor", (lambda e, hs=hs:
                                                        [e.matmul(PSN[:], VC[:], SM[:],
                                                                  start=True, stop=False),
                                                         e.matmul(PSN[:], KVS[:, hs],
                                                                  QH0[:],
                                                                  start=False, stop=True)]),
                                             deps=[sc])
                                st_ = kb.step("tensor", (lambda e, hs=hs, gc=gc:
                                                         e.matmul(PSK[:, hs], KC[:], VC[:],
                                                                  start=(gc == 0),
                                                                  stop=(gc == 31),
                                                                  skip_group_check=True)),
                                              deps=[mt])
                                sc = kb.step("vector", (lambda e, hs=hs:
                                                        e.tensor_copy(KVS[:, hs], PSK[:, hs])),
                                             deps=[st_])
                                sc2 = kb.step("vector", lambda e: e.tensor_copy(
                                    NTs[:], PSN[0:64, :]), deps=[mt, sc])
                                sc2 = kb.step("vector", lambda e: e.tensor_scalar(
                                    DR[:], PSN[64:65, :], 1e-05, None, ALU.add), deps=[sc2])
                                sc2 = kb.step("vector", lambda e: e.reciprocal(DR[:], DR[:]),
                                              deps=[sc2])
                                bt = kb.step("tensor", lambda e: e.matmul(
                                    PSG[0:64, 0:128], onesrf[0:1, 0:64], DR[:],
                                    start=True, stop=True), deps=[sc2])
                                sc = kb.step("vector", (lambda e, mq=mq, r0=r0, half=half, csl=csl:
                                                        e.tensor_tensor(
                                                            OT[mq][r0:r0 + 64,
                                                                   half * 512 + csl.start:half * 512 + csl.stop],
                                                            NTs[:], PSG[0:64, 0:128],
                                                            ALU.mult)), deps=[bt, sc])
                        sprev = sc
                    pdone = sprev
                    for ti8 in range(8):
                        edone = []
                        for n in range(2):
                            def mmp(e, ti8=ti8, n=n):
                                for k in range(8):
                                    e.matmul(PSA[:], OT[k][:, ti8 * 128:(ti8 + 1) * 128],
                                             WP[k][:, n * 512:(n + 1) * 512],
                                             start=(k == 0), stop=(k == 7))
                            mt = kb.step("tensor", mmp, deps=[pdone])
                            et = kb.step("vector", (lambda e, n=n: e.tensor_add(
                                AB[:, n * 512:(n + 1) * 512], PSA[:],
                                BPR[:, n * 512:(n + 1) * 512])), deps=[mt])
                            edone.append(et)
                            pdone = et
                        stq = kb.step("sync", (lambda e, seg=seg, ti8=ti8:
                                               kb.dma(e, g["A"][seg * SEG + ti8 * 128:
                                                                seg * SEG + (ti8 + 1) * 128, :],
                                                      AB[:])), deps=edone + [stq], n_dma=1)
                        pdone = stq
                    sprev = stq


    seq = []
    seq.append(("ffn_a", dict(kind="l0", first=True, fold=False)))
    seq.append(("ffn_b", dict(kind="l0", first=True, fold=False, out_final=False)))
    if UPTO >= 2:
        seq.append(("ut", {}))
        seq.append(("l0at", {}))
    if UPTO >= 3:
        seq.append(("ffn_a", dict(kind="l0", first=False, fold=True)))
        seq.append(("ffn_b", dict(kind="l0", first=False, fold=True, out_final=False)))
    if UPTO >= 4:
        seq.append(("ffn_a", dict(kind="l1", first=False, fold=False)))
        seq.append(("ffn_b", dict(kind="l1", first=False, fold=False, out_final=False)))
    if UPTO >= 5:
        seq.append(("l1at", {}))
    if UPTO >= 6:
        seq.append(("ffn_a", dict(kind="l1", first=False, fold=True)))
        seq.append(("ffn_b", dict(kind="l1", first=False, fold=True, out_final=True)))
    final_is_R = UPTO < 6

    for name, kw in seq:
        if name == "ffn_a":
            ffn_a(**kw)
        elif name == "ffn_b":
            ffn_b(**kw)
        elif name == "ut":
            ut_phase()
        elif name == "l0at":
            l0_attn_phase()
        elif name == "l1at":
            l1_attn_phase()

    if final_is_R:
        with ExitStack() as pc:
            RB = pc.enter_context(S("cp_rb", [128, D], F32))
            AB = pc.enter_context(S("cp_ab", [128, D], F32))
            with nc.Block() as blk:
                kb.blk = blk
                phase_entry()
                prev = None
                fold_dbg = (UPTO in (2, 5))
                for i in range(NT):
                    nd = 2 if fold_dbg else 1
                    def ld(e, i=i):
                        kb.dma(e, RB[:], g["R"][i * 128:(i + 1) * 128, :])
                        if fold_dbg:
                            kb.dma(e, AB[:], g["A"][i * 128:(i + 1) * 128, :])
                    ldt = kb.step("sync", ld, deps=[prev], n_dma=nd)
                    dep = ldt
                    if fold_dbg:
                        dep = kb.step("vector", lambda e: e.tensor_add(RB[:], RB[:], AB[:]),
                                      deps=[ldt])
                    prev = kb.step("sync", (lambda e, i=i:
                                            kb.dma(e, g["y_out"][i * 128:(i + 1) * 128, :],
                                                   RB[:])), deps=[dep], n_dma=1)
        with nc.Block() as blk:
            kb.blk = blk
            phase_entry()
            kb.step("sync", lambda e: None, deps=[("sync", kb.tick["sync"])])

    return nc, g, ctx



_BUILD_CACHE = {}

def _tile_rows(v, n=128):
    return np.broadcast_to(np.asarray(v, np.float32).reshape(1, -1), (n, len(np.ravel(v)))).copy()

def kernel(**inputs):
    I = {k: np.ascontiguousarray(np.asarray(v, np.float32)) for k, v in inputs.items()}
    keep = dropout_keep_mask()
    maskscale = np.where(keep, np.float32(1.25), np.float32(0.0)).astype(np.float32)

    nc, g, ctx = _BUILD_CACHE.get("nc", (None, None, None))
    if nc is None:
        nc, g, ctx = build_program()
        _BUILD_CACHE["nc"] = (nc, g, ctx)

    ident = np.eye(128, dtype=ml_dtypes.bfloat16)
    tri = np.zeros((128, 4 * 512), dtype=ml_dtypes.bfloat16)
    for off in range(4):
        k_idx = np.arange(128)[:, None] + off * 128
        q_idx = np.arange(512)[None, :]
        tri[:, off * 512:(off + 1) * 512] = (k_idx <= q_idx).astype(ml_dtypes.bfloat16)
    ltri = (np.arange(128)[:, None] <= np.arange(128)[None, :]).astype(ml_dtypes.bfloat16)

    common = {
        "l0_w1t": I["L0_ffn_w1"].T.copy(), "l0_w2t": I["L0_ffn_w2"].T.copy(),
        "l1_w1t": I["L1_ffn_w1"].T.copy(), "l1_w2t": I["L1_ffn_w2"].T.copy(),
        "l0_ln1w_t": _tile_rows(I["L0_ffn_ln1_w"]), "l0_ln1b_t": _tile_rows(I["L0_ffn_ln1_b"]),
        "l0_ln2w_t": _tile_rows(I["L0_ffn_ln2_w"]), "l0_ln2b_t": _tile_rows(I["L0_ffn_ln2_b"]),
        "l0_b1_t": _tile_rows(I["L0_ffn_b1"]), "l0_b2_t": _tile_rows(I["L0_ffn_b2"]),
        "l1_b1_t": _tile_rows(I["L1_ffn_b1"]), "l1_b2_t": _tile_rows(I["L1_ffn_b2"]),
        "w_dq_t": I["L0_attn_W_DQ"].T.copy(), "w_uq_t": I["L0_attn_W_UQ"].T.copy(),
        "w_dkv_t": I["L0_attn_W_DKV"].T.copy(), "w_ukv_t": I["L0_attn_W_UKV"].T.copy(),
        "w_qr_t": I["L0_attn_W_QR"].T.copy(), "w_kr_t": I["L0_attn_W_KR"].T.copy(),
        "w_o_t": I["L0_attn_W_O"].T.copy(),
        "qnw_c": I["L0_attn_q_norm_w"].reshape(64, 1).copy(),
        "knw_c": I["L0_attn_k_norm_w"].reshape(64, 1).copy(),
        "w_qkv": I["L1_attn_w_qkv"].copy(),
        "bqkv_cols": I["L1_attn_b_qkv"].reshape(24, 128).T.copy(),
        "w_gate": I["L1_attn_w_gate"].copy(),
        "bgate_cols": I["L1_attn_b_gate"].reshape(8, 128).T.copy(),
        "w_proj": I["L1_attn_w_proj"].copy(),
        "b_proj_t": _tile_rows(I["L1_attn_b_proj"]),
        "l1_lng_t": _tile_rows(I["L1_attn_ln_g"]), "l1_lnb_t": _tile_rows(I["L1_attn_ln_b"]),
        "ident": ident, "ones_col_f32": np.ones((128, 1), np.float32),
        "ones_row_f32": np.ones((1, 128), np.float32),
        "ones11": np.ones((1, 1), np.float32),
        "tri_masks": tri, "ltri_sm": ltri,
    }
    in_maps = []
    for c in range(8):
        b = c % 4
        m = dict(common)
        m["x_in"] = I["L0_attn_x"][b]
        m["mask_in"] = maskscale[b]
        in_maps.append(m)
    res = run_bass_kernel_spmd(nc, in_maps, core_ids=list(range(8)))
    out = np.stack([res.results[c]["y_out"] for c in range(4)], axis=0)
    return out.astype(np.float32)


# version 03
# speedup vs baseline: 1.1831x; 1.0772x over previous
import os
import numpy as np
import ml_dtypes
import concourse.bass as bass
import concourse.mybir as mybir
from concourse.bass_utils import run_bass_kernel_spmd
from contextlib import ExitStack

F32 = mybir.dt.float32
BF16 = mybir.dt.bfloat16
F16 = mybir.dt.float16
F8 = mybir.dt.float8e4
I8 = mybir.dt.int8
AF = mybir.ActivationFunctionType
ALU = mybir.AluOpType
AX = mybir.AxisListType

B, T, D = 4, 4096, 1024
HID = 4096
NT = T // 128
f32 = np.float32
UPTO = int(os.environ.get("KMK_UPTO", "99"))
RMS_EPS = np.float32(1.1920929e-07)


def _rotl(x, d):
    return ((x << np.uint32(d)) | (x >> np.uint32(32 - d))).astype(np.uint32)

def _tf2x32(k1, k2, x0, x1):
    ks0 = np.uint32(k1); ks1 = np.uint32(k2)
    ks2 = np.uint32(ks0 ^ ks1 ^ np.uint32(0x1BD11BDA))
    x0 = (x0 + ks0).astype(np.uint32); x1 = (x1 + ks1).astype(np.uint32)
    rot = [[13, 15, 26, 6], [17, 29, 16, 24]]
    ks = [ks0, ks1, ks2]
    sched = [(rot[0], ks[1], ks[2], 1), (rot[1], ks[2], ks[0], 2),
             (rot[0], ks[0], ks[1], 3), (rot[1], ks[1], ks[2], 4),
             (rot[0], ks[2], ks[0], 5)]
    for rs, ka, kb_, i in sched:
        for r in rs:
            x0 = (x0 + x1).astype(np.uint32); x1 = _rotl(x1, r)
            x1 = (x1 ^ x0).astype(np.uint32)
        x0 = (x0 + ka).astype(np.uint32)
        x1 = (x1 + kb_ + np.uint32(i)).astype(np.uint32)
    return x0, x1

def dropout_keep_mask():
    n = B * T * D
    x64_mode = os.environ.get("BASS_SIM_FP64") == "1"
    out = np.empty(n, dtype=bool)
    step = 1 << 22
    for s in range(0, n, step):
        i = np.arange(s, min(n, s + step), dtype=np.uint64)
        hi = (i >> np.uint64(32)).astype(np.uint32)
        lo = (i & np.uint64(0xFFFFFFFF)).astype(np.uint32)
        o0, o1 = _tf2x32(0, 42, hi, lo)
        if not x64_mode:
            bits = (o0 ^ o1).astype(np.uint32)
            fb = ((bits >> np.uint32(9)) | np.uint32(0x3F800000)).astype(np.uint32)
            u = fb.view(np.float32) - np.float32(1.0)
            out[s:s + len(i)] = u < np.float32(0.8)
        else:
            bits = (o0.astype(np.uint64) << np.uint64(32)) | o1.astype(np.uint64)
            fb = (bits >> np.uint64(12)) | np.uint64(0x3FF0000000000000)
            u = fb.view(np.float64) - 1.0
            out[s:s + len(i)] = u < 0.8
    return out.reshape(B, T, D)


class KB:
    ENGS = ("sync", "vector", "scalar", "tensor", "gpsimd")

    def __init__(self, nc, ctx):
        self.nc = nc
        self.blk = None
        self.sem = {e: ctx.enter_context(nc.semaphore("S" + e)) for e in self.ENGS}
        self.tick = {e: 0 for e in self.ENGS}

    def step(self, eng, emit, deps=(), n_dma=0):
        sem = self.sem
        deps = [d for d in deps if d is not None]
        me = sem[eng]
        self.tick[eng] += 16 * n_dma if n_dma else 1
        ticket = self.tick[eng]

        def closure(e):
            for (s, v) in deps:
                e.wait_ge(sem[s], v)
            r = emit(e)
            if not n_dma:
                if isinstance(r, list):
                    r = r[-1] if r else None
                if r is not None and hasattr(r, "then_inc"):
                    r.then_inc(me, 1)
                else:
                    e.drain().then_inc(me, 1)
        getattr(self.blk, eng)(closure)
        return (eng, ticket)

    def dma(self, e, out, in_):
        e.dma_start(out=out, in_=in_).then_inc(self.sem["sync"], 16)


def build_program():
    nc = bass.Bass("TRN2", target_bir_lowering=False)
    g = {}

    def inp(name, shape, dt=F32):
        g[name] = nc.dram_tensor(name, shape, dt, kind="ExternalInput").ap()

    inp("x_in", [T, D]); inp("mask_in", [T, D])
    inp("l0_w1t", [D, HID]); inp("l0_w2t", [HID, D])
    inp("l1_w1t", [D, HID]); inp("l1_w2t", [HID, D])
    inp("l0_ln1w_t", [128, D]); inp("l0_ln1b_t", [128, D])
    inp("l0_ln2w_t", [128, HID]); inp("l0_ln2b_t", [128, HID])
    inp("l0_b1_t", [128, HID]); inp("l0_b2_t", [128, D])
    inp("l1_b1_t", [128, HID]); inp("l1_b2_t", [128, D])
    inp("w_dq_t", [64, 512]); inp("w_uq_t", [512, 64])
    inp("w_dkv_t", [64, 1024]); inp("w_ukv_t", [1024, 128])
    inp("w_qr_t", [64, 64]); inp("w_kr_t", [64, 64]); inp("w_o_t", [64, 64])
    inp("qnw_c", [64, 1]); inp("knw_c", [64, 1])
    inp("w_qkv", [D, 3 * D]); inp("bqkv_cols", [128, 24])
    inp("w_gate", [D, D]); inp("bgate_cols", [128, 8])
    inp("w_proj", [D, D]); inp("b_proj_t", [128, D])
    inp("l1_lng_t", [128, D]); inp("l1_lnb_t", [128, D])
    inp("ident", [128, 128], BF16)
    inp("ones_col_f32", [128, 1])
    inp("ones_row_f32", [1, 128])
    inp("ones11", [1, 1])
    inp("tri_masks", [128, 4 * 512], BF16)
    inp("ltri_sm", [128, 128], BF16)

    g["y_out"] = nc.dram_tensor("y_out", [T, D], F32, kind="ExternalOutput").ap()
    for n, shape, dt in (("R", [T, D], F32), ("A", [T, D], F32), ("SC2", [T, 1], F32),
                         ("W1Q0", [D, HID], BF16), ("W2Q0", [HID, D], BF16),
                         ("W1Q1", [D, HID], BF16), ("W2Q1", [HID, D], BF16),
                         ("HQ", [T, HID], BF16), ("UT", [D, T], BF16)):
        g[n] = nc.dram_tensor(n, shape, dt, kind="Internal").ap()

    ctx = ExitStack()
    kb = KB(nc, ctx)
    ident = ctx.enter_context(nc.sbuf_tensor("identt", [128, 128], BF16))
    onescf = ctx.enter_context(nc.sbuf_tensor("onescf", [128, 1], F32))
    onesrf = ctx.enter_context(nc.sbuf_tensor("onesrf", [1, 128], F32))
    ones11 = ctx.enter_context(nc.sbuf_tensor("ones11s", [1, 1], F32))
    onescb = ctx.enter_context(nc.sbuf_tensor("onescbs", [128, 1], BF16))
    scal = ctx.enter_context(nc.sbuf_tensor("scal", [1, 8], F32))
    scol = ctx.enter_context(nc.sbuf_tensor("scol", [128, 4], F32))

    PH = [0]
    def S(name, shape, dt):
        return nc.sbuf_tensor(f"{name}_p{PH[0]}", shape, dt)
    def PP(name, shape, dt=F32):
        return nc.psum_tensor(f"{name}_p{PH[0]}", shape, dt)

    with nc.Block() as blk:
        kb.blk = blk
        kb.step("sync", lambda e: [kb.dma(e, ident[:], g["ident"][:]),
                                   kb.dma(e, onescf[:], g["ones_col_f32"][:]),
                                   kb.dma(e, onesrf[:], g["ones_row_f32"][:]),
                                   kb.dma(e, ones11[:], g["ones11"][:])], n_dma=4)
        kb.step("vector", lambda e: e.tensor_copy(onescb[:], onescf[:]),
                deps=[("sync", kb.tick["sync"])])

    with ExitStack() as pc:
        wt = [pc.enter_context(S(f"wqt{i}", [128, HID], F32)) for i in range(2)]
        acc = pc.enter_context(S("wqacc", [128, 2], F32))
        w8 = pc.enter_context(S("wqw8", [128, HID], BF16))
        wi8 = pc.enter_context(S("wqi8", [128, HID], I8))
        wf8 = pc.enter_context(S("wqf8", [128, HID], F8))
        psr = pc.enter_context(PP("wqpsr", [1, 1], F32))
        psc = pc.enter_context(PP("wqpsc", [128, 1], F32))
        with nc.Block() as blk:
            kb.blk = blk
            ldconst = kb.tick["sync"]
            for wi, (src, dst, ntile, wid) in enumerate(
                    (("l0_w1t", "W1Q0", 8, HID), ("l0_w2t", "W2Q0", 32, D))):
                red = None
                for i in range(ntile):
                    buf = wt[i % 2]
                    ldt = kb.step("sync", (lambda e, buf=buf, i=i, src=src, wid=wid:
                                           kb.dma(e, buf[:, 0:wid],
                                                  g[src][i * 128:(i + 1) * 128, :])),
                                  deps=[red], n_dma=1)
                    if i == 0:
                        red = kb.step("vector", (lambda e, buf=buf, wid=wid:
                                                 e.tensor_reduce(acc[:, 0:1], buf[:, 0:wid],
                                                                 AX.X, ALU.add,
                                                                 apply_absolute_value=True)),
                                      deps=[ldt])
                    else:
                        red = kb.step("vector", (lambda e, buf=buf, wid=wid:
                                                 e.tensor_reduce(acc[:, 1:2], buf[:, 0:wid],
                                                                 AX.X, ALU.add,
                                                                 apply_absolute_value=True)),
                                      deps=[ldt])
                        red = kb.step("vector", lambda e: e.tensor_add(acc[:, 0:1],
                                                                       acc[:, 0:1],
                                                                       acc[:, 1:2]),
                                      deps=[red])
                t = kb.step("tensor", lambda e: e.matmul(psr[:], acc[:, 0:1], onescf[:],
                                                         start=True, stop=True), deps=[red])
                t = kb.step("vector", (lambda e, ntile=ntile, wid=wid:
                                       e.tensor_scalar(scal[0:1, 0:1], psr[:],
                                                       float(1.0 / (wid * ntile * 128)),
                                                       None, ALU.mult)), deps=[t])
                t = kb.step("vector", lambda e: e.tensor_scalar_max(
                    scal[0:1, 0:1], scal[0:1, 0:1], float(1e-5)), deps=[t])
                t = kb.step("vector", lambda e: e.reciprocal(scal[0:1, 1:2],
                                                             scal[0:1, 0:1]), deps=[t])
                t = kb.step("tensor", lambda e: e.matmul(psc[:], onesrf[:], scal[0:1, 1:2],
                                                         start=True, stop=True), deps=[t])
                t = kb.step("vector", (lambda e, wi=wi:
                                       e.tensor_copy(scol[:, wi:wi + 1], psc[:])), deps=[t])
                t = kb.step("tensor", lambda e: e.matmul(psc[:], onesrf[:], scal[0:1, 0:1],
                                                         start=True, stop=True), deps=[t])
                t = kb.step("vector", (lambda e, wi=wi:
                                       e.tensor_copy(scol[:, wi + 2:wi + 3], psc[:])), deps=[t])
                qd = t
                for i in range(ntile):
                    buf = wt[i % 2]
                    ldt = kb.step("sync", (lambda e, buf=buf, i=i, src=src, wid=wid:
                                           kb.dma(e, buf[:, 0:wid],
                                                  g[src][i * 128:(i + 1) * 128, :])),
                                  deps=[qd], n_dma=1)
                    qd = kb.step("vector", (lambda e, buf=buf, wid=wid, wi=wi:
                        e.tensor_scalar(buf[:, 0:wid], buf[:, 0:wid], scol[:, wi:wi + 1],
                                        None, ALU.mult)), deps=[ldt])
                    qd = kb.step("vector", (lambda e, buf=buf, wid=wid:
                        e.tensor_scalar(buf[:, 0:wid], buf[:, 0:wid], 12582912.0, None,
                                        ALU.add)), deps=[qd])
                    qd = kb.step("vector", (lambda e, buf=buf, wid=wid:
                        e.tensor_scalar(buf[:, 0:wid], buf[:, 0:wid], -12582912.0, None,
                                        ALU.add)), deps=[qd])
                    qd = kb.step("vector", (lambda e, buf=buf, wid=wid:
                        e.tensor_scalar(w8[:, 0:wid], buf[:, 0:wid], float(1.0), float(-1.0),
                                        ALU.min, ALU.max)), deps=[qd])
                    kb.step("sync", (lambda e, i=i, dst=dst, wid=wid:
                                     kb.dma(e, g[dst][i * 128:(i + 1) * 128, :],
                                            w8[:, 0:wid])), deps=[qd], n_dma=1)
            for src, dst, ntile, wid in (("l1_w1t", "W1Q1", 8, HID),
                                         ("l1_w2t", "W2Q1", 32, D)):
                qd = None
                for i in range(ntile):
                    buf = wt[i % 2]
                    ldt = kb.step("sync", (lambda e, buf=buf, i=i, src=src, wid=wid:
                                           kb.dma(e, buf[:, 0:wid],
                                                  g[src][i * 128:(i + 1) * 128, :])),
                                  deps=[qd], n_dma=1)
                    qd = kb.step("vector", (lambda e, buf=buf, wid=wid:
                        e.tensor_copy(wf8[:, 0:wid], buf[:, 0:wid])), deps=[ldt])
                    qd = kb.step("vector", (lambda e, buf=buf, wid=wid:
                        e.tensor_copy(w8[:, 0:wid], wf8[:, 0:wid])), deps=[qd])
                    kb.step("sync", (lambda e, i=i, dst=dst, wid=wid:
                                     kb.dma(e, g[dst][i * 128:(i + 1) * 128, :],
                                            w8[:, 0:wid])), deps=[qd], n_dma=1)


    def phase_entry():
        deps_all = [(en, kb.tick[en]) for en in KB.ENGS]
        for en in KB.ENGS:
            kb.step(en, lambda e: None, deps=deps_all)

    def emit_ln(deps, CL, src, dst, scratch, d, eps=np.float32(1e-5)):
        t = kb.step("vector", lambda e: e.tensor_reduce(CL[:, 0:1], src, AX.X, ALU.add),
                    deps=deps)
        t = kb.step("vector", lambda e: e.scalar_tensor_tensor(scratch, src, float(1.0), src,
                                                               ALU.mult, ALU.mult,
                                                               accum_out=CL[:, 1:2]),
                    deps=[t])
        t = kb.step("vector", lambda e: [
            e.tensor_scalar(CL[:, 2:3], CL[:, 0:1], float(1.0 / d), None, ALU.mult),
            e.tensor_scalar(CL[:, 5:6], CL[:, 0:1], float(-1.0 / d), None, ALU.mult),
            e.tensor_scalar(CL[:, 4:5], CL[:, 1:2], float(1.0 / d), None, ALU.mult)], deps=[t])
        t = kb.step("vector", lambda e: e.tensor_tensor(CL[:, 3:4], CL[:, 2:3],
                                                        CL[:, 2:3], ALU.mult), deps=[t])
        t = kb.step("vector", lambda e: e.tensor_sub(CL[:, 4:5], CL[:, 4:5], CL[:, 3:4]),
                    deps=[t])
        t = kb.step("scalar", lambda e: e.activation(CL[:, 6:7], CL[:, 4:5], AF.Sqrt,
                                                     bias=eps, scale=f32(1.0)), deps=[t])
        t = kb.step("vector", lambda e: e.reciprocal(CL[:, 6:7], CL[:, 6:7]), deps=[t])
        t = kb.step("vector", lambda e: e.tensor_scalar(dst, src, CL[:, 5:6], CL[:, 6:7],
                                                        ALU.add, ALU.mult), deps=[t])
        return t

    def ffn_a(kind, first, fold):
        PH[0] += 1
        with ExitStack() as pc:
            RB = pc.enter_context(S("fa_rb", [128, D], F32))
            AB = pc.enter_context(S("fa_ab", [128, D], F32)) if fold else None
            U = pc.enter_context(S("fa_u", [128, D], F32))
            H = pc.enter_context(S("fa_h", [128, HID], F32))
            QB = pc.enter_context(S("fa_qb", [128, HID], BF16))
            QT = pc.enter_context(S("fa_qt", [128, D], BF16))
            TI8 = pc.enter_context(S("fa_i8", [128, HID], I8))
            FQ8 = pc.enter_context(S("fa_f8", [128, HID], F8))
            H16 = pc.enter_context(S("fa_h16", [128, HID], F16)) if kind == "l1" else None
            CL = pc.enter_context(S("fa_cl", [128, 8], F32))
            CF = pc.enter_context(S("fa_cf", [128, HID], F32))
            B1 = pc.enter_context(S("fa_b1", [128, HID], F32))
            W1 = [pc.enter_context(S(f"fa_w1{i}", [128, HID], BF16))
                  for i in range(8)]
            if kind == "l0":
                LN1W = pc.enter_context(S("fa_l1w", [128, D], F32))
                LN1B = pc.enter_context(S("fa_l1b", [128, D], F32))
                LN2W = pc.enter_context(S("fa_l2w", [128, HID], F32))
                LN2B = pc.enter_context(S("fa_l2b", [128, HID], F32))
            PSA = pc.enter_context(PP("fa_psa", [128, 512], F32))
            PSB = pc.enter_context(PP("fa_psb", [128, 512], F32))
            PST = pc.enter_context(PP("fa_pst", [128, 512], BF16))
            with nc.Block() as blk:
                kb.blk = blk
                phase_entry()
                w1src = g["W1Q0"] if kind == "l0" else g["W1Q1"]
                ldw = kb.step("sync", lambda e: [
                    *[kb.dma(e, W1[i][:], w1src[i * 128:(i + 1) * 128, :]) for i in range(8)],
                    kb.dma(e, B1[:], g["l0_b1_t" if kind == "l0" else "l1_b1_t"][:])],
                    n_dma=9)
                cdep = ldw
                if kind == "l0":
                    cdep = kb.step("sync", lambda e: [
                        kb.dma(e, LN1W[:], g["l0_ln1w_t"][:]),
                        kb.dma(e, LN1B[:], g["l0_ln1b_t"][:]),
                        kb.dma(e, LN2W[:], g["l0_ln2w_t"][:]),
                        kb.dma(e, LN2B[:], g["l0_ln2b_t"][:])], deps=[cdep], n_dma=4)
                else:
                    cdep = kb.step("vector", lambda e: e.tensor_copy(FQ8[:], B1[:]),
                                   deps=[cdep])
                    cdep = kb.step("vector", lambda e: e.tensor_copy(B1[:], FQ8[:]),
                                   deps=[cdep])

                prev = cdep
                for i in range(NT):
                    src = g["x_in"] if first else g["R"]
                    def ld_emit(e, i=i):
                        kb.dma(e, RB[:], src[i * 128:(i + 1) * 128, :])
                        if fold:
                            kb.dma(e, AB[:], g["A"][i * 128:(i + 1) * 128, :])
                    ldt = kb.step("sync", ld_emit, deps=[prev], n_dma=2 if fold else 1)
                    dep = ldt
                    if fold:
                        dep = kb.step("vector", lambda e: e.tensor_add(RB[:], RB[:], AB[:]),
                                      deps=[ldt])
                    t = emit_ln([dep], CL, RB[:], U[:], H[:, 0:D], D)
                    if kind == "l0":
                        t = emit_ln([t], CL, U[:], U[:], H[:, 0:D], D)
                        t = kb.step("vector", lambda e: e.tensor_mul(U[:], U[:], LN1W[:]),
                                    deps=[t])
                        t = kb.step("vector", lambda e: e.tensor_add(U[:], U[:], LN1B[:]),
                                    deps=[t])
                        t = kb.step("vector", lambda e: e.tensor_reduce(
                            CL[:, 0:1], U[:], AX.X, ALU.max, apply_absolute_value=True),
                                    deps=[t])
                        t = kb.step("vector", lambda e: e.tensor_scalar_max(
                            CL[:, 0:1], CL[:, 0:1], float(1e-5)), deps=[t])
                        t = kb.step("vector", lambda e: e.reciprocal(CL[:, 1:2], CL[:, 0:1]),
                                    deps=[t])
                        t = kb.step("vector", lambda e: e.tensor_scalar(
                            CL[:, 1:2], CL[:, 1:2], 127.0, None, ALU.mult), deps=[t])
                        t = kb.step("vector", lambda e: [
                            e.tensor_scalar(H[:, 0:D], U[:], CL[:, 1:2], None, ALU.mult),
                            e.tensor_scalar(CL[:, 7:8], CL[:, 0:1], scol[:, 2:3],
                                            float(1.0 / 127.0), ALU.mult, ALU.mult)], deps=[t])
                        t = kb.step("vector", lambda e: e.tensor_scalar(
                            H[:, 0:D], H[:, 0:D], 12582912.0, None, ALU.add), deps=[t])
                        t = kb.step("vector", lambda e: e.tensor_scalar(
                            QB[:, 0:D], H[:, 0:D], -12582912.0, None, ALU.add), deps=[t])
                    else:
                        t = kb.step("vector", lambda e: e.tensor_copy(FQ8[:, 0:D], U[:]),
                                    deps=[t])
                        t = kb.step("vector", lambda e: e.tensor_copy(QB[:, 0:D], FQ8[:, 0:D]),
                                    deps=[t])
                    ev = t
                    for kb4 in range(2):
                        def trb(e, kb4=kb4):
                            r = None
                            for j in range(4):
                                k = kb4 * 4 + j
                                r = e.transpose(PST[:, j * 128:(j + 1) * 128],
                                                QB[:, k * 128:(k + 1) * 128], ident[:])
                            return r
                        tk = kb.step("tensor", trb, deps=[ev])
                        ev = kb.step("vector", (lambda e, kb4=kb4:
                                                e.tensor_copy(QT[:, kb4 * 512:(kb4 + 1) * 512],
                                                              PST[:])), deps=[tk])
                    hdone = []
                    for n in range(8):
                        ps = PSA if n % 2 == 0 else PSB
                        mdeps = [ev] if n < 2 else [ev, hdone[n - 2]]
                        def mm1(e, n=n, ps=ps):
                            for k in range(8):
                                e.matmul(ps[:], QT[:, k * 128:(k + 1) * 128],
                                         W1[k][:, n * 512:(n + 1) * 512],
                                         start=(k == 0), stop=(k == 7))
                        mt = kb.step("tensor", mm1, deps=mdeps)
                        if kind == "l0":
                            et = kb.step("vector", (lambda e, n=n, ps=ps:
                                                    e.scalar_tensor_tensor(
                                                        H[:, n * 512:(n + 1) * 512], ps[:],
                                                        CL[:, 7:8],
                                                        B1[:, n * 512:(n + 1) * 512],
                                                        ALU.mult, ALU.add)), deps=[mt])
                            et = kb.step("scalar", (lambda e, n=n:
                                                    e.activation(H[:, n * 512:(n + 1) * 512],
                                                                 H[:, n * 512:(n + 1) * 512],
                                                                 AF.Gelu, bias=f32(0.0),
                                                                 scale=f32(1.0))), deps=[et])
                        else:
                            et = kb.step("vector", (lambda e, n=n, ps=ps:
                                e.tensor_add(H[:, n * 512:(n + 1) * 512], ps[:],
                                             B1[:, n * 512:(n + 1) * 512])), deps=[mt])
                            et = kb.step("vector", (lambda e, n=n:
                                e.tensor_copy(H16[:, n * 512:(n + 1) * 512],
                                              H[:, n * 512:(n + 1) * 512])), deps=[et])
                            et = kb.step("vector", (lambda e, n=n:
                                e.tensor_scalar_max(H[:, n * 512:(n + 1) * 512],
                                                    H16[:, n * 512:(n + 1) * 512],
                                                    float(0.0))), deps=[et])
                        hdone.append(et)
                    t = kb.step("vector", lambda e: None, deps=hdone[-2:])
                    if kind == "l0":
                        t = emit_ln([t], CL, H[:], H[:], CF[:], HID)
                        t = kb.step("vector", lambda e: e.tensor_mul(H[:], H[:], LN2W[:]),
                                    deps=[t])
                        t = kb.step("vector", lambda e: e.tensor_add(H[:], H[:], LN2B[:]),
                                    deps=[t])
                        t = kb.step("vector", lambda e: e.tensor_reduce(
                            CL[:, 0:1], H[:], AX.X, ALU.max, apply_absolute_value=True),
                                    deps=[t])
                        t = kb.step("vector", lambda e: e.tensor_scalar_max(
                            CL[:, 0:1], CL[:, 0:1], float(1e-5)), deps=[t])
                        t = kb.step("vector", lambda e: e.reciprocal(CL[:, 1:2], CL[:, 0:1]),
                                    deps=[t])
                        t = kb.step("vector", lambda e: e.tensor_scalar(
                            CL[:, 1:2], CL[:, 1:2], 127.0, None, ALU.mult), deps=[t])
                        t = kb.step("vector", lambda e: [
                            e.tensor_scalar(H[:], H[:], CL[:, 1:2], None, ALU.mult),
                            e.tensor_scalar(CL[:, 7:8], CL[:, 0:1], scol[:, 3:4],
                                            float(1.0 / 127.0), ALU.mult, ALU.mult)], deps=[t])
                        t = kb.step("vector", lambda e: e.tensor_scalar(
                            H[:], H[:], 12582912.0, None, ALU.add), deps=[t])
                        t = kb.step("vector", lambda e: e.tensor_scalar(
                            QB[:], H[:], -12582912.0, None, ALU.add), deps=[t])
                        kb.step("sync", (lambda e, i=i:
                                         kb.dma(e, g["SC2"][i * 128:(i + 1) * 128, :],
                                                CL[:, 7:8])), deps=[t], n_dma=1)
                    else:
                        t = kb.step("vector", lambda e: e.tensor_copy(FQ8[:], H[:]),
                                    deps=[t])
                        t = kb.step("vector", lambda e: e.tensor_copy(QB[:], FQ8[:]),
                                    deps=[t])
                    prev = kb.step("sync", (lambda e, i=i:
                                            kb.dma(e, g["HQ"][i * 128:(i + 1) * 128, :],
                                                   QB[:])), deps=[t], n_dma=1)

    def ffn_b(kind, first, fold, out_final):
        PH[0] += 1
        with ExitStack() as pc:
            RB = pc.enter_context(S("fb_rb", [128, D], F32))
            AB = pc.enter_context(S("fb_ab", [128, D], F32)) if fold else None
            MBm = pc.enter_context(S("fb_mb", [128, D], F32)) if kind == "l1" else None
            U = pc.enter_context(S("fb_u", [128, D], F32))
            H16 = pc.enter_context(S("fb_h16", [128, D], F16)) if kind == "l1" else None
            QB = pc.enter_context(S("fb_qb", [128, HID], BF16))
            QT = pc.enter_context(S("fb_qt", [128, HID], BF16))
            CL = pc.enter_context(S("fb_cl", [128, 2], F32))
            B2 = pc.enter_context(S("fb_b2", [128, D], F32))
            FQ8 = pc.enter_context(S("fb_f8", [128, D], F8)) if kind == "l1" else None
            W2 = [pc.enter_context(S(f"fb_w2{i}", [128, D], BF16))
                  for i in range(32)]
            PSA = pc.enter_context(PP("fb_psa", [128, 512], F32))
            PSB = pc.enter_context(PP("fb_psb", [128, 512], F32))
            PST = pc.enter_context(PP("fb_pst", [128, 512], BF16))
            with nc.Block() as blk:
                kb.blk = blk
                phase_entry()
                w2src = g["W2Q0"] if kind == "l0" else g["W2Q1"]
                ldw = kb.step("sync", lambda e: [
                    *[kb.dma(e, W2[i][:], w2src[i * 128:(i + 1) * 128, :]) for i in range(32)],
                    kb.dma(e, B2[:], g["l0_b2_t" if kind == "l0" else "l1_b2_t"][:])],
                    n_dma=33)
                cdep = ldw
                if kind == "l1":
                    cdep = kb.step("vector", lambda e: e.tensor_copy(FQ8[:], B2[:]),
                                   deps=[ldw])
                    cdep = kb.step("vector", lambda e: e.tensor_copy(B2[:], FQ8[:]),
                                   deps=[cdep])
                prev = cdep
                for i in range(NT):
                    src = g["x_in"] if first else g["R"]
                    def ld_emit(e, i=i):
                        kb.dma(e, RB[:], src[i * 128:(i + 1) * 128, :])
                        kb.dma(e, QB[:], g["HQ"][i * 128:(i + 1) * 128, :])
                        if fold:
                            kb.dma(e, AB[:], g["A"][i * 128:(i + 1) * 128, :])
                        if kind == "l1":
                            kb.dma(e, MBm[:], g["mask_in"][i * 128:(i + 1) * 128, :])
                        if kind == "l0":
                            kb.dma(e, CL[:, 0:1], g["SC2"][i * 128:(i + 1) * 128, :])
                    nd = 2 + (1 if fold else 0) + 1
                    ldt = kb.step("sync", ld_emit, deps=[prev], n_dma=nd)
                    dep = ldt
                    if fold:
                        dep = kb.step("vector", lambda e: e.tensor_add(RB[:], RB[:], AB[:]),
                                      deps=[ldt])
                    ev = dep
                    for kb4 in range(8):
                        def trb(e, kb4=kb4):
                            r = None
                            for j in range(4):
                                k = kb4 * 4 + j
                                r = e.transpose(PST[:, j * 128:(j + 1) * 128],
                                                QB[:, k * 128:(k + 1) * 128], ident[:])
                            return r
                        tk = kb.step("tensor", trb, deps=[ev])
                        ev = kb.step("vector", (lambda e, kb4=kb4:
                                                e.tensor_copy(QT[:, kb4 * 512:(kb4 + 1) * 512],
                                                              PST[:])), deps=[tk])
                    edone = []
                    for n in range(2):
                        ps = PSA if n % 2 == 0 else PSB
                        def mm2(e, n=n, ps=ps):
                            for k in range(32):
                                e.matmul(ps[:], QT[:, k * 128:(k + 1) * 128],
                                         W2[k][:, n * 512:(n + 1) * 512],
                                         start=(k == 0), stop=(k == 31))
                        mt = kb.step("tensor", mm2, deps=[ev])
                        sl = slice(n * 512, (n + 1) * 512)
                        if kind == "l0":
                            et = kb.step("vector", (lambda e, ps=ps, sl=sl:
                                e.scalar_tensor_tensor(U[:, sl], ps[:], CL[:, 0:1],
                                                       B2[:, sl], ALU.mult, ALU.add)),
                                         deps=[mt])
                            et = kb.step("vector", (lambda e, sl=sl:
                                e.scalar_tensor_tensor(RB[:, sl], U[:, sl], float(0.5),
                                                       RB[:, sl], ALU.mult, ALU.add)),
                                         deps=[et])
                        else:
                            et = kb.step("vector", (lambda e, ps=ps, sl=sl:
                                e.tensor_add(U[:, sl], ps[:], B2[:, sl])), deps=[mt])
                            et = kb.step("vector", (lambda e, sl=sl:
                                e.tensor_copy(H16[:, sl], U[:, sl])), deps=[et])
                            et = kb.step("vector", (lambda e, sl=sl:
                                e.tensor_tensor(U[:, sl], H16[:, sl], MBm[:, sl], ALU.mult)),
                                         deps=[et])
                            et = kb.step("vector", (lambda e, sl=sl:
                                e.scalar_tensor_tensor(RB[:, sl], U[:, sl], float(0.5),
                                                       RB[:, sl], ALU.mult, ALU.add)),
                                         deps=[et])
                        edone.append(et)
                    dst = g["y_out"] if out_final else g["R"]
                    prev = kb.step("sync", (lambda e, i=i, dst=dst:
                                            kb.dma(e, dst[i * 128:(i + 1) * 128, :], RB[:])),
                                   deps=edone, n_dma=1)

    def ut_phase():
        PH[0] += 1
        with ExitStack() as pc:
            RB = pc.enter_context(S("ut_rb", [128, D], F32))
            U = pc.enter_context(S("ut_u", [128, D], F32))
            UB = pc.enter_context(S("ut_ub", [128, D], BF16))
            UTT = pc.enter_context(S("ut_t", [128, D], BF16))
            CL = pc.enter_context(S("ut_cl", [128, 8], F32))
            PST = pc.enter_context(PP("ut_pst", [128, 512], BF16))
            with nc.Block() as blk:
                kb.blk = blk
                phase_entry()
                prev = None
                for i in range(NT):
                    ldt = kb.step("sync", (lambda e, i=i:
                                           kb.dma(e, RB[:], g["R"][i * 128:(i + 1) * 128, :])),
                                  deps=[prev], n_dma=1)
                    t = emit_ln([ldt], CL, RB[:], U[:], U[:], D)
                    t = kb.step("vector", lambda e: e.tensor_copy(UB[:], U[:]), deps=[t])
                    ev = t
                    for kb4 in range(2):
                        def trb(e, kb4=kb4):
                            r = None
                            for j in range(4):
                                k = kb4 * 4 + j
                                r = e.transpose(PST[:, j * 128:(j + 1) * 128],
                                                UB[:, k * 128:(k + 1) * 128], ident[:])
                            return r
                        tk = kb.step("tensor", trb, deps=[ev])
                        ev = kb.step("vector", (lambda e, kb4=kb4:
                                                e.tensor_copy(UTT[:, kb4 * 512:(kb4 + 1) * 512],
                                                              PST[:])), deps=[tk])
                    def st_emit(e, i=i):
                        for k in range(8):
                            kb.dma(e, g["UT"][k * 128:(k + 1) * 128, i * 128:(i + 1) * 128],
                                   UTT[:, k * 128:(k + 1) * 128])
                    prev = kb.step("sync", st_emit, deps=[ev], n_dma=8)

    def l0_attn_phase():
        PH[0] += 1
        with ExitStack() as pc:
            UH = pc.enter_context(S("la_uh", [64, T], BF16))
            WDQ = pc.enter_context(S("la_wdq", [64, 512], BF16))
            WUQ = pc.enter_context(S("la_wuq", [128, 256], BF16))
            WDKV = pc.enter_context(S("la_wdkv", [64, 1024], BF16))
            WUKV = pc.enter_context(S("la_wukv", [128, 1024], BF16))
            WQR = pc.enter_context(S("la_wqr", [64, 64], BF16))
            WKR = pc.enter_context(S("la_wkr", [64, 64], BF16))
            WO = pc.enter_context(S("la_wo", [64, 64], BF16))
            QNW = pc.enter_context(S("la_qnw", [64, 1], F32))
            KNW = pc.enter_context(S("la_knw", [64, 1], F32))
            WF = pc.enter_context(S("la_wf", [128, 1024], F32))
            CQT = [pc.enter_context(S(f"la_cqt{l}", [128, T], BF16))
                   for l in range(4)]
            CKV = pc.enter_context(S("la_ckv", [128, 4096], BF16))
            QTt = pc.enter_context(S("la_qt", [128, T], BF16))
            KTt = pc.enter_context(S("la_kt", [128, T], BF16))
            VA = pc.enter_context(S("la_va", [128, 32 * 65], BF16))
            SQs = pc.enter_context(S("la_sq", [128, 512], F32))
            U64 = pc.enter_context(S("la_u64", [64, 512], F32))
            VTs = pc.enter_context(S("la_vt", [64, 512], BF16))
            RST = pc.enter_context(S("la_rst", [1, 512], F32))
            ROW = pc.enter_context(S("la_row", [1, 512], F32))
            MX = pc.enter_context(S("la_mx", [1, 4], F32))
            BCOL = pc.enter_context(S("la_bcol", [128, 1], F32))
            ZC = pc.enter_context(S("la_zc", [128, 1], F32))
            ES = [pc.enter_context(S(f"la_es{i}", [128, 512], BF16)) for i in range(2)]
            CTXs = pc.enter_context(S("la_ctx", [64, 512], BF16))
            ZR = pc.enter_context(S("la_zr", [1, 512], F32))
            OH = pc.enter_context(S("la_oh", [128, 64], F32))
            TRI = pc.enter_context(S("la_tri", [128, 2048], BF16))
            PSA = pc.enter_context(PP("la_psa", [128, 512], F32))
            PSB = pc.enter_context(PP("la_psb", [64, 512], F32))
            PSS = pc.enter_context(PP("la_pss", [128, 512], F32))
            PSS2 = pc.enter_context(PP("la_pss2", [128, 512], F32))
            PS65 = pc.enter_context(PP("la_ps65", [65, 512], F32))
            PSO = pc.enter_context(PP("la_pso", [128, 64], F32))
            PST = pc.enter_context(PP("la_pst", [128, 128], BF16))
            with nc.Block() as blk:
                kb.blk = blk
                phase_entry()
                wdep = None
                for src, dst, p, w in (("w_dq_t", WDQ, 64, 512), ("w_uq_t", WUQ, 128, None),
                                       ("w_dkv_t", WDKV, 64, 1024), ("w_ukv_t", WUKV, 128, None),
                                       ("w_qr_t", WQR, 64, 64), ("w_kr_t", WKR, 64, 64),
                                       ("w_o_t", WO, 64, 64)):
                    if src == "w_uq_t":
                        def ld_uq(e, src=src):
                            for c in range(4):
                                kb.dma(e, WF[:, c * 64:(c + 1) * 64],
                                       g[src][c * 128:(c + 1) * 128, :])
                        lt = kb.step("sync", ld_uq, deps=[wdep], n_dma=4)
                        wdep = kb.step("vector", (lambda e, dst=dst:
                                                  e.tensor_copy(dst[:, 0:256], WF[:, 0:256])),
                                       deps=[lt])
                    elif src == "w_ukv_t":
                        def ld_ukv(e, src=src):
                            for c in range(8):
                                kb.dma(e, WF[:, c * 128:(c + 1) * 128],
                                       g[src][c * 128:(c + 1) * 128, :])
                        lt = kb.step("sync", ld_ukv, deps=[wdep], n_dma=8)
                        wdep = kb.step("vector", (lambda e, dst=dst:
                                                  e.tensor_copy(dst[:, 0:1024], WF[:, 0:1024])),
                                       deps=[lt])
                    else:
                        lt = kb.step("sync", (lambda e, src=src, p=p, w=w:
                                              kb.dma(e, WF[0:p, 0:w], g[src][:])),
                                     deps=[wdep], n_dma=1)
                        wdep = kb.step("vector", (lambda e, dst=dst, p=p, w=w:
                                                  e.tensor_copy(dst[0:p, 0:w], WF[0:p, 0:w])),
                                       deps=[lt])
                lt = kb.step("sync", lambda e: [kb.dma(e, QNW[:], g["qnw_c"][:]),
                                                kb.dma(e, KNW[:], g["knw_c"][:]),
                                                kb.dma(e, TRI[:], g["tri_masks"][:])],
                             deps=[wdep], n_dma=3)
                vset = kb.step("vector", lambda e: [
                    e.memset(VA[:, 64 + 65 * c:65 + 65 * c], 1.0) for c in range(32)],
                    deps=[lt])
                hprev = vset
                for h in range(16):
                    ldu = kb.step("sync", (lambda e, h=h:
                                           kb.dma(e, UH[:], g["UT"][h * 64:(h + 1) * 64, :])),
                                  deps=[hprev], n_dma=1)
                    tprev = ldu
                    for tc in range(8):
                        tsl = slice(tc * 512, (tc + 1) * 512)
                        for l in range(4):
                            mt = kb.step("tensor", (lambda e, l=l, tsl=tsl:
                                                    e.matmul(PSA[:], WDQ[:, l * 128:(l + 1) * 128],
                                                             UH[:, tsl], start=True, stop=True)),
                                         deps=[tprev])
                            tprev = kb.step("vector", (lambda e, l=l, tsl=tsl:
                                                       e.tensor_copy(CQT[l][:, tsl], PSA[:])),
                                            deps=[mt])
                        def mmqc(e, tsl=tsl):
                            for l in range(4):
                                e.matmul(PSB[:], WUQ[:, l * 64:(l + 1) * 64],
                                         CQT[l][:, tsl], start=(l == 0), stop=(l == 3))
                        mt = kb.step("tensor", mmqc, deps=[tprev])
                        t = kb.step("vector", lambda e: e.tensor_copy(U64[:], PSB[:]),
                                    deps=[mt])
                        t = kb.step("vector", lambda e: e.tensor_tensor(
                            SQs[0:64, :], U64[:], U64[:], ALU.mult), deps=[t])
                        t = kb.step("tensor", lambda e: e.matmul(PSS[0:1, :], onescf[0:64, :],
                                                                 SQs[0:64, :], start=True,
                                                                 stop=True), deps=[t])
                        t = kb.step("scalar", lambda e: e.activation(
                            RST[:], PSS[0:1, :], AF.Sqrt, bias=RMS_EPS,
                            scale=f32(1.0 / 64.0)), deps=[t])
                        t = kb.step("vector", lambda e: e.reciprocal(RST[:], RST[:]), deps=[t])
                        t = kb.step("tensor", lambda e: e.matmul(PSS[0:64, :], onesrf[0:1, 0:64],
                                                                 RST[:], start=True, stop=True),
                                    deps=[t])
                        t = kb.step("vector", lambda e: e.tensor_tensor(
                            U64[:], U64[:], PSS[0:64, :], ALU.mult), deps=[t])
                        t = kb.step("vector", (lambda e, tsl=tsl:
                            e.tensor_scalar(QTt[0:64, tsl], U64[:], QNW[:], float(0.125),
                                            ALU.mult, ALU.mult)), deps=[t])
                        mt = kb.step("tensor", (lambda e, tsl=tsl:
                                                e.matmul(PSB[:], WQR[:], QTt[0:64, tsl],
                                                         start=True, stop=True)), deps=[t])
                        t = kb.step("vector", (lambda e, tsl=tsl:
                                               e.tensor_copy(QTt[64:128, tsl], PSB[:])),
                                    deps=[mt])
                        for l in range(8):
                            mt = kb.step("tensor", (lambda e, l=l, tsl=tsl:
                                                    e.matmul(PSA[:], WDKV[:, l * 128:(l + 1) * 128],
                                                             UH[:, tsl], start=True, stop=True)),
                                         deps=[t])
                            t = kb.step("vector", (lambda e, l=l:
                                                   e.tensor_copy(CKV[:, l * 512:(l + 1) * 512],
                                                                 PSA[:])), deps=[mt])
                        def mmkv(e):
                            for l in range(8):
                                e.matmul(PSA[:], WUKV[:, l * 128:(l + 1) * 128],
                                         CKV[:, l * 512:(l + 1) * 512],
                                         start=(l == 0), stop=(l == 7))
                        mt = kb.step("tensor", mmkv, deps=[t])
                        t = kb.step("vector", lambda e: e.tensor_copy(U64[:], PSA[0:64, :]),
                                    deps=[mt])
                        t = kb.step("vector", lambda e: e.tensor_tensor(
                            SQs[0:64, :], U64[:], U64[:], ALU.mult), deps=[t])
                        t = kb.step("tensor", lambda e: e.matmul(PSS[0:1, :], onescf[0:64, :],
                                                                 SQs[0:64, :], start=True,
                                                                 stop=True), deps=[t])
                        t = kb.step("scalar", lambda e: e.activation(
                            RST[:], PSS[0:1, :], AF.Sqrt, bias=RMS_EPS,
                            scale=f32(1.0 / 64.0)), deps=[t])
                        t = kb.step("vector", lambda e: e.reciprocal(RST[:], RST[:]), deps=[t])
                        t = kb.step("tensor", lambda e: e.matmul(PSS[0:64, :], onesrf[0:1, 0:64],
                                                                 RST[:], start=True, stop=True),
                                    deps=[t])
                        t = kb.step("vector", lambda e: [
                            e.tensor_tensor(U64[:], U64[:], PSS[0:64, :], ALU.mult),
                            e.tensor_copy(VTs[:], PSA[64:128, :])], deps=[t])
                        t = kb.step("vector", (lambda e, tsl=tsl:
                            e.tensor_scalar(KTt[0:64, tsl], U64[:], KNW[:], None, ALU.mult)),
                                    deps=[t])
                        mt = kb.step("tensor", (lambda e, tsl=tsl:
                                                e.matmul(PSB[:], WKR[:], UH[:, tsl],
                                                         start=True, stop=True)), deps=[t])
                        t = kb.step("vector", (lambda e, tsl=tsl:
                                               e.tensor_copy(KTt[64:128, tsl], PSB[:])),
                                    deps=[mt])
                        for j in range(4):
                            tk = kb.step("tensor", (lambda e, j=j:
                                                    e.transpose(PST[0:128, 0:64],
                                                                VTs[:, j * 128:(j + 1) * 128],
                                                                ident[0:64, 0:64])), deps=[t])
                            t = kb.step("vector", (lambda e, j=j, tc=tc:
                                                   e.tensor_copy(
                                                       VA[:, (tc * 4 + j) * 65:(tc * 4 + j) * 65 + 64],
                                                       PST[0:128, 0:64])), deps=[tk])
                        tprev = t
                    t = kb.step("vector", lambda e: e.memset(MX[:, 0:2], 0.0), deps=[tprev])
                    for which, XT, mxi in (("q", QTt, 0), ("k", KTt, 1)):
                        for tc in range(8):
                            tsl = slice(tc * 512, (tc + 1) * 512)
                            t2 = kb.step("scalar", (lambda e, XT=XT, tsl=tsl:
                                                    e.activation(SQs[:], XT[:, tsl], AF.Square,
                                                                 bias=f32(0.0), scale=f32(1.0))),
                                         deps=[t])
                            t2 = kb.step("tensor", lambda e: e.matmul(PSS[0:1, :], onescf[:],
                                                                      SQs[:], start=True,
                                                                      stop=True), deps=[t2])
                            t2 = kb.step("vector", lambda e: e.tensor_copy(
                                ROW[:], PSS[0:1, :]), deps=[t2])
                            t2 = kb.step("vector", lambda e: e.tensor_reduce(
                                MX[:, 2:3], ROW[:], AX.X, ALU.max), deps=[t2])
                            t = kb.step("vector", (lambda e, mxi=mxi:
                                e.tensor_max(MX[:, mxi:mxi + 1], MX[:, mxi:mxi + 1],
                                             MX[:, 2:3])), deps=[t2])
                    t = kb.step("vector", lambda e: e.tensor_tensor(MX[:, 2:3], MX[:, 0:1],
                                                                    MX[:, 1:2], ALU.mult),
                                deps=[t])
                    t = kb.step("scalar", lambda e: e.activation(MX[:, 3:4], MX[:, 2:3],
                                                                 AF.Sqrt, bias=f32(0.0),
                                                                 scale=f32(1.0)), deps=[t])
                    t = kb.step("vector", lambda e: e.tensor_scalar(MX[:, 3:4], MX[:, 3:4],
                                                                    float(-1.0), None, ALU.mult),
                                deps=[t])
                    t = kb.step("tensor", lambda e: e.matmul(PSO[:, 0:1], onesrf[:], MX[:, 3:4],
                                                             start=True, stop=True), deps=[t])
                    t = kb.step("vector", lambda e: e.tensor_copy(BCOL[:], PSO[:, 0:1]),
                                deps=[t])
                    hq = t
                    for qt in range(8):
                        qsl = slice(qt * 512, (qt + 1) * 512)
                        nk = 4 * qt + 4
                        ctx_t = [None, None]
                        exp_t = [None, None]
                        edep = hq
                        for kt in range(nk):
                            p = kt % 2
                            ps = PSS if p == 0 else PSS2
                            es = ES[p]
                            mt = kb.step("tensor", (lambda e, kt=kt, qsl=qsl, ps=ps:
                                                    e.matmul(ps[:], KTt[:, kt * 128:(kt + 1) * 128],
                                                             QTt[:, qsl], start=True, stop=True)),
                                         deps=[hq, exp_t[p]])
                            et = kb.step("scalar", (lambda e, ps=ps, es=es: e.activation(
                                es[:], ps[:], AF.Exp, bias=BCOL[:], scale=f32(1.0))),
                                deps=[mt, ctx_t[p]])
                            if kt >= 4 * qt:
                                off = kt - 4 * qt
                                et = kb.step("vector", (lambda e, off=off, es=es:
                                                        e.tensor_tensor(es[:], es[:],
                                                                        TRI[:, off * 512:(off + 1) * 512],
                                                                        ALU.mult)), deps=[et])
                            exp_t[p] = et
                            edep = kb.step("tensor", (lambda e, kt=kt, nk=nk, es=es:
                                                      e.matmul(PS65[:], VA[:, kt * 65:kt * 65 + 65],
                                                               es[:], start=(kt == 0),
                                                               stop=(kt == nk - 1))), deps=[et])
                            ctx_t[p] = edep
                        t = kb.step("vector", lambda e: [
                            e.tensor_copy(CTXs[:], PS65[0:64, :]),
                            e.tensor_copy(ZR[:], PS65[64:65, :])], deps=[edep])
                        t = kb.step("vector", lambda e: e.tensor_scalar(
                            ZR[:], ZR[:], 1e-30, None, ALU.add), deps=[t])
                        t = kb.step("vector", lambda e: e.reciprocal(ZR[:], ZR[:]), deps=[t])
                        pvt = t
                        for qs in range(4):
                            mt = kb.step("tensor", (lambda e, qs=qs:
                                                    e.matmul(PSO[:], CTXs[:, qs * 128:(qs + 1) * 128],
                                                             WO[:], start=True, stop=True)),
                                         deps=[t, pvt])
                            zt = kb.step("tensor", (lambda e, qs=qs:
                                                    e.matmul(PSS[0:128, 0:1],
                                                             ZR[0:1, qs * 128:(qs + 1) * 128],
                                                             ones11[:], start=True, stop=True)),
                                         deps=[t, pvt])
                            vt = kb.step("vector", lambda e: e.tensor_copy(
                                ZC[:], PSS[0:128, 0:1]), deps=[mt, zt, hq])
                            vt = kb.step("vector", lambda e: e.tensor_scalar(
                                OH[:], PSO[:], ZC[:], None, ALU.mult), deps=[vt])
                            pvt = vt
                            hq = kb.step("sync", (lambda e, qt=qt, qs=qs, h=h:
                                                  kb.dma(e, g["A"][(qt * 4 + qs) * 128:(qt * 4 + qs + 1) * 128,
                                                                   h * 64:(h + 1) * 64],
                                                         OH[:])), deps=[vt], n_dma=1)
                    hprev = hq

    def l1_attn_phase():
        PH[0] += 1
        SEG = 1024
        with ExitStack() as pc:
            RB = pc.enter_context(S("lb_rb", [128, D], F32))
            XN = pc.enter_context(S("lb_xn", [128, D], F32))
            CL = pc.enter_context(S("lb_cl", [128, 8], F32))
            LNG = pc.enter_context(S("lb_lng", [128, D], F32))
            LNB = pc.enter_context(S("lb_lnb", [128, D], F32))
            BQC = pc.enter_context(S("lb_bqc", [128, 24], F32))
            BGC = pc.enter_context(S("lb_bgc", [128, 8], F32))
            BPR = pc.enter_context(S("lb_bpr", [128, D], F32))
            XNT = pc.enter_context(S("lb_xnt", [128, 4096], BF16))
            XRT = pc.enter_context(S("lb_xrt", [128, 4096], BF16))
            WQKV = [pc.enter_context(S(f"lb_wq{i}", [128, 3 * D], BF16)) for i in range(8)]
            WG = [pc.enter_context(S(f"lb_wg{i}", [128, D], BF16)) for i in range(8)]
            WP = [pc.enter_context(S(f"lb_wp{i}", [128, D], BF16)) for i in range(8)]
            WF = pc.enter_context(S("lb_wf", [128, 3 * D], F32))
            QKV = [pc.enter_context(S(f"lb_qkv{i}", [128, 512], BF16)) for i in range(24)]
            GT = [pc.enter_context(S(f"lb_gt{i}", [128, 512], BF16)) for i in range(8)]
            GR = pc.enter_context(S("lb_gr", [1, 512], F32))
            EW = pc.enter_context(S("lb_ew", [128, 512], F32))
            EW2 = pc.enter_context(S("lb_ew2", [128, 512], F32))
            OT = [pc.enter_context(S(f"lb_ot{i}", [128, SEG], BF16)) for i in range(8)]
            KC = pc.enter_context(S("lb_kc", [128, 64], BF16))
            VC = pc.enter_context(S("lb_vc", [128, 65], BF16))
            SM = pc.enter_context(S("lb_sm", [128, 128], BF16))
            NTs = pc.enter_context(S("lb_nt", [64, 128], BF16))
            DR = pc.enter_context(S("lb_dr", [1, 128], F32))
            KVS = pc.enter_context(S("lb_kvs", [64, 1536], BF16))
            LTRI = pc.enter_context(S("lb_ltri", [128, 128], BF16))
            AB = pc.enter_context(S("lb_abf", [128, D], F32))
            XB = pc.enter_context(S("lb_xb", [128, D], BF16))
            QH0 = pc.enter_context(S("lb_qh0", [64, 128], BF16))
            PSA = pc.enter_context(PP("lb_psa", [128, 512]))
            PSR = pc.enter_context(PP("lb_psr", [1, 512]))
            PSG = pc.enter_context(PP("lb_psg", [128, 512]))
            PSN = pc.enter_context(PP("lb_psn", [65, 128]))
            PSK = pc.enter_context(PP("lb_psk", [64, 1536]))
            PST = pc.enter_context(PP("lb_pst", [128, 512], BF16))
            with nc.Block() as blk:
                kb.blk = blk
                phase_entry()
                ldt = None
                for i in range(8):
                    lt = kb.step("sync", (lambda e, i=i:
                                          kb.dma(e, WF[:, 0:3 * D],
                                                 g["w_qkv"][i * 128:(i + 1) * 128, :])),
                                 deps=[ldt], n_dma=1)
                    ldt = kb.step("vector", (lambda e, i=i:
                                             e.tensor_copy(WQKV[i][:], WF[:, 0:3 * D])),
                                  deps=[lt])
                for i in range(8):
                    lt = kb.step("sync", (lambda e, i=i:
                                          kb.dma(e, WF[:, 0:D],
                                                 g["w_gate"][i * 128:(i + 1) * 128, :])),
                                 deps=[ldt], n_dma=1)
                    ldt = kb.step("vector", (lambda e, i=i:
                                             e.tensor_copy(WG[i][:], WF[:, 0:D])), deps=[lt])
                for i in range(8):
                    lt = kb.step("sync", (lambda e, i=i:
                                          kb.dma(e, WF[:, 0:D],
                                                 g["w_proj"][i * 128:(i + 1) * 128, :])),
                                 deps=[ldt], n_dma=1)
                    ldt = kb.step("vector", (lambda e, i=i:
                                             e.tensor_copy(WP[i][:], WF[:, 0:D])), deps=[lt])
                ldt = kb.step("sync", lambda e: [
                    kb.dma(e, LNG[:], g["l1_lng_t"][:]), kb.dma(e, LNB[:], g["l1_lnb_t"][:]),
                    kb.dma(e, BQC[:], g["bqkv_cols"][:]), kb.dma(e, BGC[:], g["bgate_cols"][:]),
                    kb.dma(e, BPR[:], g["b_proj_t"][:]), kb.dma(e, LTRI[:], g["ltri_sm"][:])],
                    deps=[ldt], n_dma=6)
                t = kb.step("vector", lambda e: [
                    e.memset(KVS[:], 0.0),
                    e.memset(VC[:, 64:65], 1.0)], deps=[ldt])
                sprev = t
                stq = None
                for seg in range(4):
                    for half in range(2):
                        base_tok = seg * SEG + half * 512
                        t = sprev
                        for ti in range(4):
                            tok = base_tok + ti * 128
                            ldt = kb.step("sync", (lambda e, tok=tok:
                                                   kb.dma(e, RB[:], g["R"][tok:tok + 128, :])),
                                          deps=[t], n_dma=1)
                            t = emit_ln([ldt], CL, RB[:], XN[:], XN[:], D)
                            t = kb.step("vector", lambda e: e.tensor_mul(XN[:], XN[:], LNG[:]),
                                        deps=[t])
                            t = kb.step("vector", lambda e: e.tensor_add(XN[:], XN[:], LNB[:]),
                                        deps=[t])
                            t = kb.step("vector", lambda e: e.tensor_copy(XB[:], XN[:]),
                                        deps=[t])
                            for kb4 in range(2):
                                def trb(e, kb4=kb4):
                                    r = None
                                    for j in range(4):
                                        k = kb4 * 4 + j
                                        r = e.transpose(PST[:, j * 128:(j + 1) * 128],
                                                        XB[:, k * 128:(k + 1) * 128], ident[:])
                                    return r
                                tk = kb.step("tensor", trb, deps=[t])
                                def evb(e, kb4=kb4, ti=ti):
                                    r = None
                                    for j in range(4):
                                        k = kb4 * 4 + j
                                        r = e.tensor_copy(
                                            XNT[:, k * 512 + ti * 128:k * 512 + (ti + 1) * 128],
                                            PST[:, j * 128:(j + 1) * 128])
                                    return r
                                t = kb.step("vector", evb, deps=[tk])
                            t = kb.step("vector", lambda e: e.tensor_copy(XB[:], RB[:]),
                                        deps=[t])
                            for kb4 in range(2):
                                def trb(e, kb4=kb4):
                                    r = None
                                    for j in range(4):
                                        k = kb4 * 4 + j
                                        r = e.transpose(PST[:, j * 128:(j + 1) * 128],
                                                        XB[:, k * 128:(k + 1) * 128], ident[:])
                                    return r
                                tk = kb.step("tensor", trb, deps=[t])
                                def evb(e, kb4=kb4, ti=ti):
                                    r = None
                                    for j in range(4):
                                        k = kb4 * 4 + j
                                        r = e.tensor_copy(
                                            XRT[:, k * 512 + ti * 128:k * 512 + (ti + 1) * 128],
                                            PST[:, j * 128:(j + 1) * 128])
                                    return r
                                t = kb.step("vector", evb, deps=[tk])
                        gdone = []
                        for m in range(8):
                            def mmg(e, m=m):
                                for k in range(8):
                                    e.matmul(PSA[:], WG[k][:, m * 128:(m + 1) * 128],
                                             XRT[:, k * 512:(k + 1) * 512],
                                             start=(k == 0), stop=(k == 7))
                            mt = kb.step("tensor", mmg, deps=[t] + ([gdone[-1]] if gdone else []))
                            gt = kb.step("scalar", (lambda e, m=m:
                                                    e.activation(GT[m][:], PSA[:], AF.Sigmoid,
                                                                 bias=BGC[:, m:m + 1],
                                                                 scale=f32(1.0))), deps=[mt])
                            gdone.append(gt)
                        def mmsum(e):
                            for m in range(8):
                                e.matmul(PSR[:], onescb[:], GT[m][:],
                                         start=(m == 0), stop=(m == 7))
                        t = kb.step("tensor", mmsum, deps=gdone)
                        t = kb.step("vector", lambda e: e.tensor_scalar(
                            GR[:], PSR[:], float(1.0 / D), 1e-05, ALU.mult, ALU.add), deps=[t])
                        t = kb.step("vector", lambda e: e.reciprocal(GR[:], GR[:]), deps=[t])
                        t = kb.step("tensor", lambda e: e.matmul(PSG[:], onesrf[:], GR[:],
                                                                 start=True, stop=True), deps=[t])
                        gd = t
                        qdone = []
                        for m in range(24):
                            def mmq(e, m=m):
                                for k in range(8):
                                    e.matmul(PSA[:], WQKV[k][:, m * 128:(m + 1) * 128],
                                             XNT[:, k * 512:(k + 1) * 512],
                                             start=(k == 0), stop=(k == 7))
                            mt = kb.step("tensor", mmq, deps=[gd] + ([qdone[-1]] if qdone else []))
                            qt_ = kb.step("scalar", (lambda e, m=m:
                                                     e.activation(QKV[m][:], PSA[:], AF.Identity,
                                                                  bias=BQC[:, m:m + 1],
                                                                  scale=f32(1.0))), deps=[mt])
                            qdone.append(qt_)
                        gq = qdone[-1]
                        for m in range(16):
                            gq = kb.step("vector", (lambda e, m=m:
                                                    e.tensor_tensor(EW[:], QKV[m][:], PSG[:],
                                                                    ALU.mult)), deps=[gq, qdone[m]])
                            gq = kb.step("vector", lambda e: e.tensor_scalar_min(
                                EW2[:], EW[:], float(0.0)), deps=[gq])
                            gq = kb.step("scalar", lambda e: e.activation(
                                EW2[:], EW2[:], AF.Exp, bias=f32(0.0), scale=f32(1.0)),
                                deps=[gq])
                            gq = kb.step("vector", lambda e: e.tensor_scalar_max(
                                EW[:], EW[:], float(0.0)), deps=[gq])
                            gq = kb.step("vector", (lambda e, m=m:
                                                    e.tensor_add(QKV[m][:], EW2[:], EW[:])),
                                         deps=[gq])
                        sc = gq
                        for h in range(16):
                            mq = h // 2
                            r0 = (h % 2) * 64
                            hbase = (h // 7) * 512 + (h % 7) * 65
                            hs = slice(hbase, hbase + 65)
                            for c in range(4):
                                gc = seg * 8 + half * 4 + c
                                csl = slice(c * 128, (c + 1) * 128)
                                qh = QKV[mq]
                                kh = QKV[8 + mq]
                                vh = QKV[16 + mq]
                                tk = kb.step("tensor", (lambda e, kh=kh, r0=r0, csl=csl:
                                                        e.transpose(PST[0:128, 0:64],
                                                                    kh[r0:r0 + 64, csl],
                                                                    ident[r0:r0 + 64, r0:r0 + 64])),
                                             deps=[sc])
                                sc = kb.step("vector", lambda e: e.tensor_copy(
                                    KC[:], PST[0:128, 0:64]), deps=[tk])
                                tk = kb.step("tensor", (lambda e, vh=vh, r0=r0, csl=csl:
                                                        e.transpose(PST[0:128, 0:64],
                                                                    vh[r0:r0 + 64, csl],
                                                                    ident[r0:r0 + 64, r0:r0 + 64])),
                                             deps=[sc])
                                sc = kb.step("vector", lambda e: e.tensor_copy(
                                    VC[:, 0:64], PST[0:128, 0:64]), deps=[tk])
                                mt = kb.step("tensor", (lambda e, kh=kh, qh=qh, r0=r0, csl=csl:
                                                        e.matmul(PSA[:, 0:128],
                                                                 kh[r0:r0 + 64, csl],
                                                                 qh[r0:r0 + 64, csl],
                                                                 start=True, stop=True)),
                                             deps=[sc])
                                sc = kb.step("vector", lambda e: e.tensor_tensor(
                                    SM[:], PSA[:, 0:128], LTRI[:], ALU.mult), deps=[mt])
                                sc = kb.step("vector", (lambda e, qh=qh, r0=r0, csl=csl:
                                                        e.tensor_copy(QH0[:], qh[r0:r0 + 64, csl])),
                                             deps=[sc])
                                mt = kb.step("tensor", (lambda e, hs=hs:
                                                        [e.matmul(PSN[:], VC[:], SM[:],
                                                                  start=True, stop=False),
                                                         e.matmul(PSN[:], KVS[:, hs],
                                                                  QH0[:],
                                                                  start=False, stop=True)]),
                                             deps=[sc])
                                st_ = kb.step("tensor", (lambda e, hs=hs, gc=gc:
                                                         e.matmul(PSK[:, hs], KC[:], VC[:],
                                                                  start=(gc == 0),
                                                                  stop=(gc == 31),
                                                                  skip_group_check=True)),
                                              deps=[mt])
                                sc = kb.step("vector", (lambda e, hs=hs:
                                                        e.tensor_copy(KVS[:, hs], PSK[:, hs])),
                                             deps=[st_])
                                sc2 = kb.step("vector", lambda e: e.tensor_copy(
                                    NTs[:], PSN[0:64, :]), deps=[mt, sc])
                                sc2 = kb.step("vector", lambda e: e.tensor_scalar(
                                    DR[:], PSN[64:65, :], 1e-05, None, ALU.add), deps=[sc2])
                                sc2 = kb.step("vector", lambda e: e.reciprocal(DR[:], DR[:]),
                                              deps=[sc2])
                                bt = kb.step("tensor", lambda e: e.matmul(
                                    PSG[0:64, 0:128], onesrf[0:1, 0:64], DR[:],
                                    start=True, stop=True), deps=[sc2])
                                sc = kb.step("vector", (lambda e, mq=mq, r0=r0, half=half, csl=csl:
                                                        e.tensor_tensor(
                                                            OT[mq][r0:r0 + 64,
                                                                   half * 512 + csl.start:half * 512 + csl.stop],
                                                            NTs[:], PSG[0:64, 0:128],
                                                            ALU.mult)), deps=[bt, sc])
                        sprev = sc
                    pdone = sprev
                    for ti8 in range(8):
                        edone = []
                        for n in range(2):
                            def mmp(e, ti8=ti8, n=n):
                                for k in range(8):
                                    e.matmul(PSA[:], OT[k][:, ti8 * 128:(ti8 + 1) * 128],
                                             WP[k][:, n * 512:(n + 1) * 512],
                                             start=(k == 0), stop=(k == 7))
                            mt = kb.step("tensor", mmp, deps=[pdone])
                            et = kb.step("vector", (lambda e, n=n: e.tensor_add(
                                AB[:, n * 512:(n + 1) * 512], PSA[:],
                                BPR[:, n * 512:(n + 1) * 512])), deps=[mt])
                            edone.append(et)
                            pdone = et
                        stq = kb.step("sync", (lambda e, seg=seg, ti8=ti8:
                                               kb.dma(e, g["A"][seg * SEG + ti8 * 128:
                                                                seg * SEG + (ti8 + 1) * 128, :],
                                                      AB[:])), deps=edone + [stq], n_dma=1)
                        pdone = stq
                    sprev = stq


    seq = []
    seq.append(("ffn_a", dict(kind="l0", first=True, fold=False)))
    seq.append(("ffn_b", dict(kind="l0", first=True, fold=False, out_final=False)))
    if UPTO >= 2:
        seq.append(("ut", {}))
        seq.append(("l0at", {}))
    if UPTO >= 3:
        seq.append(("ffn_a", dict(kind="l0", first=False, fold=True)))
        seq.append(("ffn_b", dict(kind="l0", first=False, fold=True, out_final=False)))
    if UPTO >= 4:
        seq.append(("ffn_a", dict(kind="l1", first=False, fold=False)))
        seq.append(("ffn_b", dict(kind="l1", first=False, fold=False, out_final=False)))
    if UPTO >= 5:
        seq.append(("l1at", {}))
    if UPTO >= 6:
        seq.append(("ffn_a", dict(kind="l1", first=False, fold=True)))
        seq.append(("ffn_b", dict(kind="l1", first=False, fold=True, out_final=True)))
    final_is_R = UPTO < 6

    for name, kw in seq:
        if name == "ffn_a":
            ffn_a(**kw)
        elif name == "ffn_b":
            ffn_b(**kw)
        elif name == "ut":
            ut_phase()
        elif name == "l0at":
            l0_attn_phase()
        elif name == "l1at":
            l1_attn_phase()

    if final_is_R:
        with ExitStack() as pc:
            RB = pc.enter_context(S("cp_rb", [128, D], F32))
            AB = pc.enter_context(S("cp_ab", [128, D], F32))
            with nc.Block() as blk:
                kb.blk = blk
                phase_entry()
                prev = None
                fold_dbg = (UPTO in (2, 5))
                for i in range(NT):
                    nd = 2 if fold_dbg else 1
                    def ld(e, i=i):
                        kb.dma(e, RB[:], g["R"][i * 128:(i + 1) * 128, :])
                        if fold_dbg:
                            kb.dma(e, AB[:], g["A"][i * 128:(i + 1) * 128, :])
                    ldt = kb.step("sync", ld, deps=[prev], n_dma=nd)
                    dep = ldt
                    if fold_dbg:
                        dep = kb.step("vector", lambda e: e.tensor_add(RB[:], RB[:], AB[:]),
                                      deps=[ldt])
                    prev = kb.step("sync", (lambda e, i=i:
                                            kb.dma(e, g["y_out"][i * 128:(i + 1) * 128, :],
                                                   RB[:])), deps=[dep], n_dma=1)
        with nc.Block() as blk:
            kb.blk = blk
            phase_entry()
            kb.step("sync", lambda e: None, deps=[("sync", kb.tick["sync"])])

    return nc, g, ctx



_BUILD_CACHE = {}

def _tile_rows(v, n=128):
    return np.broadcast_to(np.asarray(v, np.float32).reshape(1, -1), (n, len(np.ravel(v)))).copy()

def kernel(**inputs):
    I = {k: np.ascontiguousarray(np.asarray(v, np.float32)) for k, v in inputs.items()}
    keep = dropout_keep_mask()
    maskscale = np.where(keep, np.float32(1.25), np.float32(0.0)).astype(np.float32)

    nc, g, ctx = _BUILD_CACHE.get("nc", (None, None, None))
    if nc is None:
        nc, g, ctx = build_program()
        _BUILD_CACHE["nc"] = (nc, g, ctx)

    ident = np.eye(128, dtype=ml_dtypes.bfloat16)
    tri = np.zeros((128, 4 * 512), dtype=ml_dtypes.bfloat16)
    for off in range(4):
        k_idx = np.arange(128)[:, None] + off * 128
        q_idx = np.arange(512)[None, :]
        tri[:, off * 512:(off + 1) * 512] = (k_idx <= q_idx).astype(ml_dtypes.bfloat16)
    ltri = (np.arange(128)[:, None] <= np.arange(128)[None, :]).astype(ml_dtypes.bfloat16)

    common = {
        "l0_w1t": I["L0_ffn_w1"].T.copy(), "l0_w2t": I["L0_ffn_w2"].T.copy(),
        "l1_w1t": I["L1_ffn_w1"].T.copy(), "l1_w2t": I["L1_ffn_w2"].T.copy(),
        "l0_ln1w_t": _tile_rows(I["L0_ffn_ln1_w"]), "l0_ln1b_t": _tile_rows(I["L0_ffn_ln1_b"]),
        "l0_ln2w_t": _tile_rows(I["L0_ffn_ln2_w"]), "l0_ln2b_t": _tile_rows(I["L0_ffn_ln2_b"]),
        "l0_b1_t": _tile_rows(I["L0_ffn_b1"]), "l0_b2_t": _tile_rows(I["L0_ffn_b2"]),
        "l1_b1_t": _tile_rows(I["L1_ffn_b1"]), "l1_b2_t": _tile_rows(I["L1_ffn_b2"]),
        "w_dq_t": I["L0_attn_W_DQ"].T.copy(), "w_uq_t": I["L0_attn_W_UQ"].T.copy(),
        "w_dkv_t": I["L0_attn_W_DKV"].T.copy(), "w_ukv_t": I["L0_attn_W_UKV"].T.copy(),
        "w_qr_t": I["L0_attn_W_QR"].T.copy(), "w_kr_t": I["L0_attn_W_KR"].T.copy(),
        "w_o_t": I["L0_attn_W_O"].T.copy(),
        "qnw_c": I["L0_attn_q_norm_w"].reshape(64, 1).copy(),
        "knw_c": I["L0_attn_k_norm_w"].reshape(64, 1).copy(),
        "w_qkv": I["L1_attn_w_qkv"].copy(),
        "bqkv_cols": I["L1_attn_b_qkv"].reshape(24, 128).T.copy(),
        "w_gate": I["L1_attn_w_gate"].copy(),
        "bgate_cols": I["L1_attn_b_gate"].reshape(8, 128).T.copy(),
        "w_proj": I["L1_attn_w_proj"].copy(),
        "b_proj_t": _tile_rows(I["L1_attn_b_proj"]),
        "l1_lng_t": _tile_rows(I["L1_attn_ln_g"]), "l1_lnb_t": _tile_rows(I["L1_attn_ln_b"]),
        "ident": ident, "ones_col_f32": np.ones((128, 1), np.float32),
        "ones_row_f32": np.ones((1, 128), np.float32),
        "ones11": np.ones((1, 1), np.float32),
        "tri_masks": tri, "ltri_sm": ltri,
    }
    in_maps = []
    for c in range(8):
        b = c % 4
        m = dict(common)
        m["x_in"] = I["L0_attn_x"][b]
        m["mask_in"] = maskscale[b]
        in_maps.append(m)
    res = run_bass_kernel_spmd(nc, in_maps, core_ids=list(range(8)))
    out = np.stack([res.results[c]["y_out"] for c in range(4)], axis=0)
    return out.astype(np.float32)


# version 04 (final)
# speedup vs baseline: 1.1975x; 1.0122x over previous
import os
import numpy as np
import ml_dtypes
import concourse.bass as bass
import concourse.mybir as mybir
from concourse.bass_utils import run_bass_kernel_spmd
from contextlib import ExitStack

F32 = mybir.dt.float32
BF16 = mybir.dt.bfloat16
F16 = mybir.dt.float16
F8 = mybir.dt.float8e4
I8 = mybir.dt.int8
AF = mybir.ActivationFunctionType
ALU = mybir.AluOpType
AX = mybir.AxisListType

B, T, D = 4, 4096, 1024
HID = 4096
NT = T // 128
f32 = np.float32
UPTO = int(os.environ.get("KMK_UPTO", "99"))
RMS_EPS = np.float32(1.1920929e-07)


def _rotl(x, d):
    return ((x << np.uint32(d)) | (x >> np.uint32(32 - d))).astype(np.uint32)

def _tf2x32(k1, k2, x0, x1):
    ks0 = np.uint32(k1); ks1 = np.uint32(k2)
    ks2 = np.uint32(ks0 ^ ks1 ^ np.uint32(0x1BD11BDA))
    x0 = (x0 + ks0).astype(np.uint32); x1 = (x1 + ks1).astype(np.uint32)
    rot = [[13, 15, 26, 6], [17, 29, 16, 24]]
    ks = [ks0, ks1, ks2]
    sched = [(rot[0], ks[1], ks[2], 1), (rot[1], ks[2], ks[0], 2),
             (rot[0], ks[0], ks[1], 3), (rot[1], ks[1], ks[2], 4),
             (rot[0], ks[2], ks[0], 5)]
    for rs, ka, kb_, i in sched:
        for r in rs:
            x0 = (x0 + x1).astype(np.uint32); x1 = _rotl(x1, r)
            x1 = (x1 ^ x0).astype(np.uint32)
        x0 = (x0 + ka).astype(np.uint32)
        x1 = (x1 + kb_ + np.uint32(i)).astype(np.uint32)
    return x0, x1

def dropout_keep_mask():
    n = B * T * D
    x64_mode = os.environ.get("BASS_SIM_FP64") == "1"
    out = np.empty(n, dtype=bool)
    step = 1 << 22
    for s in range(0, n, step):
        i = np.arange(s, min(n, s + step), dtype=np.uint64)
        hi = (i >> np.uint64(32)).astype(np.uint32)
        lo = (i & np.uint64(0xFFFFFFFF)).astype(np.uint32)
        o0, o1 = _tf2x32(0, 42, hi, lo)
        if not x64_mode:
            bits = (o0 ^ o1).astype(np.uint32)
            fb = ((bits >> np.uint32(9)) | np.uint32(0x3F800000)).astype(np.uint32)
            u = fb.view(np.float32) - np.float32(1.0)
            out[s:s + len(i)] = u < np.float32(0.8)
        else:
            bits = (o0.astype(np.uint64) << np.uint64(32)) | o1.astype(np.uint64)
            fb = (bits >> np.uint64(12)) | np.uint64(0x3FF0000000000000)
            u = fb.view(np.float64) - 1.0
            out[s:s + len(i)] = u < 0.8
    return out.reshape(B, T, D)


class KB:
    ENGS = ("sync", "vector", "scalar", "tensor", "gpsimd")

    def __init__(self, nc, ctx):
        self.nc = nc
        self.blk = None
        self.sem = {e: ctx.enter_context(nc.semaphore("S" + e)) for e in self.ENGS}
        self.tick = {e: 0 for e in self.ENGS}

    def step(self, eng, emit, deps=(), n_dma=0):
        sem = self.sem
        deps = [d for d in deps if d is not None]
        me = sem[eng]
        self.tick[eng] += 16 * n_dma if n_dma else 1
        ticket = self.tick[eng]

        def closure(e):
            for (s, v) in deps:
                e.wait_ge(sem[s], v)
            r = emit(e)
            if not n_dma:
                if isinstance(r, list):
                    r = r[-1] if r else None
                if r is not None and hasattr(r, "then_inc"):
                    r.then_inc(me, 1)
                else:
                    e.drain().then_inc(me, 1)
        getattr(self.blk, eng)(closure)
        return (eng, ticket)

    def dma(self, e, out, in_):
        e.dma_start(out=out, in_=in_).then_inc(self.sem["sync"], 16)


def build_program():
    nc = bass.Bass("TRN2", target_bir_lowering=False)
    g = {}

    def inp(name, shape, dt=F32):
        g[name] = nc.dram_tensor(name, shape, dt, kind="ExternalInput").ap()

    inp("x_in", [T, D]); inp("mask_in", [T, D])
    inp("l0_w1t", [D, HID]); inp("l0_w2t", [HID, D])
    inp("l1_w1t", [D, HID]); inp("l1_w2t", [HID, D])
    inp("l0_ln1w_t", [128, D]); inp("l0_ln1b_t", [128, D])
    inp("l0_ln2w_t", [128, HID]); inp("l0_ln2b_t", [128, HID])
    inp("l0_b1_t", [128, HID]); inp("l0_b2_t", [128, D])
    inp("l1_b1_t", [128, HID]); inp("l1_b2_t", [128, D])
    inp("w_dq_t", [64, 512]); inp("w_uq_t", [512, 64])
    inp("w_dkv_t", [64, 1024]); inp("w_ukv_t", [1024, 128])
    inp("w_qr_t", [64, 64]); inp("w_kr_t", [64, 64]); inp("w_o_t", [64, 64])
    inp("qnw_c", [64, 1]); inp("knw_c", [64, 1])
    inp("w_qkv", [D, 3 * D]); inp("bqkv_cols", [128, 24])
    inp("w_gate", [D, D]); inp("bgate_cols", [128, 8])
    inp("w_proj", [D, D]); inp("b_proj_t", [128, D])
    inp("l1_lng_t", [128, D]); inp("l1_lnb_t", [128, D])
    inp("ident", [128, 128], BF16)
    inp("ones_col_f32", [128, 1])
    inp("ones_row_f32", [1, 128])
    inp("ones11", [1, 1])
    inp("tri_masks", [128, 4 * 512], BF16)
    inp("ltri_sm", [128, 128], BF16)

    g["y_out"] = nc.dram_tensor("y_out", [T, D], F32, kind="ExternalOutput").ap()
    for n, shape, dt in (("R", [T, D], F32), ("A", [T, D], F32), ("SC2", [T, 1], F32),
                         ("W1Q0", [D, HID], BF16), ("W2Q0", [HID, D], BF16),
                         ("W1Q1", [D, HID], BF16), ("W2Q1", [HID, D], BF16),
                         ("HQ", [T, HID], BF16), ("UT", [D, T], BF16)):
        g[n] = nc.dram_tensor(n, shape, dt, kind="Internal").ap()

    ctx = ExitStack()
    kb = KB(nc, ctx)
    ident = ctx.enter_context(nc.sbuf_tensor("identt", [128, 128], BF16))
    onescf = ctx.enter_context(nc.sbuf_tensor("onescf", [128, 1], F32))
    onesrf = ctx.enter_context(nc.sbuf_tensor("onesrf", [1, 128], F32))
    ones11 = ctx.enter_context(nc.sbuf_tensor("ones11s", [1, 1], F32))
    onescb = ctx.enter_context(nc.sbuf_tensor("onescbs", [128, 1], BF16))
    scal = ctx.enter_context(nc.sbuf_tensor("scal", [1, 8], F32))
    scol = ctx.enter_context(nc.sbuf_tensor("scol", [128, 4], F32))

    PH = [0]
    def S(name, shape, dt):
        return nc.sbuf_tensor(f"{name}_p{PH[0]}", shape, dt)
    def PP(name, shape, dt=F32):
        return nc.psum_tensor(f"{name}_p{PH[0]}", shape, dt)

    with nc.Block() as blk:
        kb.blk = blk
        kb.step("sync", lambda e: [kb.dma(e, ident[:], g["ident"][:]),
                                   kb.dma(e, onescf[:], g["ones_col_f32"][:]),
                                   kb.dma(e, onesrf[:], g["ones_row_f32"][:]),
                                   kb.dma(e, ones11[:], g["ones11"][:])], n_dma=4)
        kb.step("vector", lambda e: e.tensor_copy(onescb[:], onescf[:]),
                deps=[("sync", kb.tick["sync"])])

    with ExitStack() as pc:
        wt = [pc.enter_context(S(f"wqt{i}", [128, HID], F32)) for i in range(2)]
        acc = pc.enter_context(S("wqacc", [128, 2], F32))
        w8 = pc.enter_context(S("wqw8", [128, HID], BF16))
        wi8 = pc.enter_context(S("wqi8", [128, HID], I8))
        wf8 = pc.enter_context(S("wqf8", [128, HID], F8))
        psr = pc.enter_context(PP("wqpsr", [1, 1], F32))
        psc = pc.enter_context(PP("wqpsc", [128, 1], F32))
        with nc.Block() as blk:
            kb.blk = blk
            ldconst = kb.tick["sync"]
            for wi, (src, dst, ntile, wid) in enumerate(
                    (("l0_w1t", "W1Q0", 8, HID), ("l0_w2t", "W2Q0", 32, D))):
                red = None
                for i in range(ntile):
                    buf = wt[i % 2]
                    ldt = kb.step("sync", (lambda e, buf=buf, i=i, src=src, wid=wid:
                                           kb.dma(e, buf[:, 0:wid],
                                                  g[src][i * 128:(i + 1) * 128, :])),
                                  deps=[red], n_dma=1)
                    if i == 0:
                        red = kb.step("vector", (lambda e, buf=buf, wid=wid:
                                                 e.tensor_reduce(acc[:, 0:1], buf[:, 0:wid],
                                                                 AX.X, ALU.add,
                                                                 apply_absolute_value=True)),
                                      deps=[ldt])
                    else:
                        red = kb.step("vector", (lambda e, buf=buf, wid=wid:
                                                 e.tensor_reduce(acc[:, 1:2], buf[:, 0:wid],
                                                                 AX.X, ALU.add,
                                                                 apply_absolute_value=True)),
                                      deps=[ldt])
                        red = kb.step("vector", lambda e: e.tensor_add(acc[:, 0:1],
                                                                       acc[:, 0:1],
                                                                       acc[:, 1:2]),
                                      deps=[red])
                t = kb.step("tensor", lambda e: e.matmul(psr[:], acc[:, 0:1], onescf[:],
                                                         start=True, stop=True), deps=[red])
                t = kb.step("vector", (lambda e, ntile=ntile, wid=wid:
                                       e.tensor_scalar(scal[0:1, 0:1], psr[:],
                                                       float(1.0 / (wid * ntile * 128)),
                                                       None, ALU.mult)), deps=[t])
                t = kb.step("vector", lambda e: e.tensor_scalar_max(
                    scal[0:1, 0:1], scal[0:1, 0:1], float(1e-5)), deps=[t])
                t = kb.step("vector", lambda e: e.reciprocal(scal[0:1, 1:2],
                                                             scal[0:1, 0:1]), deps=[t])
                t = kb.step("tensor", lambda e: e.matmul(psc[:], onesrf[:], scal[0:1, 1:2],
                                                         start=True, stop=True), deps=[t])
                t = kb.step("vector", (lambda e, wi=wi:
                                       e.tensor_copy(scol[:, wi:wi + 1], psc[:])), deps=[t])
                t = kb.step("tensor", lambda e: e.matmul(psc[:], onesrf[:], scal[0:1, 0:1],
                                                         start=True, stop=True), deps=[t])
                t = kb.step("vector", (lambda e, wi=wi:
                                       e.tensor_copy(scol[:, wi + 2:wi + 3], psc[:])), deps=[t])
                qd = t
                for i in range(ntile):
                    buf = wt[i % 2]
                    ldt = kb.step("sync", (lambda e, buf=buf, i=i, src=src, wid=wid:
                                           kb.dma(e, buf[:, 0:wid],
                                                  g[src][i * 128:(i + 1) * 128, :])),
                                  deps=[qd], n_dma=1)
                    qd = kb.step("vector", (lambda e, buf=buf, wid=wid, wi=wi:
                        e.tensor_scalar(buf[:, 0:wid], buf[:, 0:wid], scol[:, wi:wi + 1],
                                        None, ALU.mult)), deps=[ldt])
                    qd = kb.step("vector", (lambda e, buf=buf, wid=wid:
                        e.tensor_scalar(buf[:, 0:wid], buf[:, 0:wid], 12582912.0, None,
                                        ALU.add)), deps=[qd])
                    qd = kb.step("vector", (lambda e, buf=buf, wid=wid:
                        e.tensor_scalar(buf[:, 0:wid], buf[:, 0:wid], -12582912.0, None,
                                        ALU.add)), deps=[qd])
                    qd = kb.step("vector", (lambda e, buf=buf, wid=wid:
                        e.tensor_scalar(w8[:, 0:wid], buf[:, 0:wid], float(1.0), float(-1.0),
                                        ALU.min, ALU.max)), deps=[qd])
                    kb.step("sync", (lambda e, i=i, dst=dst, wid=wid:
                                     kb.dma(e, g[dst][i * 128:(i + 1) * 128, :],
                                            w8[:, 0:wid])), deps=[qd], n_dma=1)
            for src, dst, ntile, wid in (("l1_w1t", "W1Q1", 8, HID),
                                         ("l1_w2t", "W2Q1", 32, D)):
                qd = None
                for i in range(ntile):
                    buf = wt[i % 2]
                    ldt = kb.step("sync", (lambda e, buf=buf, i=i, src=src, wid=wid:
                                           kb.dma(e, buf[:, 0:wid],
                                                  g[src][i * 128:(i + 1) * 128, :])),
                                  deps=[qd], n_dma=1)
                    qd = kb.step("vector", (lambda e, buf=buf, wid=wid:
                        e.tensor_copy(wf8[:, 0:wid], buf[:, 0:wid])), deps=[ldt])
                    qd = kb.step("vector", (lambda e, buf=buf, wid=wid:
                        e.tensor_copy(w8[:, 0:wid], wf8[:, 0:wid])), deps=[qd])
                    kb.step("sync", (lambda e, i=i, dst=dst, wid=wid:
                                     kb.dma(e, g[dst][i * 128:(i + 1) * 128, :],
                                            w8[:, 0:wid])), deps=[qd], n_dma=1)


    def phase_entry():
        deps_all = [(en, kb.tick[en]) for en in KB.ENGS]
        for en in KB.ENGS:
            kb.step(en, lambda e: None, deps=deps_all)

    def emit_ln(deps, CL, src, dst, scratch, d, eps=np.float32(1e-5)):
        t = kb.step("vector", lambda e: e.tensor_reduce(CL[:, 0:1], src, AX.X, ALU.add),
                    deps=deps)
        t = kb.step("vector", lambda e: e.scalar_tensor_tensor(scratch, src, float(1.0), src,
                                                               ALU.mult, ALU.mult,
                                                               accum_out=CL[:, 1:2]),
                    deps=[t])
        t = kb.step("vector", lambda e: [
            e.tensor_scalar(CL[:, 2:3], CL[:, 0:1], float(1.0 / d), None, ALU.mult),
            e.tensor_scalar(CL[:, 5:6], CL[:, 0:1], float(-1.0 / d), None, ALU.mult),
            e.tensor_scalar(CL[:, 4:5], CL[:, 1:2], float(1.0 / d), None, ALU.mult)], deps=[t])
        t = kb.step("vector", lambda e: e.tensor_tensor(CL[:, 3:4], CL[:, 2:3],
                                                        CL[:, 2:3], ALU.mult), deps=[t])
        t = kb.step("vector", lambda e: e.tensor_sub(CL[:, 4:5], CL[:, 4:5], CL[:, 3:4]),
                    deps=[t])
        t = kb.step("scalar", lambda e: e.activation(CL[:, 6:7], CL[:, 4:5], AF.Sqrt,
                                                     bias=eps, scale=f32(1.0)), deps=[t])
        t = kb.step("vector", lambda e: e.reciprocal(CL[:, 6:7], CL[:, 6:7]), deps=[t])
        t = kb.step("vector", lambda e: e.tensor_scalar(dst, src, CL[:, 5:6], CL[:, 6:7],
                                                        ALU.add, ALU.mult), deps=[t])
        return t

    def ffn_a(kind, first, fold):
        PH[0] += 1
        with ExitStack() as pc:
            RB = pc.enter_context(S("fa_rb", [128, D], F32))
            AB = pc.enter_context(S("fa_ab", [128, D], F32)) if fold else None
            U = pc.enter_context(S("fa_u", [128, D], F32))
            H = pc.enter_context(S("fa_h", [128, HID], F32))
            QB = pc.enter_context(S("fa_qb", [128, HID], BF16))
            QT = pc.enter_context(S("fa_qt", [128, D], BF16))
            TI8 = pc.enter_context(S("fa_i8", [128, HID], I8))
            FQ8 = pc.enter_context(S("fa_f8", [128, HID], F8))
            H16 = pc.enter_context(S("fa_h16", [128, HID], F16)) if kind == "l1" else None
            CL = pc.enter_context(S("fa_cl", [128, 8], F32))
            CF = pc.enter_context(S("fa_cf", [128, HID], F32))
            B1 = pc.enter_context(S("fa_b1", [128, HID], F32))
            W1 = [pc.enter_context(S(f"fa_w1{i}", [128, HID], BF16))
                  for i in range(8)]
            if kind == "l0":
                LN1W = pc.enter_context(S("fa_l1w", [128, D], F32))
                LN1B = pc.enter_context(S("fa_l1b", [128, D], F32))
                LN2W = pc.enter_context(S("fa_l2w", [128, HID], F32))
                LN2B = pc.enter_context(S("fa_l2b", [128, HID], F32))
            PSA = pc.enter_context(PP("fa_psa", [128, 512], F32))
            PSB = pc.enter_context(PP("fa_psb", [128, 512], F32))
            PST = pc.enter_context(PP("fa_pst", [128, 512], BF16))
            PSTb = pc.enter_context(PP("fa_pstb", [128, 512], BF16))
            with nc.Block() as blk:
                kb.blk = blk
                phase_entry()
                w1src = g["W1Q0"] if kind == "l0" else g["W1Q1"]
                ldw = kb.step("sync", lambda e: [
                    *[kb.dma(e, W1[i][:], w1src[i * 128:(i + 1) * 128, :]) for i in range(8)],
                    kb.dma(e, B1[:], g["l0_b1_t" if kind == "l0" else "l1_b1_t"][:])],
                    n_dma=9)
                cdep = ldw
                if kind == "l0":
                    cdep = kb.step("sync", lambda e: [
                        kb.dma(e, LN1W[:], g["l0_ln1w_t"][:]),
                        kb.dma(e, LN1B[:], g["l0_ln1b_t"][:]),
                        kb.dma(e, LN2W[:], g["l0_ln2w_t"][:]),
                        kb.dma(e, LN2B[:], g["l0_ln2b_t"][:])], deps=[cdep], n_dma=4)
                else:
                    cdep = kb.step("vector", lambda e: e.tensor_copy(FQ8[:], B1[:]),
                                   deps=[cdep])
                    cdep = kb.step("vector", lambda e: e.tensor_copy(B1[:], FQ8[:]),
                                   deps=[cdep])

                prev = cdep
                for i in range(NT):
                    src = g["x_in"] if first else g["R"]
                    def ld_emit(e, i=i):
                        kb.dma(e, RB[:], src[i * 128:(i + 1) * 128, :])
                        if fold:
                            kb.dma(e, AB[:], g["A"][i * 128:(i + 1) * 128, :])
                    ldt = kb.step("sync", ld_emit, deps=[prev], n_dma=2 if fold else 1)
                    dep = ldt
                    if fold:
                        dep = kb.step("vector", lambda e: e.tensor_add(RB[:], RB[:], AB[:]),
                                      deps=[ldt])
                    t = emit_ln([dep], CL, RB[:], U[:], H[:, 0:D], D)
                    if kind == "l0":
                        t = emit_ln([t], CL, U[:], U[:], H[:, 0:D], D)
                        t = kb.step("vector", lambda e: e.tensor_mul(U[:], U[:], LN1W[:]),
                                    deps=[t])
                        t = kb.step("vector", lambda e: e.tensor_add(U[:], U[:], LN1B[:]),
                                    deps=[t])
                        t = kb.step("vector", lambda e: e.tensor_reduce(
                            CL[:, 0:1], U[:], AX.X, ALU.max, apply_absolute_value=True),
                                    deps=[t])
                        t = kb.step("vector", lambda e: e.tensor_scalar_max(
                            CL[:, 0:1], CL[:, 0:1], float(1e-5)), deps=[t])
                        t = kb.step("vector", lambda e: e.reciprocal(CL[:, 1:2], CL[:, 0:1]),
                                    deps=[t])
                        t = kb.step("vector", lambda e: e.tensor_scalar(
                            CL[:, 1:2], CL[:, 1:2], 127.0, None, ALU.mult), deps=[t])
                        t = kb.step("vector", lambda e: [
                            e.tensor_scalar(H[:, 0:D], U[:], CL[:, 1:2], None, ALU.mult),
                            e.tensor_scalar(CL[:, 7:8], CL[:, 0:1], scol[:, 2:3],
                                            float(1.0 / 127.0), ALU.mult, ALU.mult)], deps=[t])
                        t = kb.step("vector", lambda e: e.tensor_scalar(
                            H[:, 0:D], H[:, 0:D], 12582912.0, None, ALU.add), deps=[t])
                        t = kb.step("vector", lambda e: e.tensor_scalar(
                            QB[:, 0:D], H[:, 0:D], -12582912.0, None, ALU.add), deps=[t])
                    else:
                        t = kb.step("vector", lambda e: e.tensor_copy(FQ8[:, 0:D], U[:]),
                                    deps=[t])
                        t = kb.step("vector", lambda e: e.tensor_copy(QB[:, 0:D], FQ8[:, 0:D]),
                                    deps=[t])
                    evs = [t, t]
                    ev = t
                    for kb4 in range(2):
                        pst = PST if kb4 % 2 == 0 else PSTb
                        def trb(e, kb4=kb4, pst=pst):
                            r = None
                            for j in range(4):
                                k = kb4 * 4 + j
                                r = e.transpose(pst[:, j * 128:(j + 1) * 128],
                                                QB[:, k * 128:(k + 1) * 128], ident[:])
                            return r
                        tk = kb.step("tensor", trb, deps=[t, evs[kb4 % 2]])
                        ev = kb.step("vector", (lambda e, kb4=kb4, pst=pst:
                                                e.tensor_copy(QT[:, kb4 * 512:(kb4 + 1) * 512],
                                                              pst[:])), deps=[tk])
                        evs[kb4 % 2] = ev
                    hdone = []
                    for n in range(8):
                        ps = PSA if n % 2 == 0 else PSB
                        mdeps = [ev] if n < 2 else [ev, hdone[n - 2]]
                        def mm1(e, n=n, ps=ps):
                            for k in range(8):
                                e.matmul(ps[:], QT[:, k * 128:(k + 1) * 128],
                                         W1[k][:, n * 512:(n + 1) * 512],
                                         start=(k == 0), stop=(k == 7))
                        mt = kb.step("tensor", mm1, deps=mdeps)
                        if kind == "l0":
                            et = kb.step("vector", (lambda e, n=n, ps=ps:
                                                    e.scalar_tensor_tensor(
                                                        H[:, n * 512:(n + 1) * 512], ps[:],
                                                        CL[:, 7:8],
                                                        B1[:, n * 512:(n + 1) * 512],
                                                        ALU.mult, ALU.add)), deps=[mt])
                            et = kb.step("scalar", (lambda e, n=n:
                                                    e.activation(H[:, n * 512:(n + 1) * 512],
                                                                 H[:, n * 512:(n + 1) * 512],
                                                                 AF.Gelu, bias=f32(0.0),
                                                                 scale=f32(1.0))), deps=[et])
                        else:
                            et = kb.step("vector", (lambda e, n=n, ps=ps:
                                e.tensor_add(H[:, n * 512:(n + 1) * 512], ps[:],
                                             B1[:, n * 512:(n + 1) * 512])), deps=[mt])
                            et = kb.step("vector", (lambda e, n=n:
                                e.tensor_copy(H16[:, n * 512:(n + 1) * 512],
                                              H[:, n * 512:(n + 1) * 512])), deps=[et])
                            et = kb.step("vector", (lambda e, n=n:
                                e.tensor_scalar_max(H[:, n * 512:(n + 1) * 512],
                                                    H16[:, n * 512:(n + 1) * 512],
                                                    float(0.0))), deps=[et])
                        hdone.append(et)
                    t = kb.step("vector", lambda e: None, deps=hdone[-2:])
                    if kind == "l0":
                        t = emit_ln([t], CL, H[:], H[:], CF[:], HID)
                        t = kb.step("vector", lambda e: e.tensor_mul(H[:], H[:], LN2W[:]),
                                    deps=[t])
                        t = kb.step("vector", lambda e: e.tensor_add(H[:], H[:], LN2B[:]),
                                    deps=[t])
                        t = kb.step("vector", lambda e: e.tensor_reduce(
                            CL[:, 0:1], H[:], AX.X, ALU.max, apply_absolute_value=True),
                                    deps=[t])
                        t = kb.step("vector", lambda e: e.tensor_scalar_max(
                            CL[:, 0:1], CL[:, 0:1], float(1e-5)), deps=[t])
                        t = kb.step("vector", lambda e: e.reciprocal(CL[:, 1:2], CL[:, 0:1]),
                                    deps=[t])
                        t = kb.step("vector", lambda e: e.tensor_scalar(
                            CL[:, 1:2], CL[:, 1:2], 127.0, None, ALU.mult), deps=[t])
                        t = kb.step("vector", lambda e: [
                            e.tensor_scalar(H[:], H[:], CL[:, 1:2], None, ALU.mult),
                            e.tensor_scalar(CL[:, 7:8], CL[:, 0:1], scol[:, 3:4],
                                            float(1.0 / 127.0), ALU.mult, ALU.mult)], deps=[t])
                        t = kb.step("vector", lambda e: e.tensor_scalar(
                            H[:], H[:], 12582912.0, None, ALU.add), deps=[t])
                        t = kb.step("vector", lambda e: e.tensor_scalar(
                            QB[:], H[:], -12582912.0, None, ALU.add), deps=[t])
                        kb.step("sync", (lambda e, i=i:
                                         kb.dma(e, g["SC2"][i * 128:(i + 1) * 128, :],
                                                CL[:, 7:8])), deps=[t], n_dma=1)
                    else:
                        t = kb.step("vector", lambda e: e.tensor_copy(FQ8[:], H[:]),
                                    deps=[t])
                        t = kb.step("vector", lambda e: e.tensor_copy(QB[:], FQ8[:]),
                                    deps=[t])
                    prev = kb.step("sync", (lambda e, i=i:
                                            kb.dma(e, g["HQ"][i * 128:(i + 1) * 128, :],
                                                   QB[:])), deps=[t], n_dma=1)

    def ffn_b(kind, first, fold, out_final):
        PH[0] += 1
        with ExitStack() as pc:
            RB = pc.enter_context(S("fb_rb", [128, D], F32))
            AB = pc.enter_context(S("fb_ab", [128, D], F32)) if fold else None
            MBm = pc.enter_context(S("fb_mb", [128, D], F32)) if kind == "l1" else None
            U = pc.enter_context(S("fb_u", [128, D], F32))
            H16 = pc.enter_context(S("fb_h16", [128, D], F16)) if kind == "l1" else None
            QB = pc.enter_context(S("fb_qb", [128, HID], BF16))
            QT = pc.enter_context(S("fb_qt", [128, HID], BF16))
            CL = pc.enter_context(S("fb_cl", [128, 2], F32))
            B2 = pc.enter_context(S("fb_b2", [128, D], F32))
            FQ8 = pc.enter_context(S("fb_f8", [128, D], F8)) if kind == "l1" else None
            W2 = [pc.enter_context(S(f"fb_w2{i}", [128, D], BF16))
                  for i in range(32)]
            PSA = pc.enter_context(PP("fb_psa", [128, 512], F32))
            PSB = pc.enter_context(PP("fb_psb", [128, 512], F32))
            PST = pc.enter_context(PP("fb_pst", [128, 512], BF16))
            PSTb = pc.enter_context(PP("fb_pstb", [128, 512], BF16))
            with nc.Block() as blk:
                kb.blk = blk
                phase_entry()
                w2src = g["W2Q0"] if kind == "l0" else g["W2Q1"]
                ldw = kb.step("sync", lambda e: [
                    *[kb.dma(e, W2[i][:], w2src[i * 128:(i + 1) * 128, :]) for i in range(32)],
                    kb.dma(e, B2[:], g["l0_b2_t" if kind == "l0" else "l1_b2_t"][:])],
                    n_dma=33)
                cdep = ldw
                if kind == "l1":
                    cdep = kb.step("vector", lambda e: e.tensor_copy(FQ8[:], B2[:]),
                                   deps=[ldw])
                    cdep = kb.step("vector", lambda e: e.tensor_copy(B2[:], FQ8[:]),
                                   deps=[cdep])
                prev = cdep
                for i in range(NT):
                    src = g["x_in"] if first else g["R"]
                    def ld_emit(e, i=i):
                        kb.dma(e, RB[:], src[i * 128:(i + 1) * 128, :])
                        kb.dma(e, QB[:], g["HQ"][i * 128:(i + 1) * 128, :])
                        if fold:
                            kb.dma(e, AB[:], g["A"][i * 128:(i + 1) * 128, :])
                        if kind == "l1":
                            kb.dma(e, MBm[:], g["mask_in"][i * 128:(i + 1) * 128, :])
                        if kind == "l0":
                            kb.dma(e, CL[:, 0:1], g["SC2"][i * 128:(i + 1) * 128, :])
                    nd = 2 + (1 if fold else 0) + 1
                    ldt = kb.step("sync", ld_emit, deps=[prev], n_dma=nd)
                    dep = ldt
                    if fold:
                        dep = kb.step("vector", lambda e: e.tensor_add(RB[:], RB[:], AB[:]),
                                      deps=[ldt])
                    evs = [dep, dep]
                    ev = dep
                    for kb4 in range(8):
                        pst = PST if kb4 % 2 == 0 else PSTb
                        def trb(e, kb4=kb4, pst=pst):
                            r = None
                            for j in range(4):
                                k = kb4 * 4 + j
                                r = e.transpose(pst[:, j * 128:(j + 1) * 128],
                                                QB[:, k * 128:(k + 1) * 128], ident[:])
                            return r
                        tk = kb.step("tensor", trb, deps=[dep, evs[kb4 % 2]])
                        ev = kb.step("vector", (lambda e, kb4=kb4, pst=pst:
                                                e.tensor_copy(QT[:, kb4 * 512:(kb4 + 1) * 512],
                                                              pst[:])), deps=[tk])
                        evs[kb4 % 2] = ev
                    edone = []
                    for n in range(2):
                        ps = PSA if n % 2 == 0 else PSB
                        def mm2(e, n=n, ps=ps):
                            for k in range(32):
                                e.matmul(ps[:], QT[:, k * 128:(k + 1) * 128],
                                         W2[k][:, n * 512:(n + 1) * 512],
                                         start=(k == 0), stop=(k == 31))
                        mt = kb.step("tensor", mm2, deps=[ev])
                        sl = slice(n * 512, (n + 1) * 512)
                        if kind == "l0":
                            et = kb.step("vector", (lambda e, ps=ps, sl=sl:
                                e.scalar_tensor_tensor(U[:, sl], ps[:], CL[:, 0:1],
                                                       B2[:, sl], ALU.mult, ALU.add)),
                                         deps=[mt])
                            et = kb.step("vector", (lambda e, sl=sl:
                                e.scalar_tensor_tensor(RB[:, sl], U[:, sl], float(0.5),
                                                       RB[:, sl], ALU.mult, ALU.add)),
                                         deps=[et])
                        else:
                            et = kb.step("vector", (lambda e, ps=ps, sl=sl:
                                e.tensor_add(U[:, sl], ps[:], B2[:, sl])), deps=[mt])
                            et = kb.step("vector", (lambda e, sl=sl:
                                e.tensor_copy(H16[:, sl], U[:, sl])), deps=[et])
                            et = kb.step("vector", (lambda e, sl=sl:
                                e.tensor_tensor(U[:, sl], H16[:, sl], MBm[:, sl], ALU.mult)),
                                         deps=[et])
                            et = kb.step("vector", (lambda e, sl=sl:
                                e.scalar_tensor_tensor(RB[:, sl], U[:, sl], float(0.5),
                                                       RB[:, sl], ALU.mult, ALU.add)),
                                         deps=[et])
                        edone.append(et)
                    dst = g["y_out"] if out_final else g["R"]
                    prev = kb.step("sync", (lambda e, i=i, dst=dst:
                                            kb.dma(e, dst[i * 128:(i + 1) * 128, :], RB[:])),
                                   deps=edone, n_dma=1)

    def ut_phase():
        PH[0] += 1
        with ExitStack() as pc:
            RB = pc.enter_context(S("ut_rb", [128, D], F32))
            U = pc.enter_context(S("ut_u", [128, D], F32))
            UB = pc.enter_context(S("ut_ub", [128, D], BF16))
            UTT = pc.enter_context(S("ut_t", [128, D], BF16))
            CL = pc.enter_context(S("ut_cl", [128, 8], F32))
            PST = pc.enter_context(PP("ut_pst", [128, 512], BF16))
            with nc.Block() as blk:
                kb.blk = blk
                phase_entry()
                prev = None
                for i in range(NT):
                    ldt = kb.step("sync", (lambda e, i=i:
                                           kb.dma(e, RB[:], g["R"][i * 128:(i + 1) * 128, :])),
                                  deps=[prev], n_dma=1)
                    t = emit_ln([ldt], CL, RB[:], U[:], U[:], D)
                    t = kb.step("vector", lambda e: e.tensor_copy(UB[:], U[:]), deps=[t])
                    ev = t
                    for kb4 in range(2):
                        def trb(e, kb4=kb4):
                            r = None
                            for j in range(4):
                                k = kb4 * 4 + j
                                r = e.transpose(PST[:, j * 128:(j + 1) * 128],
                                                UB[:, k * 128:(k + 1) * 128], ident[:])
                            return r
                        tk = kb.step("tensor", trb, deps=[ev])
                        ev = kb.step("vector", (lambda e, kb4=kb4:
                                                e.tensor_copy(UTT[:, kb4 * 512:(kb4 + 1) * 512],
                                                              PST[:])), deps=[tk])
                    def st_emit(e, i=i):
                        for k in range(8):
                            kb.dma(e, g["UT"][k * 128:(k + 1) * 128, i * 128:(i + 1) * 128],
                                   UTT[:, k * 128:(k + 1) * 128])
                    prev = kb.step("sync", st_emit, deps=[ev], n_dma=8)

    def l0_attn_phase():
        PH[0] += 1
        with ExitStack() as pc:
            UH = pc.enter_context(S("la_uh", [64, T], BF16))
            WDQ = pc.enter_context(S("la_wdq", [64, 512], BF16))
            WUQ = pc.enter_context(S("la_wuq", [128, 256], BF16))
            WDKV = pc.enter_context(S("la_wdkv", [64, 1024], BF16))
            WUKV = pc.enter_context(S("la_wukv", [128, 1024], BF16))
            WQR = pc.enter_context(S("la_wqr", [64, 64], BF16))
            WKR = pc.enter_context(S("la_wkr", [64, 64], BF16))
            WO = pc.enter_context(S("la_wo", [64, 64], BF16))
            QNW = pc.enter_context(S("la_qnw", [64, 1], F32))
            KNW = pc.enter_context(S("la_knw", [64, 1], F32))
            WF = pc.enter_context(S("la_wf", [128, 1024], F32))
            CQT = [pc.enter_context(S(f"la_cqt{l}", [128, T], BF16))
                   for l in range(4)]
            CKV = pc.enter_context(S("la_ckv", [128, 4096], BF16))
            QTt = pc.enter_context(S("la_qt", [128, T], BF16))
            KTt = pc.enter_context(S("la_kt", [128, T], BF16))
            VA = pc.enter_context(S("la_va", [128, 32 * 65], BF16))
            SQs = pc.enter_context(S("la_sq", [128, 512], F32))
            U64 = pc.enter_context(S("la_u64", [64, 512], F32))
            VTs = pc.enter_context(S("la_vt", [64, 512], BF16))
            RST = pc.enter_context(S("la_rst", [1, 512], F32))
            ROW = pc.enter_context(S("la_row", [1, 512], F32))
            MX = pc.enter_context(S("la_mx", [1, 4], F32))
            BCOL = pc.enter_context(S("la_bcol", [128, 1], F32))
            ZC = pc.enter_context(S("la_zc", [128, 1], F32))
            ES = [pc.enter_context(S(f"la_es{i}", [128, 512], BF16)) for i in range(2)]
            CTXs = pc.enter_context(S("la_ctx", [64, 512], BF16))
            ZR = pc.enter_context(S("la_zr", [1, 512], F32))
            OH = pc.enter_context(S("la_oh", [128, 64], F32))
            TRI = pc.enter_context(S("la_tri", [128, 2048], BF16))
            PSA = pc.enter_context(PP("la_psa", [128, 512], F32))
            PSB = pc.enter_context(PP("la_psb", [64, 512], F32))
            PSS = pc.enter_context(PP("la_pss", [128, 512], F32))
            PSS2 = pc.enter_context(PP("la_pss2", [128, 512], F32))
            PS65 = pc.enter_context(PP("la_ps65", [65, 512], F32))
            PSO = pc.enter_context(PP("la_pso", [128, 64], F32))
            PST = pc.enter_context(PP("la_pst", [128, 128], BF16))
            with nc.Block() as blk:
                kb.blk = blk
                phase_entry()
                wdep = None
                for src, dst, p, w in (("w_dq_t", WDQ, 64, 512), ("w_uq_t", WUQ, 128, None),
                                       ("w_dkv_t", WDKV, 64, 1024), ("w_ukv_t", WUKV, 128, None),
                                       ("w_qr_t", WQR, 64, 64), ("w_kr_t", WKR, 64, 64),
                                       ("w_o_t", WO, 64, 64)):
                    if src == "w_uq_t":
                        def ld_uq(e, src=src):
                            for c in range(4):
                                kb.dma(e, WF[:, c * 64:(c + 1) * 64],
                                       g[src][c * 128:(c + 1) * 128, :])
                        lt = kb.step("sync", ld_uq, deps=[wdep], n_dma=4)
                        wdep = kb.step("vector", (lambda e, dst=dst:
                                                  e.tensor_copy(dst[:, 0:256], WF[:, 0:256])),
                                       deps=[lt])
                    elif src == "w_ukv_t":
                        def ld_ukv(e, src=src):
                            for c in range(8):
                                kb.dma(e, WF[:, c * 128:(c + 1) * 128],
                                       g[src][c * 128:(c + 1) * 128, :])
                        lt = kb.step("sync", ld_ukv, deps=[wdep], n_dma=8)
                        wdep = kb.step("vector", (lambda e, dst=dst:
                                                  e.tensor_copy(dst[:, 0:1024], WF[:, 0:1024])),
                                       deps=[lt])
                    else:
                        lt = kb.step("sync", (lambda e, src=src, p=p, w=w:
                                              kb.dma(e, WF[0:p, 0:w], g[src][:])),
                                     deps=[wdep], n_dma=1)
                        wdep = kb.step("vector", (lambda e, dst=dst, p=p, w=w:
                                                  e.tensor_copy(dst[0:p, 0:w], WF[0:p, 0:w])),
                                       deps=[lt])
                lt = kb.step("sync", lambda e: [kb.dma(e, QNW[:], g["qnw_c"][:]),
                                                kb.dma(e, KNW[:], g["knw_c"][:]),
                                                kb.dma(e, TRI[:], g["tri_masks"][:])],
                             deps=[wdep], n_dma=3)
                vset = kb.step("vector", lambda e: [
                    e.memset(VA[:, 64 + 65 * c:65 + 65 * c], 1.0) for c in range(32)],
                    deps=[lt])
                hprev = vset
                for h in range(16):
                    ldu = kb.step("sync", (lambda e, h=h:
                                           kb.dma(e, UH[:], g["UT"][h * 64:(h + 1) * 64, :])),
                                  deps=[hprev], n_dma=1)
                    tprev = ldu
                    for tc in range(8):
                        tsl = slice(tc * 512, (tc + 1) * 512)
                        for l in range(4):
                            mt = kb.step("tensor", (lambda e, l=l, tsl=tsl:
                                                    e.matmul(PSA[:], WDQ[:, l * 128:(l + 1) * 128],
                                                             UH[:, tsl], start=True, stop=True)),
                                         deps=[tprev])
                            tprev = kb.step("vector", (lambda e, l=l, tsl=tsl:
                                                       e.tensor_copy(CQT[l][:, tsl], PSA[:])),
                                            deps=[mt])
                        def mmqc(e, tsl=tsl):
                            for l in range(4):
                                e.matmul(PSB[:], WUQ[:, l * 64:(l + 1) * 64],
                                         CQT[l][:, tsl], start=(l == 0), stop=(l == 3))
                        mt = kb.step("tensor", mmqc, deps=[tprev])
                        t = kb.step("vector", lambda e: e.tensor_copy(U64[:], PSB[:]),
                                    deps=[mt])
                        t = kb.step("vector", lambda e: e.tensor_tensor(
                            SQs[0:64, :], U64[:], U64[:], ALU.mult), deps=[t])
                        t = kb.step("tensor", lambda e: e.matmul(PSS[0:1, :], onescf[0:64, :],
                                                                 SQs[0:64, :], start=True,
                                                                 stop=True), deps=[t])
                        t = kb.step("scalar", lambda e: e.activation(
                            RST[:], PSS[0:1, :], AF.Sqrt, bias=RMS_EPS,
                            scale=f32(1.0 / 64.0)), deps=[t])
                        t = kb.step("vector", lambda e: e.reciprocal(RST[:], RST[:]), deps=[t])
                        t = kb.step("tensor", lambda e: e.matmul(PSS[0:64, :], onesrf[0:1, 0:64],
                                                                 RST[:], start=True, stop=True),
                                    deps=[t])
                        t = kb.step("vector", lambda e: e.tensor_tensor(
                            U64[:], U64[:], PSS[0:64, :], ALU.mult), deps=[t])
                        t = kb.step("vector", (lambda e, tsl=tsl:
                            e.tensor_scalar(QTt[0:64, tsl], U64[:], QNW[:], float(0.125),
                                            ALU.mult, ALU.mult)), deps=[t])
                        mt = kb.step("tensor", (lambda e, tsl=tsl:
                                                e.matmul(PSB[:], WQR[:], QTt[0:64, tsl],
                                                         start=True, stop=True)), deps=[t])
                        t = kb.step("vector", (lambda e, tsl=tsl:
                                               e.tensor_copy(QTt[64:128, tsl], PSB[:])),
                                    deps=[mt])
                        for l in range(8):
                            mt = kb.step("tensor", (lambda e, l=l, tsl=tsl:
                                                    e.matmul(PSA[:], WDKV[:, l * 128:(l + 1) * 128],
                                                             UH[:, tsl], start=True, stop=True)),
                                         deps=[t])
                            t = kb.step("vector", (lambda e, l=l:
                                                   e.tensor_copy(CKV[:, l * 512:(l + 1) * 512],
                                                                 PSA[:])), deps=[mt])
                        def mmkv(e):
                            for l in range(8):
                                e.matmul(PSA[:], WUKV[:, l * 128:(l + 1) * 128],
                                         CKV[:, l * 512:(l + 1) * 512],
                                         start=(l == 0), stop=(l == 7))
                        mt = kb.step("tensor", mmkv, deps=[t])
                        t = kb.step("vector", lambda e: e.tensor_copy(U64[:], PSA[0:64, :]),
                                    deps=[mt])
                        t = kb.step("vector", lambda e: e.tensor_tensor(
                            SQs[0:64, :], U64[:], U64[:], ALU.mult), deps=[t])
                        t = kb.step("tensor", lambda e: e.matmul(PSS[0:1, :], onescf[0:64, :],
                                                                 SQs[0:64, :], start=True,
                                                                 stop=True), deps=[t])
                        t = kb.step("scalar", lambda e: e.activation(
                            RST[:], PSS[0:1, :], AF.Sqrt, bias=RMS_EPS,
                            scale=f32(1.0 / 64.0)), deps=[t])
                        t = kb.step("vector", lambda e: e.reciprocal(RST[:], RST[:]), deps=[t])
                        t = kb.step("tensor", lambda e: e.matmul(PSS[0:64, :], onesrf[0:1, 0:64],
                                                                 RST[:], start=True, stop=True),
                                    deps=[t])
                        t = kb.step("vector", lambda e: [
                            e.tensor_tensor(U64[:], U64[:], PSS[0:64, :], ALU.mult),
                            e.tensor_copy(VTs[:], PSA[64:128, :])], deps=[t])
                        t = kb.step("vector", (lambda e, tsl=tsl:
                            e.tensor_scalar(KTt[0:64, tsl], U64[:], KNW[:], None, ALU.mult)),
                                    deps=[t])
                        mt = kb.step("tensor", (lambda e, tsl=tsl:
                                                e.matmul(PSB[:], WKR[:], UH[:, tsl],
                                                         start=True, stop=True)), deps=[t])
                        t = kb.step("vector", (lambda e, tsl=tsl:
                                               e.tensor_copy(KTt[64:128, tsl], PSB[:])),
                                    deps=[mt])
                        for j in range(4):
                            tk = kb.step("tensor", (lambda e, j=j:
                                                    e.transpose(PST[0:128, 0:64],
                                                                VTs[:, j * 128:(j + 1) * 128],
                                                                ident[0:64, 0:64])), deps=[t])
                            t = kb.step("vector", (lambda e, j=j, tc=tc:
                                                   e.tensor_copy(
                                                       VA[:, (tc * 4 + j) * 65:(tc * 4 + j) * 65 + 64],
                                                       PST[0:128, 0:64])), deps=[tk])
                        tprev = t
                    t = kb.step("vector", lambda e: e.memset(MX[:, 0:2], 0.0), deps=[tprev])
                    for which, XT, mxi in (("q", QTt, 0), ("k", KTt, 1)):
                        for tc in range(8):
                            tsl = slice(tc * 512, (tc + 1) * 512)
                            t2 = kb.step("scalar", (lambda e, XT=XT, tsl=tsl:
                                                    e.activation(SQs[:], XT[:, tsl], AF.Square,
                                                                 bias=f32(0.0), scale=f32(1.0))),
                                         deps=[t])
                            t2 = kb.step("tensor", lambda e: e.matmul(PSS[0:1, :], onescf[:],
                                                                      SQs[:], start=True,
                                                                      stop=True), deps=[t2])
                            t2 = kb.step("vector", lambda e: e.tensor_copy(
                                ROW[:], PSS[0:1, :]), deps=[t2])
                            t2 = kb.step("vector", lambda e: e.tensor_reduce(
                                MX[:, 2:3], ROW[:], AX.X, ALU.max), deps=[t2])
                            t = kb.step("vector", (lambda e, mxi=mxi:
                                e.tensor_max(MX[:, mxi:mxi + 1], MX[:, mxi:mxi + 1],
                                             MX[:, 2:3])), deps=[t2])
                    t = kb.step("vector", lambda e: e.tensor_tensor(MX[:, 2:3], MX[:, 0:1],
                                                                    MX[:, 1:2], ALU.mult),
                                deps=[t])
                    t = kb.step("scalar", lambda e: e.activation(MX[:, 3:4], MX[:, 2:3],
                                                                 AF.Sqrt, bias=f32(0.0),
                                                                 scale=f32(1.0)), deps=[t])
                    t = kb.step("vector", lambda e: e.tensor_scalar(MX[:, 3:4], MX[:, 3:4],
                                                                    float(-1.0), None, ALU.mult),
                                deps=[t])
                    t = kb.step("tensor", lambda e: e.matmul(PSO[:, 0:1], onesrf[:], MX[:, 3:4],
                                                             start=True, stop=True), deps=[t])
                    t = kb.step("vector", lambda e: e.tensor_copy(BCOL[:], PSO[:, 0:1]),
                                deps=[t])
                    hq = t
                    for qt in range(8):
                        qsl = slice(qt * 512, (qt + 1) * 512)
                        nk = 4 * qt + 4
                        ctx_t = [None, None]
                        exp_t = [None, None]
                        edep = hq
                        for kt in range(nk):
                            p = kt % 2
                            ps = PSS if p == 0 else PSS2
                            es = ES[p]
                            mt = kb.step("tensor", (lambda e, kt=kt, qsl=qsl, ps=ps:
                                                    e.matmul(ps[:], KTt[:, kt * 128:(kt + 1) * 128],
                                                             QTt[:, qsl], start=True, stop=True)),
                                         deps=[hq, exp_t[p]])
                            et = kb.step("scalar", (lambda e, ps=ps, es=es: e.activation(
                                es[:], ps[:], AF.Exp, bias=BCOL[:], scale=f32(1.0))),
                                deps=[mt, ctx_t[p]])
                            if kt >= 4 * qt:
                                off = kt - 4 * qt
                                et = kb.step("vector", (lambda e, off=off, es=es:
                                                        e.tensor_tensor(es[:], es[:],
                                                                        TRI[:, off * 512:(off + 1) * 512],
                                                                        ALU.mult)), deps=[et])
                            exp_t[p] = et
                            edep = kb.step("tensor", (lambda e, kt=kt, nk=nk, es=es:
                                                      e.matmul(PS65[:], VA[:, kt * 65:kt * 65 + 65],
                                                               es[:], start=(kt == 0),
                                                               stop=(kt == nk - 1))), deps=[et])
                            ctx_t[p] = edep
                        t = kb.step("vector", lambda e: [
                            e.tensor_copy(CTXs[:], PS65[0:64, :]),
                            e.tensor_copy(ZR[:], PS65[64:65, :])], deps=[edep])
                        t = kb.step("vector", lambda e: e.tensor_scalar(
                            ZR[:], ZR[:], 1e-30, None, ALU.add), deps=[t])
                        t = kb.step("vector", lambda e: e.reciprocal(ZR[:], ZR[:]), deps=[t])
                        pvt = t
                        for qs in range(4):
                            mt = kb.step("tensor", (lambda e, qs=qs:
                                                    e.matmul(PSO[:], CTXs[:, qs * 128:(qs + 1) * 128],
                                                             WO[:], start=True, stop=True)),
                                         deps=[t, pvt])
                            zt = kb.step("tensor", (lambda e, qs=qs:
                                                    e.matmul(PSS[0:128, 0:1],
                                                             ZR[0:1, qs * 128:(qs + 1) * 128],
                                                             ones11[:], start=True, stop=True)),
                                         deps=[t, pvt])
                            vt = kb.step("vector", lambda e: e.tensor_copy(
                                ZC[:], PSS[0:128, 0:1]), deps=[mt, zt, hq])
                            vt = kb.step("vector", lambda e: e.tensor_scalar(
                                OH[:], PSO[:], ZC[:], None, ALU.mult), deps=[vt])
                            pvt = vt
                            hq = kb.step("sync", (lambda e, qt=qt, qs=qs, h=h:
                                                  kb.dma(e, g["A"][(qt * 4 + qs) * 128:(qt * 4 + qs + 1) * 128,
                                                                   h * 64:(h + 1) * 64],
                                                         OH[:])), deps=[vt], n_dma=1)
                    hprev = hq

    def l1_attn_phase():
        PH[0] += 1
        SEG = 1024
        with ExitStack() as pc:
            RB = pc.enter_context(S("lb_rb", [128, D], F32))
            XN = pc.enter_context(S("lb_xn", [128, D], F32))
            CL = pc.enter_context(S("lb_cl", [128, 8], F32))
            LNG = pc.enter_context(S("lb_lng", [128, D], F32))
            LNB = pc.enter_context(S("lb_lnb", [128, D], F32))
            BQC = pc.enter_context(S("lb_bqc", [128, 24], F32))
            BGC = pc.enter_context(S("lb_bgc", [128, 8], F32))
            BPR = pc.enter_context(S("lb_bpr", [128, D], F32))
            XNT = pc.enter_context(S("lb_xnt", [128, 4096], BF16))
            XRT = pc.enter_context(S("lb_xrt", [128, 4096], BF16))
            WQKV = [pc.enter_context(S(f"lb_wq{i}", [128, 3 * D], BF16)) for i in range(8)]
            WG = [pc.enter_context(S(f"lb_wg{i}", [128, D], BF16)) for i in range(8)]
            WP = [pc.enter_context(S(f"lb_wp{i}", [128, D], BF16)) for i in range(8)]
            WF = pc.enter_context(S("lb_wf", [128, 3 * D], F32))
            QKV = [pc.enter_context(S(f"lb_qkv{i}", [128, 512], BF16)) for i in range(24)]
            GT = [pc.enter_context(S(f"lb_gt{i}", [128, 512], BF16)) for i in range(8)]
            GR = pc.enter_context(S("lb_gr", [1, 512], F32))
            EW = pc.enter_context(S("lb_ew", [128, 512], F32))
            EW2 = pc.enter_context(S("lb_ew2", [128, 512], F32))
            OT = [pc.enter_context(S(f"lb_ot{i}", [128, SEG], BF16)) for i in range(8)]
            KC = pc.enter_context(S("lb_kc", [128, 64], BF16))
            VC = pc.enter_context(S("lb_vc", [128, 65], BF16))
            SM = pc.enter_context(S("lb_sm", [128, 128], BF16))
            NTs = pc.enter_context(S("lb_nt", [64, 128], BF16))
            DR = pc.enter_context(S("lb_dr", [1, 128], F32))
            KVS = pc.enter_context(S("lb_kvs", [64, 1536], BF16))
            LTRI = pc.enter_context(S("lb_ltri", [128, 128], BF16))
            AB = pc.enter_context(S("lb_abf", [128, D], F32))
            XB = pc.enter_context(S("lb_xb", [128, D], BF16))
            QH0 = pc.enter_context(S("lb_qh0", [64, 128], BF16))
            PSA = pc.enter_context(PP("lb_psa", [128, 512]))
            PSR = pc.enter_context(PP("lb_psr", [1, 512]))
            PSG = pc.enter_context(PP("lb_psg", [128, 512]))
            PSN = pc.enter_context(PP("lb_psn", [65, 128]))
            PSK = pc.enter_context(PP("lb_psk", [64, 1536]))
            PST = pc.enter_context(PP("lb_pst", [128, 512], BF16))
            with nc.Block() as blk:
                kb.blk = blk
                phase_entry()
                ldt = None
                for i in range(8):
                    lt = kb.step("sync", (lambda e, i=i:
                                          kb.dma(e, WF[:, 0:3 * D],
                                                 g["w_qkv"][i * 128:(i + 1) * 128, :])),
                                 deps=[ldt], n_dma=1)
                    ldt = kb.step("vector", (lambda e, i=i:
                                             e.tensor_copy(WQKV[i][:], WF[:, 0:3 * D])),
                                  deps=[lt])
                for i in range(8):
                    lt = kb.step("sync", (lambda e, i=i:
                                          kb.dma(e, WF[:, 0:D],
                                                 g["w_gate"][i * 128:(i + 1) * 128, :])),
                                 deps=[ldt], n_dma=1)
                    ldt = kb.step("vector", (lambda e, i=i:
                                             e.tensor_copy(WG[i][:], WF[:, 0:D])), deps=[lt])
                for i in range(8):
                    lt = kb.step("sync", (lambda e, i=i:
                                          kb.dma(e, WF[:, 0:D],
                                                 g["w_proj"][i * 128:(i + 1) * 128, :])),
                                 deps=[ldt], n_dma=1)
                    ldt = kb.step("vector", (lambda e, i=i:
                                             e.tensor_copy(WP[i][:], WF[:, 0:D])), deps=[lt])
                ldt = kb.step("sync", lambda e: [
                    kb.dma(e, LNG[:], g["l1_lng_t"][:]), kb.dma(e, LNB[:], g["l1_lnb_t"][:]),
                    kb.dma(e, BQC[:], g["bqkv_cols"][:]), kb.dma(e, BGC[:], g["bgate_cols"][:]),
                    kb.dma(e, BPR[:], g["b_proj_t"][:]), kb.dma(e, LTRI[:], g["ltri_sm"][:])],
                    deps=[ldt], n_dma=6)
                t = kb.step("vector", lambda e: [
                    e.memset(KVS[:], 0.0),
                    e.memset(VC[:, 64:65], 1.0)], deps=[ldt])
                sprev = t
                stq = None
                for seg in range(4):
                    for half in range(2):
                        base_tok = seg * SEG + half * 512
                        t = sprev
                        for ti in range(4):
                            tok = base_tok + ti * 128
                            ldt = kb.step("sync", (lambda e, tok=tok:
                                                   kb.dma(e, RB[:], g["R"][tok:tok + 128, :])),
                                          deps=[t], n_dma=1)
                            t = emit_ln([ldt], CL, RB[:], XN[:], XN[:], D)
                            t = kb.step("vector", lambda e: e.tensor_mul(XN[:], XN[:], LNG[:]),
                                        deps=[t])
                            t = kb.step("vector", lambda e: e.tensor_add(XN[:], XN[:], LNB[:]),
                                        deps=[t])
                            t = kb.step("vector", lambda e: e.tensor_copy(XB[:], XN[:]),
                                        deps=[t])
                            for kb4 in range(2):
                                def trb(e, kb4=kb4):
                                    r = None
                                    for j in range(4):
                                        k = kb4 * 4 + j
                                        r = e.transpose(PST[:, j * 128:(j + 1) * 128],
                                                        XB[:, k * 128:(k + 1) * 128], ident[:])
                                    return r
                                tk = kb.step("tensor", trb, deps=[t])
                                def evb(e, kb4=kb4, ti=ti):
                                    r = None
                                    for j in range(4):
                                        k = kb4 * 4 + j
                                        r = e.tensor_copy(
                                            XNT[:, k * 512 + ti * 128:k * 512 + (ti + 1) * 128],
                                            PST[:, j * 128:(j + 1) * 128])
                                    return r
                                t = kb.step("vector", evb, deps=[tk])
                            t = kb.step("vector", lambda e: e.tensor_copy(XB[:], RB[:]),
                                        deps=[t])
                            for kb4 in range(2):
                                def trb(e, kb4=kb4):
                                    r = None
                                    for j in range(4):
                                        k = kb4 * 4 + j
                                        r = e.transpose(PST[:, j * 128:(j + 1) * 128],
                                                        XB[:, k * 128:(k + 1) * 128], ident[:])
                                    return r
                                tk = kb.step("tensor", trb, deps=[t])
                                def evb(e, kb4=kb4, ti=ti):
                                    r = None
                                    for j in range(4):
                                        k = kb4 * 4 + j
                                        r = e.tensor_copy(
                                            XRT[:, k * 512 + ti * 128:k * 512 + (ti + 1) * 128],
                                            PST[:, j * 128:(j + 1) * 128])
                                    return r
                                t = kb.step("vector", evb, deps=[tk])
                        gdone = []
                        for m in range(8):
                            def mmg(e, m=m):
                                for k in range(8):
                                    e.matmul(PSA[:], WG[k][:, m * 128:(m + 1) * 128],
                                             XRT[:, k * 512:(k + 1) * 512],
                                             start=(k == 0), stop=(k == 7))
                            mt = kb.step("tensor", mmg, deps=[t] + ([gdone[-1]] if gdone else []))
                            gt = kb.step("scalar", (lambda e, m=m:
                                                    e.activation(GT[m][:], PSA[:], AF.Sigmoid,
                                                                 bias=BGC[:, m:m + 1],
                                                                 scale=f32(1.0))), deps=[mt])
                            gdone.append(gt)
                        def mmsum(e):
                            for m in range(8):
                                e.matmul(PSR[:], onescb[:], GT[m][:],
                                         start=(m == 0), stop=(m == 7))
                        t = kb.step("tensor", mmsum, deps=gdone)
                        t = kb.step("vector", lambda e: e.tensor_scalar(
                            GR[:], PSR[:], float(1.0 / D), 1e-05, ALU.mult, ALU.add), deps=[t])
                        t = kb.step("vector", lambda e: e.reciprocal(GR[:], GR[:]), deps=[t])
                        t = kb.step("tensor", lambda e: e.matmul(PSG[:], onesrf[:], GR[:],
                                                                 start=True, stop=True), deps=[t])
                        gd = t
                        qdone = []
                        for m in range(24):
                            def mmq(e, m=m):
                                for k in range(8):
                                    e.matmul(PSA[:], WQKV[k][:, m * 128:(m + 1) * 128],
                                             XNT[:, k * 512:(k + 1) * 512],
                                             start=(k == 0), stop=(k == 7))
                            mt = kb.step("tensor", mmq, deps=[gd] + ([qdone[-1]] if qdone else []))
                            qt_ = kb.step("scalar", (lambda e, m=m:
                                                     e.activation(QKV[m][:], PSA[:], AF.Identity,
                                                                  bias=BQC[:, m:m + 1],
                                                                  scale=f32(1.0))), deps=[mt])
                            qdone.append(qt_)
                        gq = qdone[-1]
                        for m in range(16):
                            gq = kb.step("vector", (lambda e, m=m:
                                                    e.tensor_tensor(EW[:], QKV[m][:], PSG[:],
                                                                    ALU.mult)), deps=[gq, qdone[m]])
                            gq = kb.step("vector", lambda e: e.tensor_scalar_min(
                                EW2[:], EW[:], float(0.0)), deps=[gq])
                            gq = kb.step("scalar", lambda e: e.activation(
                                EW2[:], EW2[:], AF.Exp, bias=f32(0.0), scale=f32(1.0)),
                                deps=[gq])
                            gq = kb.step("vector", lambda e: e.tensor_scalar_max(
                                EW[:], EW[:], float(0.0)), deps=[gq])
                            gq = kb.step("vector", (lambda e, m=m:
                                                    e.tensor_add(QKV[m][:], EW2[:], EW[:])),
                                         deps=[gq])
                        sc = gq
                        for h in range(16):
                            mq = h // 2
                            r0 = (h % 2) * 64
                            hbase = (h // 7) * 512 + (h % 7) * 65
                            hs = slice(hbase, hbase + 65)
                            for c in range(4):
                                gc = seg * 8 + half * 4 + c
                                csl = slice(c * 128, (c + 1) * 128)
                                qh = QKV[mq]
                                kh = QKV[8 + mq]
                                vh = QKV[16 + mq]
                                tk = kb.step("tensor", (lambda e, kh=kh, r0=r0, csl=csl:
                                                        e.transpose(PST[0:128, 0:64],
                                                                    kh[r0:r0 + 64, csl],
                                                                    ident[r0:r0 + 64, r0:r0 + 64])),
                                             deps=[sc])
                                sc = kb.step("vector", lambda e: e.tensor_copy(
                                    KC[:], PST[0:128, 0:64]), deps=[tk])
                                tk = kb.step("tensor", (lambda e, vh=vh, r0=r0, csl=csl:
                                                        e.transpose(PST[0:128, 0:64],
                                                                    vh[r0:r0 + 64, csl],
                                                                    ident[r0:r0 + 64, r0:r0 + 64])),
                                             deps=[sc])
                                sc = kb.step("vector", lambda e: e.tensor_copy(
                                    VC[:, 0:64], PST[0:128, 0:64]), deps=[tk])
                                mt = kb.step("tensor", (lambda e, kh=kh, qh=qh, r0=r0, csl=csl:
                                                        e.matmul(PSA[:, 0:128],
                                                                 kh[r0:r0 + 64, csl],
                                                                 qh[r0:r0 + 64, csl],
                                                                 start=True, stop=True)),
                                             deps=[sc])
                                sc = kb.step("vector", lambda e: e.tensor_tensor(
                                    SM[:], PSA[:, 0:128], LTRI[:], ALU.mult), deps=[mt])
                                sc = kb.step("vector", (lambda e, qh=qh, r0=r0, csl=csl:
                                                        e.tensor_copy(QH0[:], qh[r0:r0 + 64, csl])),
                                             deps=[sc])
                                mt = kb.step("tensor", (lambda e, hs=hs:
                                                        [e.matmul(PSN[:], VC[:], SM[:],
                                                                  start=True, stop=False),
                                                         e.matmul(PSN[:], KVS[:, hs],
                                                                  QH0[:],
                                                                  start=False, stop=True)]),
                                             deps=[sc])
                                st_ = kb.step("tensor", (lambda e, hs=hs, gc=gc:
                                                         e.matmul(PSK[:, hs], KC[:], VC[:],
                                                                  start=(gc == 0),
                                                                  stop=(gc == 31),
                                                                  skip_group_check=True)),
                                              deps=[mt])
                                sc = kb.step("vector", (lambda e, hs=hs:
                                                        e.tensor_copy(KVS[:, hs], PSK[:, hs])),
                                             deps=[st_])
                                sc2 = kb.step("vector", lambda e: e.tensor_copy(
                                    NTs[:], PSN[0:64, :]), deps=[mt, sc])
                                sc2 = kb.step("vector", lambda e: e.tensor_scalar(
                                    DR[:], PSN[64:65, :], 1e-05, None, ALU.add), deps=[sc2])
                                sc2 = kb.step("vector", lambda e: e.reciprocal(DR[:], DR[:]),
                                              deps=[sc2])
                                bt = kb.step("tensor", lambda e: e.matmul(
                                    PSG[0:64, 0:128], onesrf[0:1, 0:64], DR[:],
                                    start=True, stop=True), deps=[sc2])
                                sc = kb.step("vector", (lambda e, mq=mq, r0=r0, half=half, csl=csl:
                                                        e.tensor_tensor(
                                                            OT[mq][r0:r0 + 64,
                                                                   half * 512 + csl.start:half * 512 + csl.stop],
                                                            NTs[:], PSG[0:64, 0:128],
                                                            ALU.mult)), deps=[bt, sc])
                        sprev = sc
                    pdone = sprev
                    for ti8 in range(8):
                        edone = []
                        for n in range(2):
                            def mmp(e, ti8=ti8, n=n):
                                for k in range(8):
                                    e.matmul(PSA[:], OT[k][:, ti8 * 128:(ti8 + 1) * 128],
                                             WP[k][:, n * 512:(n + 1) * 512],
                                             start=(k == 0), stop=(k == 7))
                            mt = kb.step("tensor", mmp, deps=[pdone])
                            et = kb.step("vector", (lambda e, n=n: e.tensor_add(
                                AB[:, n * 512:(n + 1) * 512], PSA[:],
                                BPR[:, n * 512:(n + 1) * 512])), deps=[mt])
                            edone.append(et)
                            pdone = et
                        stq = kb.step("sync", (lambda e, seg=seg, ti8=ti8:
                                               kb.dma(e, g["A"][seg * SEG + ti8 * 128:
                                                                seg * SEG + (ti8 + 1) * 128, :],
                                                      AB[:])), deps=edone + [stq], n_dma=1)
                        pdone = stq
                    sprev = stq


    seq = []
    seq.append(("ffn_a", dict(kind="l0", first=True, fold=False)))
    seq.append(("ffn_b", dict(kind="l0", first=True, fold=False, out_final=False)))
    if UPTO >= 2:
        seq.append(("ut", {}))
        seq.append(("l0at", {}))
    if UPTO >= 3:
        seq.append(("ffn_a", dict(kind="l0", first=False, fold=True)))
        seq.append(("ffn_b", dict(kind="l0", first=False, fold=True, out_final=False)))
    if UPTO >= 4:
        seq.append(("ffn_a", dict(kind="l1", first=False, fold=False)))
        seq.append(("ffn_b", dict(kind="l1", first=False, fold=False, out_final=False)))
    if UPTO >= 5:
        seq.append(("l1at", {}))
    if UPTO >= 6:
        seq.append(("ffn_a", dict(kind="l1", first=False, fold=True)))
        seq.append(("ffn_b", dict(kind="l1", first=False, fold=True, out_final=True)))
    final_is_R = UPTO < 6

    for name, kw in seq:
        if name == "ffn_a":
            ffn_a(**kw)
        elif name == "ffn_b":
            ffn_b(**kw)
        elif name == "ut":
            ut_phase()
        elif name == "l0at":
            l0_attn_phase()
        elif name == "l1at":
            l1_attn_phase()

    if final_is_R:
        with ExitStack() as pc:
            RB = pc.enter_context(S("cp_rb", [128, D], F32))
            AB = pc.enter_context(S("cp_ab", [128, D], F32))
            with nc.Block() as blk:
                kb.blk = blk
                phase_entry()
                prev = None
                fold_dbg = (UPTO in (2, 5))
                for i in range(NT):
                    nd = 2 if fold_dbg else 1
                    def ld(e, i=i):
                        kb.dma(e, RB[:], g["R"][i * 128:(i + 1) * 128, :])
                        if fold_dbg:
                            kb.dma(e, AB[:], g["A"][i * 128:(i + 1) * 128, :])
                    ldt = kb.step("sync", ld, deps=[prev], n_dma=nd)
                    dep = ldt
                    if fold_dbg:
                        dep = kb.step("vector", lambda e: e.tensor_add(RB[:], RB[:], AB[:]),
                                      deps=[ldt])
                    prev = kb.step("sync", (lambda e, i=i:
                                            kb.dma(e, g["y_out"][i * 128:(i + 1) * 128, :],
                                                   RB[:])), deps=[dep], n_dma=1)
        with nc.Block() as blk:
            kb.blk = blk
            phase_entry()
            kb.step("sync", lambda e: None, deps=[("sync", kb.tick["sync"])])

    return nc, g, ctx



_BUILD_CACHE = {}

def _tile_rows(v, n=128):
    return np.broadcast_to(np.asarray(v, np.float32).reshape(1, -1), (n, len(np.ravel(v)))).copy()

def kernel(**inputs):
    I = {k: np.ascontiguousarray(np.asarray(v, np.float32)) for k, v in inputs.items()}
    keep = dropout_keep_mask()
    maskscale = np.where(keep, np.float32(1.25), np.float32(0.0)).astype(np.float32)

    nc, g, ctx = _BUILD_CACHE.get("nc", (None, None, None))
    if nc is None:
        nc, g, ctx = build_program()
        _BUILD_CACHE["nc"] = (nc, g, ctx)

    ident = np.eye(128, dtype=ml_dtypes.bfloat16)
    tri = np.zeros((128, 4 * 512), dtype=ml_dtypes.bfloat16)
    for off in range(4):
        k_idx = np.arange(128)[:, None] + off * 128
        q_idx = np.arange(512)[None, :]
        tri[:, off * 512:(off + 1) * 512] = (k_idx <= q_idx).astype(ml_dtypes.bfloat16)
    ltri = (np.arange(128)[:, None] <= np.arange(128)[None, :]).astype(ml_dtypes.bfloat16)

    common = {
        "l0_w1t": I["L0_ffn_w1"].T.copy(), "l0_w2t": I["L0_ffn_w2"].T.copy(),
        "l1_w1t": I["L1_ffn_w1"].T.copy(), "l1_w2t": I["L1_ffn_w2"].T.copy(),
        "l0_ln1w_t": _tile_rows(I["L0_ffn_ln1_w"]), "l0_ln1b_t": _tile_rows(I["L0_ffn_ln1_b"]),
        "l0_ln2w_t": _tile_rows(I["L0_ffn_ln2_w"]), "l0_ln2b_t": _tile_rows(I["L0_ffn_ln2_b"]),
        "l0_b1_t": _tile_rows(I["L0_ffn_b1"]), "l0_b2_t": _tile_rows(I["L0_ffn_b2"]),
        "l1_b1_t": _tile_rows(I["L1_ffn_b1"]), "l1_b2_t": _tile_rows(I["L1_ffn_b2"]),
        "w_dq_t": I["L0_attn_W_DQ"].T.copy(), "w_uq_t": I["L0_attn_W_UQ"].T.copy(),
        "w_dkv_t": I["L0_attn_W_DKV"].T.copy(), "w_ukv_t": I["L0_attn_W_UKV"].T.copy(),
        "w_qr_t": I["L0_attn_W_QR"].T.copy(), "w_kr_t": I["L0_attn_W_KR"].T.copy(),
        "w_o_t": I["L0_attn_W_O"].T.copy(),
        "qnw_c": I["L0_attn_q_norm_w"].reshape(64, 1).copy(),
        "knw_c": I["L0_attn_k_norm_w"].reshape(64, 1).copy(),
        "w_qkv": I["L1_attn_w_qkv"].copy(),
        "bqkv_cols": I["L1_attn_b_qkv"].reshape(24, 128).T.copy(),
        "w_gate": I["L1_attn_w_gate"].copy(),
        "bgate_cols": I["L1_attn_b_gate"].reshape(8, 128).T.copy(),
        "w_proj": I["L1_attn_w_proj"].copy(),
        "b_proj_t": _tile_rows(I["L1_attn_b_proj"]),
        "l1_lng_t": _tile_rows(I["L1_attn_ln_g"]), "l1_lnb_t": _tile_rows(I["L1_attn_ln_b"]),
        "ident": ident, "ones_col_f32": np.ones((128, 1), np.float32),
        "ones_row_f32": np.ones((1, 128), np.float32),
        "ones11": np.ones((1, 1), np.float32),
        "tri_masks": tri, "ltri_sm": ltri,
    }
    in_maps = []
    for c in range(8):
        b = c % 4
        m = dict(common)
        m["x_in"] = I["L0_attn_x"][b]
        m["mask_in"] = maskscale[b]
        in_maps.append(m)
    res = run_bass_kernel_spmd(nc, in_maps, core_ids=list(range(8)))
    out = np.stack([res.results[c]["y_out"] for c in range(4)], axis=0)
    return out.astype(np.float32)
